# version 01
# baseline (speedup 1.0000x reference)
import numpy as np
import ml_dtypes
from contextlib import ExitStack

import concourse.bass as bass
import concourse.mybir as mybir
from concourse.bass_utils import run_bass_kernel_spmd

F32 = mybir.dt.float32
BF16 = mybir.dt.bfloat16
I32 = mybir.dt.int32
AF = mybir.ActivationFunctionType
ALU = mybir.AluOpType
AX = mybir.AxisListType

HID = 4096
H = 128
P = 64
N = 128
G = 8
KCONV = 4
CH = 128
INTER = 8192
NKC = 32
TOK = 1024
W = 1152
NCH = 8
EPS = 1e-5

FC_DT = [144]
FC_B = list(range(128, 136))
FC_C = list(range(136, 144))
FC_HS = list(range(64, 128))
FC_GATE = list(range(0, 64))

MAGIC_X = 0x5AC70001

DEBUG = False
BUILD_STAGE = 6
IP_LIMIT = 0
ARENA_BYTES = 160 * 1024



def _stream4(wmat, n_oc, n_kc):
    t = wmat.reshape(n_kc, 128, n_oc, 128).transpose(2, 0, 1, 3)
    return np.ascontiguousarray(t)


def host_prep(inputs):
    x = np.asarray(inputs['L0_attn_hidden_states'])
    shared = {}
    shared['ipw_st'] = _stream4(
        np.ascontiguousarray(np.asarray(inputs['L0_attn_in_proj_w']).T), 145, NKC)
    shared['opw_st'] = _stream4(
        np.ascontiguousarray(np.asarray(inputs['L0_attn_out_proj_w']).T), 32, 64)
    shared['upw_st'] = _stream4(np.asarray(inputs['L0_ffn_up_w']), 64, NKC)
    shared['dnw_st'] = _stream4(np.asarray(inputs['L0_ffn_down_w']), 32, 64)
    shared['fc1_st'] = _stream4(np.asarray(inputs['L0_ffn_fc1_w']), 16, NKC)
    shared['fc2_st'] = _stream4(np.asarray(inputs['L0_ffn_fc2_w']), 32, 16)
    shared['exp_st'] = _stream4(np.asarray(inputs['L0_ffn_expert_w']), 16, 16)
    shared['gatew'] = np.ascontiguousarray(
        np.asarray(inputs['L0_ffn_gate_w']).reshape(NKC, 128, 8))
    conv_w = np.asarray(inputs['L0_attn_conv_w'])
    shared['cw'] = np.ascontiguousarray(
        conv_w[:, 0, :].reshape(80, 128, KCONV).transpose(1, 2, 0))
    shared['cb'] = np.ascontiguousarray(
        np.asarray(inputs['L0_attn_conv_b']).reshape(80, 128).T)
    shared['dtb'] = np.ascontiguousarray(np.asarray(inputs['L0_attn_dt_bias']).reshape(128, 1))
    shared['alog'] = np.ascontiguousarray(np.asarray(inputs['L0_attn_A_log']).reshape(128, 1))
    shared['drow'] = np.ascontiguousarray(np.asarray(inputs['L0_attn_D']).reshape(1, 128))
    shared['nw'] = np.ascontiguousarray(np.asarray(inputs['L0_attn_norm_weight']).reshape(64, 128).T)
    shared['lng'] = np.ascontiguousarray(np.asarray(inputs['L0_ffn_ln_gamma']).reshape(32, 128).T)
    shared['lnb'] = np.ascontiguousarray(np.asarray(inputs['L0_ffn_ln_beta']).reshape(32, 128).T)
    shared['ident'] = np.eye(128, dtype=np.float32).astype(ml_dtypes.bfloat16)
    shared['ident32'] = np.eye(128, dtype=np.float32)
    shared['mask_lges'] = np.ascontiguousarray(
        np.triu(np.ones((128, 128), np.float32)).astype(ml_dtypes.bfloat16))

    per_core = []
    for c in range(8):
        bb, q = c // 4, c % 4
        d = dict(shared)

        def tslice(q0):
            lo = q0 * TOK
            seg = x[bb, lo:lo + TOK]
            halo = np.zeros((128, HID), np.float32) if lo == 0 else x[bb, lo - 128:lo]
            return np.ascontiguousarray(np.concatenate([halo, seg], 0).T.astype(np.float32))

        d['xT_own'] = tslice(q)
        d['xT_ast'] = tslice(q - 2 if q >= 2 else q)
        me = c % 2
        ot = 1 - me
        S_OWN = lambda s: 2 * s
        S_AST = lambda s: 2 * s + 1
        Msel = np.zeros((4, 4), np.float32)
        wsel = np.zeros((4,), np.float32)
        if q == 1:
            wsel[S_OWN(ot)] = 1.0
        elif q == 2:
            wsel[S_AST(me)] = 1.0
            Msel[S_AST(me), S_AST(ot)] = 1.0
            wsel[S_AST(ot)] = 1.0
        elif q == 3:
            wsel[S_AST(ot)] = 1.0
            Msel[S_AST(ot), S_AST(me)] = 1.0
            Msel[S_AST(ot), S_OWN(ot)] = 1.0
            wsel[S_AST(me)] = 1.0
            Msel[S_AST(me), S_OWN(ot)] = 1.0
            wsel[S_OWN(ot)] = 1.0
        d['Msel'] = np.ascontiguousarray(
            np.broadcast_to(Msel.reshape(1, 16), (128, 16)).astype(np.float32))
        d['wsel'] = np.ascontiguousarray(
            np.broadcast_to(wsel.reshape(1, 4), (128, 4)).astype(np.float32))
        per_core.append(d)
    return per_core



ENGS = ('sync', 'vector', 'scalar', 'tensor', 'gpsimd')
DMA_QUEUES = ('sync', 'gpsimd', 'scalar')
NPOOL = 8
CAP = 30000


class Op:
    __slots__ = ('eng', 'fn', 'reads', 'writes', 'dma', 'marker', 'deps', 'tag')

    def __init__(self, eng, fn, reads, writes, dma, tag):
        self.eng = eng
        self.fn = fn
        self.reads = tuple(reads)
        self.writes = tuple(writes)
        self.dma = dma
        self.tag = tag


class Prog:
    def __init__(self, bld):
        self.bld = bld
        self.segments = [[]]
        self.custom_emitters = {}
        self.fences = set()

    def op(self, eng, fn, reads=(), writes=(), dma=False, tag=''):
        o = Op(eng, fn, reads, writes, dma, tag)
        self.segments[-1].append(o)
        return o

    def dma(self, eng, fn, reads=(), writes=(), tag=''):
        return self.op(eng, fn, reads, writes, dma=True, tag=tag)

    def barrier(self, fence=True):
        if fence:
            self.fences.add(len(self.segments))
        self.segments.append([])

    def custom(self, emitter):
        self.custom_emitters[len(self.segments)] = emitter
        self.segments.append([])

    def resolve(self, key):
        return self.marker_sem(self.last_write[key])

    def marker_sem(self, mk):
        if mk[0] == 'dma':
            semh = self.qsems[mk[1]][(mk[2] - 1) % NPOOL]
            val = ((mk[2] - 1) // NPOOL + 1) * 16
        else:
            gen = (mk[2] - 1) // CAP
            semh = self.csems[(mk[1], gen)]
            val = ((mk[2] - 1) % CAP) + 1
        return semh, val

    def plan(self):
        bld = self.bld
        dma_count = {qn: 0 for qn in DMA_QUEUES}
        comp_count = {e: 0 for e in ENGS}
        last_write = {}
        readers = {}
        fence_pending = {}
        for si, seg in enumerate(self.segments):
            if si in self.fences:
                snap = [('dma', qn, dma_count[qn]) for qn in DMA_QUEUES if dma_count[qn] > 0]
                fence_pending = {e: list(snap) for e in ENGS}
            for o in seg:
                deps = []
                if fence_pending.get(o.eng):
                    deps.extend(fence_pending[o.eng])
                    fence_pending[o.eng] = []
                for r in o.reads:
                    wsrc = last_write.get(r)
                    if wsrc is not None:
                        deps.append(wsrc)
                for w in o.writes:
                    wsrc = last_write.get(w)
                    if wsrc is not None:
                        deps.append(wsrc)
                    for rd in readers.get(w, ()):
                        deps.append(rd)
                if o.dma:
                    qn = o.eng
                    dma_count[qn] += 1
                    marker = ('dma', qn, dma_count[qn])
                    if dma_count[qn] > NPOOL:
                        deps.append(('dma', qn, dma_count[qn] - NPOOL))
                else:
                    comp_count[o.eng] += 1
                    marker = ('comp', o.eng, comp_count[o.eng])
                o.marker = marker
                o.deps = sorted(set(
                    d for d in deps
                    if d != marker
                    and not (d[0] == 'comp' and d[1] == 'tensor'
                             and o.eng == 'tensor' and not o.dma)))
                for w in o.writes:
                    last_write[w] = marker
                    readers[w] = []
                for r in o.reads:
                    readers.setdefault(r, []).append(marker)
        self.last_write = last_write
        self.qsems = {qn: [bld.sem(f'q_{qn}_{i}') for i in range(NPOOL)]
                      for qn in DMA_QUEUES if dma_count[qn] > 0}
        self.csems = {}
        for e in ENGS:
            for gen in range((comp_count[e] + CAP - 1) // CAP):
                self.csems[(e, gen)] = bld.sem(f'c_{e}_{gen}')
        self.stats = dict(dma=dict(dma_count), comp=dict(comp_count))

    def emit(self):
        self.plan()
        nc = self.bld.nc
        for si, seg in enumerate(self.segments):
            if si in self.custom_emitters:
                self.custom_emitters[si](self.bld, self)
            if not seg:
                continue
            per_eng = {e: [] for e in ENGS}
            for o in seg:
                per_eng[o.eng].append(o)
            with nc.Block(f'seg{si}') as blk:
                for eng in ENGS:
                    items = per_eng[eng]
                    if not items:
                        continue

                    def run(e, items=items):
                        guar = {}
                        for o in items:
                            for d in o.deps:
                                semh, val = self.marker_sem(d)
                                k = id(semh)
                                if guar.get(k, 0) < val:
                                    e.wait_ge(semh, val)
                                    guar[k] = val
                            inst = o.fn(e)
                            semh, val = self.marker_sem(o.marker)
                            inst.then_inc(semh, 16 if o.dma else 1)
                    getattr(blk, eng)(run)


class Builder:
    def __init__(self, debug=False):
        self.nc = bass.Bass('TRN2', target_bir_lowering=False)
        self.es = ExitStack()
        self.debug = debug
        self.din = {}
        self.dram = {}
        self.sb = {}
        self.ps = {}
        self._arena = None
        self._abase = 0
        self._vn = 0

    def dinp(self, name, shape, dt=F32):
        t = self.nc.dram_tensor(name, list(shape), dt, kind='ExternalInput')
        self.din[name] = t
        return t

    def dout(self, name, shape, dt=F32):
        t = self.nc.dram_tensor(name, list(shape), dt, kind='ExternalOutput')
        self.dram[name] = t
        return t

    def dpark(self, name, shape, dt=BF16, shared=False):
        kind = 'ExternalOutput' if (self.debug and not shared) else 'Internal'
        kw = {'addr_space': 'Shared'} if shared else {}
        t = self.nc.dram_tensor(name, list(shape), dt, kind=kind, **kw)
        self.dram[name] = t
        return t

    def sbuf(self, name, shape, dt=F32):
        t = self.es.enter_context(self.nc.sbuf_tensor(name, list(shape), dt))
        self.sb[name] = t
        return t

    def psum(self, name, shape, dt=F32):
        t = self.es.enter_context(self.nc.psum_tensor(name, list(shape), dt))
        self.ps[name] = t
        return t

    def sem(self, name):
        return self.es.enter_context(self.nc.semaphore(name))

    def arena_init(self):
        self._arena = self.nc.alloc_sbuf_tensor('ARENA', [128, ARENA_BYTES], mybir.dt.uint8)
        self._abase = self.nc.lookup_mloc(self._arena).addr

    def carve(self, layout, base=0, limit=ARENA_BYTES):
        off = base
        for name, shape, dt in layout:
            esz = {F32: 4, BF16: 2, I32: 4}[dt]
            nb = int(np.prod(shape[1:])) * esz
            off = (off + 63) & ~63
            assert off + nb <= limit, f'arena overflow at {name}: {off + nb} > {limit}'
            self._vn += 1
            h = self.nc.alloc_sbuf_tensor_at(
                f'{name}_v{self._vn}', list(shape), dt, offset=self._abase + off)
            off += nb
            self.sb[name] = h
        return off



def declare(b, stage):
    b.dinp('xT_own', (HID, W)); b.dinp('xT_ast', (HID, W))
    b.dinp('ipw_st', (145, NKC, 128, 128))
    b.dinp('opw_st', (32, 64, 128, 128))
    b.dinp('upw_st', (64, NKC, 128, 128))
    b.dinp('dnw_st', (32, 64, 128, 128))
    b.dinp('fc1_st', (16, NKC, 128, 128))
    b.dinp('fc2_st', (32, 16, 128, 128))
    b.dinp('exp_st', (16, 16, 128, 128))
    b.dinp('gatew', (NKC, 128, 8))
    b.dinp('cw', (128, KCONV, 80)); b.dinp('cb', (128, 80))
    b.dinp('dtb', (128, 1)); b.dinp('alog', (128, 1)); b.dinp('drow', (1, 128))
    b.dinp('nw', (128, 64)); b.dinp('lng', (128, 32)); b.dinp('lnb', (128, 32))
    b.dinp('ident', (128, 128), BF16)
    b.dinp('ident32', (128, 128), F32)
    b.dinp('mask_lges', (128, 128), BF16)
    b.dinp('Msel', (128, 16)); b.dinp('wsel', (128, 4))

    b.dout('out_fm', (HID, TOK), F32)
    b.dpark('p_hsT_own', (NCH, 128, INTER))
    b.dpark('p_hsT_ast', (NCH, 128, INTER))
    b.dpark('p_sgate', (64, 128, TOK))
    b.dpark('p_B_own', (8, 128, TOK))
    b.dpark('p_B_ast', (8, 128, TOK))
    b.dpark('p_C', (8, 128, TOK))
    b.dpark('p_dt', (2, 128, TOK), F32)
    b.dpark('p_acs', (2, 128, TOK), F32)
    b.dpark('p_sdoT', (NCH, 128, 128), F32)
    b.dpark('p_y1', (NCH, 128, INTER))
    b.dpark('p_lp', (NCH, 128, INTER), F32)
    b.dpark('p_h', (64, 128, TOK))
    b.dpark('p_xlat', (16, 128, TOK))
    b.dpark('p_fpre', (32, 128, TOK))
    b.dpark('p_fout', (32, 128, TOK))
    b.dpark('p_yf', (64, 128, TOK))
    b.dpark('p_xn', (NKC, 128, W))
    b.dpark('acs_scr', (2, 128, 128), F32)
    b.dpark('p_Sstage', (2, 128, INTER))
    b.dpark('p_I', (128, INTER), F32)
    b.dpark('ones_scr', (1, 1024), F32)
    b.dpark('xch_S', (2, 2, 128, INTER), BF16, shared=True)
    b.dpark('xch_A', (2, 4, 128), F32, shared=True)
    b.dpark('xch_F', (2, 4), I32, shared=True)

    b.sbuf('c_ident_bf', (128, 128), BF16)
    b.sbuf('c_ident32', (128, 128), F32)
    b.sbuf('c_Drep_bf', (128, 128), BF16)
    b.sbuf('c_mask_lges', (128, 128), BF16)
    b.sbuf('c_cw', (128, KCONV * 80), F32)
    b.sbuf('c_cb', (128, 80), F32)
    b.sbuf('c_dtb', (128, 1), F32)
    b.sbuf('c_A', (128, 1), F32)
    b.sbuf('c_Aexp', (128, 1), F32)
    b.sbuf('c_Drep', (128, 128), F32)
    b.sbuf('c_nw', (128, 64), F32)
    b.sbuf('c_lng', (128, 32), F32)
    b.sbuf('c_lnb', (128, 32), F32)
    b.sbuf('c_Msel', (128, 16), F32)
    b.sbuf('c_wsel', (128, 4), F32)
    b.sbuf('c_ones_col_bf', (128, 1), BF16)
    b.sbuf('c_ones_row_bf', (1, 128), BF16)
    b.sbuf('c_ones_row_f32', (1, 128), F32)
    b.sbuf('c_zeros_128', (128, 128), F32)
    b.sbuf('c_magic', (1, 4), I32)
    b.sbuf('c_drow_f32', (1, 128), F32)
    b.sbuf('c_drow_bf', (1, 128), BF16)
    b.sbuf('c_eps', (128, 1), F32)
    b.sbuf('c_mhalf', (128, 1), F32)
    b.sbuf('c_aseg', (128, 4), F32)
    b.sbuf('c_flags', (1, 8), I32)
    b.sbuf('c_dI', (128, 2), F32)
    b.sbuf('prefix', (128, INTER), F32)

    for i in range(6):
        b.psum(f'pb{i}', (128, 512), F32)
    b.psum('pb6', (128, 1024), BF16)
    b.psum('pb7', (128, 1024), BF16)

    b.xq_sem = b.sem('xq')
    b.xp_sem = b.sem('xp')
    b.xdone_sem = b.sem('xdone')
    b.arena_init()


XN_BYTES = NKC * W * 2
SCR_BASE = 74 * 1024

LAY_LN = [
    ('ln_x', (128, 2 * W), F32),
    ('ln_xbf', (128, 2 * W), BF16),
    ('ln_sq', (128, 2 * W), BF16),
    ('ln_rep', (128, 2 * W), BF16),
    ('ln_rows', (1, 6 * W), F32),
    ('ln_rows_bf', (1, 2 * W), BF16),
]
LAY_C = [
    ('cs_hsT', (128, 2 * INTER), BF16),
    ('cs_y1', (128, INTER), BF16),
    ('cs_Bc', (128, 1024), BF16),
    ('cs_BT', (128, 1024), BF16),
    ('cs_Cc', (128, 1024), BF16),
    ('cs_Gm', (128, 1024), BF16),
    ('cs_M', (128, 1024), BF16),
    ('cs_mtmp', (128, 1024), F32),
    ('cs_X', (128, 512), BF16),
    ('cs_Xd', (128, 512), BF16),
    ('cs_dres', (128, 512), BF16),
    ('cs_dtc', (128, 128), F32),
    ('cs_acsc', (128, 128), F32),
    ('cs_nacs', (128, 128), F32),
    ('cs_dtT', (128, 128), BF16),
    ('cs_DecT', (128, 128), BF16),
    ('cs_tmp', (128, 128), F32),
    ('cs_drow', (1, 128), F32),
    ('cs_drep', (128, 128), F32),
    ('cs_flat_l', (2, 1024), F32),
    ('cs_flat_r', (2, 1024), F32),
]

LAY_FFN_A = [
    ('ff_wf32', (128, 2 * 1024), F32),
    ('ff_wbf', (128, 2 * 4096), BF16),
    ('ff_hrow', (128, 2 * TOK), BF16),
    ('ff_xle', (128, 2 * TOK), BF16),
    ('ff_gw', (128, 32 * 8), F32),
    ('ff_gwb', (128, 32 * 8), BF16),
    ('ff_lgt', (128, 8), F32),
    ('ff_pairs', (128, 64), F32),
    ('ff_s', (128, 8), F32),
    ('ff_srow', (1, 1024), F32),
    ('ff_srep', (128, 1024), F32),
]
LAY_FFN_B = [
    ('fb_h', (128, 64 * 512), BF16),
    ('fb_xlat', (128, 16 * 512), BF16),
    ('fb_exp', (128, 16 * 512), BF16),
    ('fb_wf32', (128, 2 * 1024), F32),
    ('fb_wbf', (128, 2 * 4096), BF16),
    ('fb_pre', (128, 2 * 512), BF16),
    ('fb_sq', (128, 512), BF16),
    ('fb_rows', (1, 6 * 512), F32),
    ('fb_rowbf', (1, 2 * 512), BF16),
    ('fb_mrep', (128, 512), BF16),
    ('fb_rrep', (128, 512), BF16),
    ('fb_ln', (128, 2 * 512), BF16),
]

LAY_G = [
    ('gs_y1', (128, INTER), BF16),
    ('gs_lp', (128, INTER), F32),
    ('gs_prev', (128, INTER), F32),
    ('gs_prevbf', (128, INTER), BF16),
    ('gs_C', (128, 1024), BF16),
    ('gs_sdoT', (128, 128), F32),
    ('gs_t', (128, 512), BF16),
    ('gs_ytok', (128, INTER), BF16),
    ('gs_yf', (128, INTER), BF16),
    ('gs_dcol', (128, 4), F32),
    ('gs_drow', (1, 128), F32),
    ('gs_dIrep', (128, 128), F32),
]
LAY_H = [
    ('hn', (128, 64 * TOK), BF16),
    ('hs_y', (128, 2 * TOK), BF16),
    ('hs_sg', (128, 2 * TOK), BF16),
    ('hs_sq', (128, TOK), BF16),
    ('hs_rows', (1, 2 * TOK), F32),
    ('hs_rowbf', (1, TOK), BF16),
    ('hs_rep', (128, TOK), BF16),
]
LAY_H2 = [
    ('ho_wf32', (128, 1024), F32),
    ('ho_wbf', (128, 2 * 4096), BF16),
    ('ho_x', (128, 2 * 512), F32),
    ('ho_f', (128, 2 * 512), BF16),
    ('ho_o', (128, 2 * 512), F32),
]

LAY_IP = [
    ('ip_wf32', (128, 2 * 1024), F32),
    ('ip_wbf', (128, 2 * HID), BF16),
    ('ip_pre', (128, 2 * W), F32),
    ('ip_conv', (128, 2 * TOK), F32),
    ('ip_out', (128, 2 * TOK), BF16),
    ('ip_trst', (128, 2 * TOK), BF16),
    ('sp_t1', (128, TOK), F32),
    ('sp_t2', (128, TOK), F32),
    ('dt_cur', (128, TOK), F32),
    ('acs_cur', (128, TOK), F32),
]


def carve_lnip(b):
    b._vn += 1
    b.sb['xn'] = b.nc.alloc_sbuf_tensor_at(
        f'xn_v{b._vn}', [128, NKC * W], BF16, offset=b._abase)
    end1 = b.carve(LAY_LN, base=SCR_BASE)
    end2 = b.carve(LAY_IP, base=SCR_BASE)
    end3 = b.carve(LAY_C, base=SCR_BASE)
    end4 = b.carve(LAY_FFN_A, base=SCR_BASE)
    end5 = b.carve(LAY_FFN_B, base=0)
    end6 = b.carve(LAY_G, base=0)
    end7 = b.carve(LAY_H, base=0)
    end8 = b.carve(LAY_H2, base=ARENA_BYTES - 31 * 1024)
    assert max(end1, end2, end3, end4, end5, end6, end7, end8) <= ARENA_BYTES, (
        end1, end2, end3, end4, end5, end6, end7, end8)



def ph_consts(b, p):
    sb, din, ps = b.sb, b.din, b.ps
    loads = [
        (sb['c_ident_bf'][:, :], din['ident'].ap(), 'c_ident_bf'),
        (sb['c_ident32'][:, :], din['ident32'].ap(), 'c_ident32'),
        (sb['c_mask_lges'][:, :], din['mask_lges'].ap(), 'c_mask_lges'),
        (sb['c_cw'][:, :], din['cw'].ap().rearrange('p a b -> p (a b)'), 'c_cw'),
        (sb['c_cb'][:, :], din['cb'].ap(), 'c_cb'),
        (sb['c_dtb'][:, :], din['dtb'].ap(), 'c_dtb'),
        (sb['c_A'][:, :], din['alog'].ap(), 'c_A'),
        (sb['c_nw'][:, :], din['nw'].ap(), 'c_nw'),
        (sb['c_lng'][:, :], din['lng'].ap(), 'c_lng'),
        (sb['c_lnb'][:, :], din['lnb'].ap(), 'c_lnb'),
        (sb['c_Msel'][:, :], din['Msel'].ap(), 'c_Msel'),
        (sb['c_wsel'][:, :], din['wsel'].ap(), 'c_wsel'),
        (sb['c_drow_f32'][:, :], din['drow'].ap(), 'c_drow_f32'),
    ]
    for dst, src, key in loads:
        p.dma('sync', lambda e, dst=dst, src=src: e.dma_start(out=dst, in_=src),
              writes=[key], tag='const')
    p.op('vector', lambda e: e.memset(sb['c_ones_col_bf'][:, :], 1.0), writes=['c_ones_col'])
    p.op('vector', lambda e: e.memset(sb['c_ones_row_bf'][:, :], 1.0), writes=['c_ones_row'])
    p.op('vector', lambda e: e.memset(sb['c_ones_row_f32'][:, :], 1.0), writes=['c_ones_row_f32'])
    p.op('vector', lambda e: e.memset(sb['c_zeros_128'][:, :], 0.0), writes=['c_zeros_128'])
    p.op('vector', lambda e: e.memset(sb['c_magic'][:, :], MAGIC_X), writes=['c_magic'])
    p.op('vector', lambda e: e.memset(sb['prefix'][:, :], 0.0), writes=['prefix'])
    p.op('vector', lambda e: e.memset(sb['c_eps'][:, :], EPS), writes=['c_eps'])
    p.op('vector', lambda e: e.memset(sb['c_mhalf'][:, :], -0.5), writes=['c_mhalf'])
    p.op('scalar', lambda e: e.activation(sb['c_Aexp'][:, :], sb['c_A'][:, :], AF.Exp),
         reads=['c_A'], writes=['c_Aexp'])
    p.op('vector', lambda e: e.tensor_scalar_mul(sb['c_A'][:, :], sb['c_Aexp'][:, :], -1.0),
         reads=['c_Aexp'], writes=['c_A'])
    p.op('vector', lambda e: e.tensor_copy(sb['c_drow_bf'][:, :], sb['c_drow_f32'][:, :]),
         reads=['c_drow_f32'], writes=['c_drow_bf'])
    p.op('tensor', lambda e: e.matmul(ps['pb0'][:, 0:128], sb['c_ones_row_bf'][:, :],
                                      sb['c_drow_bf'][:, :], start=True, stop=True),
         reads=['c_ones_row', 'c_drow_bf'], writes=['pb0'])
    p.op('scalar', lambda e: e.copy(sb['c_Drep'][:, :], ps['pb0'][:, 0:128]),
         reads=['pb0'], writes=['c_Drep'])
    p.op('vector', lambda e: e.tensor_copy(sb['c_Drep_bf'][:, :], sb['c_Drep'][:, :]),
         reads=['c_Drep'], writes=['c_Drep_bf'])


def ph_ln(b, p, which):
    sb, din, ps = b.sb, b.din, b.ps
    xn = sb['xn']
    xbf = sb['ln_xbf']
    BL = [(0, 512), (512, 512), (1024, 128)]
    SBANK = ['pb0', 'pb1', 'pb2']
    QBANK = ['pb3', 'pb4', 'pb5']
    xk = din['xT_' + which].ap().rearrange('(a p) w -> a p w', p=128)
    for k in range(NKC):
        s = k % 2
        sl = slice(s * W, s * W + W)
        p.dma('sync', lambda e, k=k, sl=sl: e.dma_start(out=sb['ln_x'][:, sl], in_=xk[k]),
              writes=[f'ln_x:{s}'], tag='lnload')
        p.op('vector', lambda e, sl=sl: e.tensor_copy(xbf[:, sl], sb['ln_x'][:, sl]),
             reads=[f'ln_x:{s}'], writes=[f'ln_xbf:{s}'])
        p.op('scalar', lambda e, sl=sl: e.activation(sb['ln_sq'][:, sl], xbf[:, sl], AF.Square),
             reads=[f'ln_xbf:{s}'], writes=[f'ln_sq:{s}'])
        for bi, (lo, ln_) in enumerate(BL):
            p.op('tensor', lambda e, s=s, k=k, lo=lo, ln_=ln_, bi=bi:
                 e.matmul(ps[SBANK[bi]][0:1, 0:ln_], sb['c_ones_col_bf'][:, :],
                          xbf[:, s * W + lo:s * W + lo + ln_],
                          start=(k == 0), stop=(k == NKC - 1)),
                 reads=[f'ln_xbf:{s}', 'c_ones_col'], writes=[SBANK[bi]])
            p.op('tensor', lambda e, s=s, k=k, lo=lo, ln_=ln_, bi=bi:
                 e.matmul(ps[QBANK[bi]][0:1, 0:ln_], sb['c_ones_col_bf'][:, :],
                          sb['ln_sq'][:, s * W + lo:s * W + lo + ln_],
                          start=(k == 0), stop=(k == NKC - 1)),
                 reads=[f'ln_sq:{s}', 'c_ones_col'], writes=[QBANK[bi]])
    r = sb['ln_rows']

    def R(i, lo=0, ln_=W):
        return r[0:1, i * W + lo:i * W + lo + ln_]
    for bi, (lo, ln_) in enumerate(BL):
        p.op('vector', lambda e, bi=bi, lo=lo, ln_=ln_:
             e.tensor_scalar_mul(R(0, lo, ln_), ps[SBANK[bi]][0:1, 0:ln_], 1.0 / HID),
             reads=[SBANK[bi]], writes=['ln_mean'])
        p.op('vector', lambda e, bi=bi, lo=lo, ln_=ln_:
             e.tensor_scalar_mul(R(1, lo, ln_), ps[QBANK[bi]][0:1, 0:ln_], 1.0 / HID),
             reads=[QBANK[bi]], writes=['ln_msq'])
    p.op('vector', lambda e: e.tensor_mul(R(2), R(0), R(0)),
         reads=['ln_mean'], writes=['ln_m2'])
    p.op('vector', lambda e: e.tensor_sub(R(3), R(1), R(2)),
         reads=['ln_msq', 'ln_m2'], writes=['ln_var'])
    p.op('scalar', lambda e: e.activation(R(4), R(3), AF.Ln, bias=sb['c_eps'][0:1, :]),
         reads=['ln_var', 'c_eps'], writes=['ln_lnv'])
    p.op('scalar', lambda e: e.activation(R(5), R(4), AF.Exp, scale=sb['c_mhalf'][0:1, :]),
         reads=['ln_lnv', 'c_mhalf'], writes=['ln_rstd'])
    rb = sb['ln_rows_bf']
    p.op('vector', lambda e: e.tensor_copy(rb[0:1, 0:W], R(0)),
         reads=['ln_mean'], writes=['ln_mean_bf'])
    p.op('vector', lambda e: e.tensor_copy(rb[0:1, W:2 * W], R(5)),
         reads=['ln_rstd'], writes=['ln_rstd_bf'])
    for bi, (lo, ln_) in enumerate(BL):
        p.op('tensor', lambda e, lo=lo, ln_=ln_:
             e.matmul(ps['pb0'][:, 0:ln_], sb['c_ones_row_bf'][:, :],
                      rb[0:1, lo:lo + ln_], start=True, stop=True),
             reads=['ln_mean_bf', 'c_ones_row'], writes=['pb0'])
        p.op('scalar', lambda e, lo=lo, ln_=ln_:
             e.copy(sb['ln_rep'][:, lo:lo + ln_], ps['pb0'][:, 0:ln_]),
             reads=['pb0'], writes=['ln_rep:m'])
        p.op('tensor', lambda e, lo=lo, ln_=ln_:
             e.matmul(ps['pb1'][:, 0:ln_], sb['c_ones_row_bf'][:, :],
                      rb[0:1, W + lo:W + lo + ln_], start=True, stop=True),
             reads=['ln_rstd_bf', 'c_ones_row'], writes=['pb1'])
        p.op('scalar', lambda e, lo=lo, ln_=ln_:
             e.copy(sb['ln_rep'][:, W + lo:W + lo + ln_], ps['pb1'][:, 0:ln_]),
             reads=['pb1'], writes=['ln_rep:r'])
    repdeps = ['ln_rep:m', 'ln_rep:r']
    for k in range(NKC):
        s = k % 2
        sl = slice(s * W, s * W + W)
        osl = slice(k * W, (k + 1) * W)
        p.dma('sync', lambda e, k=k, sl=sl: e.dma_start(out=sb['ln_x'][:, sl], in_=xk[k]),
              writes=[f'ln_x:{s}'], tag='lnload2')
        p.op('vector', lambda e, sl=sl: e.tensor_copy(xbf[:, sl], sb['ln_x'][:, sl]),
             reads=[f'ln_x:{s}'], writes=[f'ln_xbf:{s}'])
        p.op('vector', lambda e, sl=sl: e.tensor_sub(xbf[:, sl], xbf[:, sl], sb['ln_rep'][:, 0:W]),
             reads=[f'ln_xbf:{s}'] + repdeps, writes=[f'ln_xbf:{s}'])
        p.op('vector', lambda e, sl=sl, osl=osl, k=k:
             e.tensor_tensor(xn[:, osl], xbf[:, sl], sb['ln_rep'][:, W:2 * W], ALU.mult),
             reads=[f'ln_xbf:{s}'] + repdeps, writes=[f'xn:{k}'])


def ph_inproj(b, p, which):
    sb, din, ps = b.sb, b.din, b.ps
    xn = sb['xn']
    own = which == 'own'
    fcs = (FC_DT + FC_B + (FC_C if own else []) + FC_HS + (FC_GATE if own else []))
    if IP_LIMIT:
        fcs = fcs[:IP_LIMIT]
    wst = din['ipw_st'].ap()
    ipb = [ps['pb0'], ps['pb1'], ps['pb2'], ps['pb3']]
    tbi = [0]
    xn_deps = [f'xn:{a}' for a in range(NKC)]

    for u, fc in enumerate(fcs):
        fam = ('dt' if fc in FC_DT else 'B' if fc in FC_B else
               'C' if fc in FC_C else 'hs' if fc in FC_HS else 'gate')
        conv = fam in ('B', 'C', 'hs')
        BL = [(0, 512), (512, 512), (1024, 128)] if conv else [(128, 512), (640, 512)]
        ub = u % 2
        bkey = f'ip_wbf:{ub}'
        for qtr in range(4):
            wq = qtr % 2
            p.dma('sync', lambda e, fc=fc, qtr=qtr, wq=wq:
                  e.dma_start(out=sb['ip_wf32'][:, wq * 1024:(wq + 1) * 1024]
                              .rearrange('p (a f) -> p a f', f=128),
                              in_=wst[fc].rearrange('a p f -> p a f')
                              [:, qtr * 8:(qtr + 1) * 8, :]),
                  writes=[f'ip_wf32:{wq}'], tag='wip')
            p.op('scalar', lambda e, ub=ub, qtr=qtr, wq=wq:
                 e.copy(sb['ip_wbf'][:, ub * HID + qtr * 1024:ub * HID + (qtr + 1) * 1024],
                        sb['ip_wf32'][:, wq * 1024:(wq + 1) * 1024]),
                 reads=[f'ip_wf32:{wq}'], writes=[bkey], tag='wcast')
        for lo, ln_ in BL:
            bank = ipb[tbi[0] % 4]
            bank_key = f'pb{tbi[0] % 4}'
            tbi[0] += 1
            for a in range(NKC):
                p.op('tensor', lambda e, a=a, lo=lo, ln_=ln_, ub=ub, bank=bank:
                     e.matmul(bank[:, 0:ln_],
                              sb['ip_wbf'][:, ub * HID + a * 128:ub * HID + (a + 1) * 128],
                              xn[:, a * W + lo:a * W + lo + ln_],
                              start=(a == 0), stop=(a == NKC - 1)),
                     reads=[bkey, xn_deps[a]], writes=[bank_key], tag='ipmm')
            if conv:
                p.op('vector', lambda e, lo=lo, ln_=ln_, ub=ub, bank=bank:
                     e.tensor_copy(sb['ip_pre'][:, ub * W + lo:ub * W + lo + ln_],
                                   bank[:, 0:ln_]),
                     reads=[bank_key], writes=[f'ip_pre:{ub}'], tag='ipevac')
            elif fam == 'dt':
                p.op('vector', lambda e, lo=lo, ln_=ln_, bank=bank:
                     e.tensor_scalar_add(sb['sp_t1'][:, lo - 128:lo - 128 + ln_],
                                         bank[:, 0:ln_], sb['c_dtb'][:, :]),
                     reads=[bank_key, 'c_dtb'], writes=['sp_t1'], tag='dtbias')
            else:
                p.op('scalar', lambda e, lo=lo, ln_=ln_, ub=ub, bank=bank:
                     e.activation(sb['ip_out'][:, ub * TOK + lo - 128:ub * TOK + lo - 128 + ln_],
                                  bank[:, 0:ln_], AF.Silu),
                     reads=[bank_key], writes=[f'ip_out:{ub}'], tag='gsilu')
        if fam == 'dt':
            _emit_dt_chain(b, p, which)
        elif conv:
            _emit_conv_unit(b, p, which, fam, fc, ub)
        else:
            p.dma('gpsimd', lambda e, fc=fc, ub=ub: e.dma_start(
                out=b.dram['p_sgate'].ap()[fc],
                in_=sb['ip_out'][:, ub * TOK:(ub + 1) * TOK]),
                reads=[f'ip_out:{ub}'], tag='gpark')


def _emit_conv_unit(b, p, which, fam, fc, ub):
    sb, ps = b.sb, b.ps
    if fam == 'hs':
        ci = fc - 64
    elif fam == 'B':
        ci = 64 + (fc - 128)
    else:
        ci = 72 + (fc - 136)
    cw = sb['c_cw']
    pre = sb['ip_pre']
    conv = sb['ip_conv']
    cvs = slice(ub * TOK, (ub + 1) * TOK)
    p.op('vector', lambda e: e.tensor_scalar_mul(
        conv[:, cvs], pre[:, ub * W + 128:ub * W + 128 + TOK],
        cw[:, 3 * 80 + ci:3 * 80 + ci + 1]),
        reads=[f'ip_pre:{ub}', 'c_cw'], writes=[f'ip_conv:{ub}'], tag='conv')
    for k in range(3):
        off = 128 - 3 + k
        p.op('vector', lambda e, k=k, off=off: e.scalar_tensor_tensor(
            conv[:, cvs], pre[:, ub * W + off:ub * W + off + TOK],
            cw[:, k * 80 + ci:k * 80 + ci + 1], conv[:, cvs],
            ALU.mult, ALU.add),
            reads=[f'ip_pre:{ub}', f'ip_conv:{ub}', 'c_cw'],
            writes=[f'ip_conv:{ub}'], tag='conv')
    p.op('scalar', lambda e: e.activation(
        sb['ip_out'][:, cvs], conv[:, cvs], AF.Silu, bias=sb['c_cb'][:, ci:ci + 1]),
        reads=[f'ip_conv:{ub}', 'c_cb'], writes=[f'ip_out:{ub}'], tag='silu')
    if fam == 'B':
        g = fc - 128
        p.dma('gpsimd', lambda e, g=g: e.dma_start(
            out=b.dram['p_B_' + which].ap()[g], in_=sb['ip_out'][:, cvs]),
            reads=[f'ip_out:{ub}'], tag='bpark')
    elif fam == 'C':
        g = fc - 136
        p.dma('gpsimd', lambda e, g=g: e.dma_start(
            out=b.dram['p_C'].ap()[g], in_=sb['ip_out'][:, cvs]),
            reads=[f'ip_out:{ub}'], tag='cpark')
    else:
        j = fc - 64
        for c in range(NCH):
            trb = ps['pb6'] if (c % 2) == 0 else ps['pb7']
            trk = f'pb{6 + (c % 2)}'
            p.op('tensor', lambda e, c=c, trb=trb: e.transpose(
                trb[:, 0:128], sb['ip_out'][:, ub * TOK + c * 128:ub * TOK + (c + 1) * 128],
                b.sb['c_ident_bf'][:, :]),
                reads=[f'ip_out:{ub}', 'c_ident_bf'], writes=[trk], tag='hstr')
            p.op('vector', lambda e, c=c, trb=trb: e.tensor_copy(
                sb['ip_trst'][:, ub * TOK + c * 128:ub * TOK + (c + 1) * 128], trb[:, 0:128]),
                reads=[trk], writes=[f'ip_trst:{ub}:{c}'], tag='hstrev')
            p.dma('gpsimd', lambda e, c=c, j=j: e.dma_start(
                out=b.dram['p_hsT_' + which].ap()[c][:, j * 128:(j + 1) * 128],
                in_=sb['ip_trst'][:, ub * TOK + c * 128:ub * TOK + (c + 1) * 128]),
                reads=[f'ip_trst:{ub}:{c}'], tag='hspark')


def _emit_dt_chain(b, p, which):
    sb = b.sb
    i = 0 if which == 'own' else 1
    dtt = sb['dt_cur']
    acs = sb['acs_cur']
    t1 = sb['sp_t1']
    t2 = sb['sp_t2']
    p.op('scalar', lambda e: e.activation(t2[:, :], t1[:, :], AF.Abs),
         reads=['sp_t1'], writes=['sp_t2'], tag='sp')
    p.op('scalar', lambda e: e.activation(t2[:, :], t2[:, :], AF.Exp, scale=-1.0),
         reads=['sp_t2'], writes=['sp_t2'], tag='sp')
    p.op('scalar', lambda e: e.activation(t2[:, :], t2[:, :], AF.Ln, bias=1.0),
         reads=['sp_t2'], writes=['sp_t2'], tag='sp')
    p.op('vector', lambda e: e.tensor_scalar_max(t1[:, :], t1[:, :], 0.0),
         reads=['sp_t1'], writes=['sp_t1'], tag='sp')
    p.op('vector', lambda e: e.tensor_add(dtt[:, :], t1[:, :], t2[:, :]),
         reads=['sp_t1', 'sp_t2'], writes=['dt_cur'], tag='sp')
    p.op('vector', lambda e: e.tensor_scalar_mul(t1[:, :], dtt[:, :], sb['c_A'][:, :]),
         reads=['dt_cur', 'c_A'], writes=['sp_t1'], tag='adt')
    for c in range(NCH):
        p.op('vector', lambda e, c=c: e.tensor_tensor_scan(
            acs[:, c * 128:(c + 1) * 128], t1[:, c * 128:(c + 1) * 128],
            sb['c_zeros_128'][:, :], 0.0, ALU.add, ALU.add),
            reads=['sp_t1', 'c_zeros_128'], writes=[f'acs_cur:{c}'], tag='acs')
    p.op('vector', lambda e: e.tensor_copy(sb['c_aseg'][:, i:i + 1], acs[:, 127:128]),
         reads=['acs_cur:0'], writes=[f'c_aseg:{i}'], tag='aseg')
    for c in range(1, NCH):
        p.op('vector', lambda e, c=c: e.tensor_add(
            sb['c_aseg'][:, i:i + 1], sb['c_aseg'][:, i:i + 1], acs[:, c * 128 + 127:c * 128 + 128]),
            reads=[f'c_aseg:{i}', f'acs_cur:{c}'], writes=[f'c_aseg:{i}'], tag='aseg')
    acsdeps = [f'acs_cur:{c}' for c in range(NCH)]
    p.dma('gpsimd', lambda e, i=i: e.dma_start(out=b.dram['p_dt'].ap()[i], in_=dtt[:, :]),
          reads=['dt_cur'], tag='dtpark')
    p.dma('gpsimd', lambda e, i=i: e.dma_start(out=b.dram['p_acs'].ap()[i], in_=acs[:, :]),
          reads=acsdeps, tag='acspark')


def ph_scan(b, p, which):
    sb, ps, dram = b.sb, b.ps, b.dram
    own = which == 'own'
    i = 0 if own else 1
    hsT = dram['p_hsT_' + which].ap()
    Bpark = dram['p_B_' + which].ap()

    p.op('vector', lambda e: e.memset(sb['cs_flat_l'][0:1, :], 1.0), writes=['fl_ones'])
    p.op('vector', lambda e: e.memset(sb['cs_mtmp'][0:1, 0:1024], 1.0), writes=['cs_mtmp'])
    p.dma('gpsimd', lambda e: e.dma_start(out=dram['ones_scr'].ap(), in_=sb['cs_mtmp'][0:1, 0:1024]),
          reads=['cs_mtmp'], writes=['ones_scr'], tag='ones')
    p.dma('gpsimd', lambda e: e.dma_start(out=sb['cs_flat_r'][1:2, :], in_=dram['ones_scr'].ap()),
          reads=['ones_scr'], writes=['fr_ones'], tag='ones')

    for c in range(NCH):
        hb = c % 2
        p.dma('sync', lambda e, c=c, hb=hb: e.dma_start(
            out=sb['cs_hsT'][:, hb * INTER:(hb + 1) * INTER], in_=hsT[c]),
            writes=[f'cs_hsT:{hb}'], tag='chsT')
        p.dma('sync', lambda e, c=c: e.dma_start(
            out=sb['cs_Bc'][:, :].rearrange('p (g t) -> p g t', t=128),
            in_=Bpark.rearrange('g p t -> p g t')[:, :, c * 128:(c + 1) * 128]),
            writes=['cs_Bc'], tag='cB')
        if own:
            p.dma('sync', lambda e, c=c: e.dma_start(
                out=sb['cs_Cc'][:, :].rearrange('p (g t) -> p g t', t=128),
                in_=dram['p_C'].ap().rearrange('g p t -> p g t')[:, :, c * 128:(c + 1) * 128]),
                writes=['cs_Cc'], tag='cC')
        p.dma('sync', lambda e, c=c: e.dma_start(
            out=sb['cs_dtc'][:, :], in_=dram['p_dt'].ap()[i][:, c * 128:(c + 1) * 128]),
            writes=['cs_dtc'], tag='cdt')
        p.dma('sync', lambda e, c=c: e.dma_start(
            out=sb['cs_acsc'][:, :], in_=dram['p_acs'].ap()[i][:, c * 128:(c + 1) * 128]),
            writes=['cs_acsc'], tag='cacs')
        if own:
            p.dma('gpsimd', lambda e, c=c: e.dma_start(
                out=dram['p_lp'].ap()[c], in_=sb['prefix'][:, :]),
                reads=['prefix'], writes=[f'p_lp:{c}'], tag='lppark')
        p.op('vector', lambda e: e.tensor_scalar_mul(sb['cs_nacs'][:, :], sb['cs_acsc'][:, :], -1.0),
             reads=['cs_acsc'], writes=['cs_nacs'], tag='nacs')
        p.dma('gpsimd', lambda e: e.dma_start(out=dram['acs_scr'].ap()[0], in_=sb['cs_acsc'][:, :]),
              reads=['cs_acsc'], writes=['acs_scr0'], tag='acsrt')
        p.dma('gpsimd', lambda e: e.dma_start(out=dram['acs_scr'].ap()[1], in_=sb['cs_nacs'][:, :]),
              reads=['cs_nacs'], writes=['acs_scr1'], tag='acsrt')
        p.op('vector', lambda e: e.tensor_copy(sb['cs_tmp'][:, :], sb['cs_dtc'][:, :]),
             reads=['cs_dtc'], writes=['cs_tmp'], tag='x')
        p.op('tensor', lambda e: e.transpose(ps['pb2'][:, 0:128], sb['cs_tmp'][:, :],
                                             sb['c_ident32'][:, :]),
             reads=['cs_tmp', 'c_ident32'], writes=['pb2'], tag='trdt')
        p.op('vector', lambda e: e.tensor_copy(sb['cs_dtT'][:, :], ps['pb2'][:, 0:128]),
             reads=['pb2'], writes=['cs_dtT'], tag='trdtev')
        p.op('vector', lambda e: e.tensor_scalar_sub(sb['cs_tmp'][:, :], sb['cs_acsc'][:, :],
                                                     sb['cs_acsc'][:, 127:128]),
             reads=['cs_acsc'], writes=['cs_tmp'], tag='dec')
        p.op('scalar', lambda e: e.activation(sb['cs_tmp'][:, :], sb['cs_tmp'][:, :],
                                              AF.Exp, scale=-1.0),
             reads=['cs_tmp'], writes=['cs_tmp'], tag='dec')
        p.op('tensor', lambda e: e.transpose(ps['pb3'][:, 0:128], sb['cs_tmp'][:, :],
                                             sb['c_ident32'][:, :]),
             reads=['cs_tmp', 'c_ident32'], writes=['pb3'], tag='trdec')
        p.op('vector', lambda e: e.tensor_copy(sb['cs_DecT'][:, :], ps['pb3'][:, 0:128]),
             reads=['pb3'], writes=['cs_DecT'], tag='trdecev')
        if own:
            p.op('scalar', lambda e: e.activation(sb['cs_tmp'][:, :], sb['cs_acsc'][:, :], AF.Exp),
                 reads=['cs_acsc'], writes=['cs_tmp'], tag='sdo')
            p.op('tensor', lambda e: e.transpose(ps['pb2'][:, 0:128], sb['cs_tmp'][:, :],
                                                 sb['c_ident32'][:, :]),
                 reads=['cs_tmp', 'c_ident32'], writes=['pb2'], tag='trsdo')
            p.op('vector', lambda e: e.tensor_copy(sb['cs_nacs'][:, :], ps['pb2'][:, 0:128]),
                 reads=['pb2'], writes=['cs_nacs'], tag='trsdoev')
            p.dma('gpsimd', lambda e, c=c: e.dma_start(
                out=dram['p_sdoT'].ap()[c], in_=sb['cs_nacs'][:, :]),
                reads=['cs_nacs'], writes=[f'p_sdoT:{c}'], tag='sdopark')
        p.op('scalar', lambda e: e.activation(sb['cs_tmp'][:, 0:1], sb['cs_acsc'][:, 127:128], AF.Exp),
             reads=['cs_acsc'], writes=['cs_tmp'], tag='dcol')
        p.op('tensor', lambda e: e.transpose(ps['pb2'][0:1, 0:128], sb['cs_tmp'][:, 0:1],
                                             sb['c_ident32'][:, :]),
             reads=['cs_tmp', 'c_ident32'], writes=['pb2'], tag='trd')
        p.op('vector', lambda e: e.tensor_copy(sb['cs_drow'][:, :], ps['pb2'][0:1, 0:128]),
             reads=['pb2'], writes=['cs_drow'], tag='trdev')
        p.op('tensor', lambda e: e.matmul(ps['pb2'][:, 0:128], sb['c_ones_row_f32'][:, :],
                                          sb['cs_drow'][:, :], start=True, stop=True),
             reads=['cs_drow', 'c_ones_row_f32'], writes=['pb2'], tag='drepmm')
        p.op('vector', lambda e: e.tensor_copy(sb['cs_drep'][:, :], ps['pb2'][:, 0:128]),
             reads=['pb2'], writes=['cs_drep'], tag='drepev')
        p.op('vector', lambda e: e.tensor_tensor(
            sb['prefix'][:, :].rearrange('n (h q) -> n h q', q=P), sb['prefix'][:, :].rearrange('n (h q) -> n h q', q=P),
            sb['cs_drep'][:, :].unsqueeze(2).broadcast_to((128, 128, P)), ALU.mult),
            reads=['prefix', 'cs_drep'], writes=['prefix'], tag='pfxmul')
        if own:
            for g in range(8):
                bank = ps['pb0'] if g < 4 else ps['pb1']
                bk = 'pb0' if g < 4 else 'pb1'
                p.op('tensor', lambda e, g=g, bank=bank: e.matmul(
                    bank[:, (g % 4) * 128:(g % 4) * 128 + 128],
                    sb['cs_Bc'][:, g * 128:(g + 1) * 128],
                    sb['cs_Cc'][:, g * 128:(g + 1) * 128],
                    start=True, stop=True),
                    reads=['cs_Bc', 'cs_Cc'], writes=[bk], tag='gmm')
            for half in range(2):
                bank = ps['pb0'] if half == 0 else ps['pb1']
                bk = 'pb0' if half == 0 else 'pb1'
                p.op('vector', lambda e, half=half, bank=bank: e.tensor_tensor(
                    sb['cs_Gm'][:, half * 512:(half + 1) * 512].rearrange('p (g t) -> p g t', t=128),
                    bank[:, 0:512].rearrange('p (g t) -> p g t', t=128),
                    sb['c_mask_lges'][:, :].unsqueeze(1).broadcast_to((128, 4, 128)), ALU.mult),
                    reads=[bk, 'c_mask_lges'], writes=['cs_Gm'], tag='gmmask')
        for g in range(8):
            trb = ps['pb6'] if (g % 2) == 0 else ps['pb7']
            trk = f'pb{6 + (g % 2)}'
            p.op('tensor', lambda e, g=g, trb=trb: e.transpose(
                trb[:, 0:128], sb['cs_Bc'][:, g * 128:(g + 1) * 128], sb['c_ident_bf'][:, :]),
                reads=['cs_Bc', 'c_ident_bf'], writes=[trk], tag='btr')
            p.op('vector', lambda e, g=g, trb=trb: e.tensor_copy(
                sb['cs_BT'][:, g * 128:(g + 1) * 128], trb[:, 0:128]),
                reads=[trk], writes=['cs_BT'], tag='btrev')
        for w in range(16):
            if own:
                p.dma('scalar', lambda e, w=w: e.dma_start(
                    out=sb['cs_flat_r'][0:1, :],
                    in_=dram['acs_scr'].ap()[0].rearrange('p t -> (p t)').unsqueeze(0)
                    [:, w * 1024:(w + 1) * 1024]),
                    reads=['acs_scr0'], writes=['fr'], tag='cflat')
                p.dma('scalar', lambda e, w=w: e.dma_start(
                    out=sb['cs_flat_l'][1:2, :],
                    in_=dram['acs_scr'].ap()[1].rearrange('p t -> (p t)').unsqueeze(0)
                    [:, w * 1024:(w + 1) * 1024]),
                    reads=['acs_scr1'], writes=['fl'], tag='cflat')
            ybank, yk = (ps['pb4'], 'pb4') if w % 2 == 0 else (ps['pb0'], 'pb0')
            sbank, sk = (ps['pb5'], 'pb5') if w % 2 == 0 else (ps['pb1'], 'pb1')
            if own:
                for j in range(8):
                    dbank, dk = (ps['pb2'], 'pb2') if j < 4 else (ps['pb3'], 'pb3')
                    p.op('tensor', lambda e, j=j, dbank=dbank: e.matmul(
                        dbank[:, (j % 4) * 128:(j % 4) * 128 + 128],
                        sb['cs_flat_l'][:, j * 128:(j + 1) * 128],
                        sb['cs_flat_r'][:, j * 128:(j + 1) * 128],
                        start=True, stop=True),
                        reads=['fl', 'fr', 'fl_ones', 'fr_ones'],
                        writes=[dk], tag='diffmm')
                for half, dk in ((0, 'pb2'), (1, 'pb3')):
                    bank = ps[dk]
                    p.op('vector', lambda e, half=half, bank=bank: e.tensor_scalar_min(
                        sb['cs_mtmp'][:, half * 512:(half + 1) * 512], bank[:, 0:512], 0.0),
                        reads=[dk], writes=['cs_mtmp'], tag='mmin')
                p.op('scalar', lambda e: e.activation(sb['cs_M'][:, :], sb['cs_mtmp'][:, :], AF.Exp),
                     reads=['cs_mtmp'], writes=['cs_M'], tag='mexp')
                p.op('vector', lambda e: e.tensor_tensor(sb['cs_M'][:, :], sb['cs_M'][:, :],
                                                         sb['cs_Gm'][:, :], ALU.mult),
                     reads=['cs_M', 'cs_Gm'], writes=['cs_M'], tag='mmul')
            p.op('vector', lambda e, w=w, hb=hb: e.tensor_tensor(
                sb['cs_X'][:, :].rearrange('p (h q) -> p h q', q=P),
                sb['cs_hsT'][:, hb * INTER + w * 512:hb * INTER + (w + 1) * 512]
                .rearrange('p (h q) -> p h q', q=P),
                sb['cs_dtT'][:, w * 8:(w + 1) * 8].unsqueeze(2).broadcast_to((128, 8, P)),
                ALU.mult),
                reads=[f'cs_hsT:{hb}', 'cs_dtT'], writes=['cs_X'], tag='xbuild')
            p.op('vector', lambda e, w=w: e.tensor_tensor(
                sb['cs_Xd'][:, :].rearrange('p (h q) -> p h q', q=P),
                sb['cs_X'][:, :].rearrange('p (h q) -> p h q', q=P),
                sb['cs_DecT'][:, w * 8:(w + 1) * 8].unsqueeze(2).broadcast_to((128, 8, P)),
                ALU.mult),
                reads=['cs_X', 'cs_DecT'], writes=['cs_Xd'], tag='xdbuild')
            if own:
                for j in range(8):
                    p.op('tensor', lambda e, j=j, ybank=ybank: e.matmul(
                        ybank[:, j * 64:(j + 1) * 64],
                        sb['cs_M'][:, j * 128:(j + 1) * 128],
                        sb['cs_X'][:, j * 64:(j + 1) * 64], start=True, stop=True),
                        reads=['cs_M', 'cs_X'], writes=[yk], tag='ydiagmm')
            for j in range(8):
                p.op('tensor', lambda e, j=j, sbank=sbank: e.matmul(
                    sbank[:, j * 64:(j + 1) * 64],
                    sb['cs_BT'][:, j * 128:(j + 1) * 128],
                    sb['cs_Xd'][:, j * 64:(j + 1) * 64], start=True, stop=True),
                    reads=['cs_BT', 'cs_Xd'], writes=[sk], tag='statesmm')
            if own:
                p.op('vector', lambda e, w=w, hb=hb: e.tensor_tensor(
                    sb['cs_dres'][:, :].rearrange('p (h q) -> p h q', q=P),
                    sb['cs_hsT'][:, hb * INTER + w * 512:hb * INTER + (w + 1) * 512]
                    .rearrange('p (h q) -> p h q', q=P),
                    sb['c_Drep_bf'][:, w * 8:(w + 1) * 8].unsqueeze(2).broadcast_to((128, 8, P)),
                    ALU.mult),
                    reads=[f'cs_hsT:{hb}', 'c_Drep_bf'], writes=['cs_dres'], tag='dres')
                p.op('vector', lambda e, w=w, ybank=ybank: e.tensor_tensor(
                    sb['cs_y1'][:, w * 512:(w + 1) * 512], ybank[:, 0:512],
                    sb['cs_dres'][:, :], ALU.add),
                    reads=[yk, 'cs_dres'], writes=['cs_y1'], tag='y1add')
            p.op('vector', lambda e, w=w, sbank=sbank: e.tensor_tensor(
                sb['prefix'][:, w * 512:(w + 1) * 512], sb['prefix'][:, w * 512:(w + 1) * 512],
                sbank[:, 0:512], ALU.add),
                reads=['prefix', sk], writes=['prefix'], tag='pfxadd')
        if own:
            p.dma('gpsimd', lambda e, c=c: e.dma_start(
                out=dram['p_y1'].ap()[c], in_=sb['cs_y1'][:, :]),
                reads=['cs_y1'], writes=[f'p_y1:{c}'], tag='y1park')
    p.op('vector', lambda e: e.tensor_copy(sb['cs_y1'][:, :], sb['prefix'][:, :]),
         reads=['prefix'], writes=['cs_y1'], tag='sexp')
    p.dma('gpsimd', lambda e, i=i: e.dma_start(out=dram['p_Sstage'].ap()[i], in_=sb['cs_y1'][:, :]),
          reads=['cs_y1'], writes=[f'p_Sstage:{i}'], tag='sexp')
    p.op('vector', lambda e: e.memset(sb['prefix'][:, :], 0.0),
         reads=['cs_y1'], writes=['prefix'], tag='pfxreset')


def ph_park_xn(b, p):
    for k in range(NKC):
        p.dma('gpsimd', lambda e, k=k: e.dma_start(
            out=b.dram['p_xn'].ap()[k], in_=b.sb['xn'][:, k * W:(k + 1) * W]),
            reads=[f'xn:{k}'], writes=[f'p_xn:{k}'], tag='xnpark')


def ph_load_xn(b, p):
    for k in range(NKC):
        p.dma('sync', lambda e, k=k: e.dma_start(
            out=b.sb['xn'][:, k * W:(k + 1) * W], in_=b.dram['p_xn'].ap()[k]),
            reads=[f'p_xn:{k}'], writes=[f'xn:{k}'], tag='xnload')


def make_exchange_emitter(b):
    def emitter(bld, sched):
        nc = bld.nc
        dram = bld.dram
        sb = bld.sb
        deps = [sched.resolve(k) for k in
                ['p_Sstage:0', 'p_Sstage:1', 'c_aseg:0', 'c_aseg:1', 'c_magic']]
        xq = bld.xq_sem
        for arm in nc.switch_core_id([nc.gpsimd], n=128):
            slot = int(arm) % 2
            g = nc.gpsimd
            for semh, val in deps:
                g.wait_ge(semh, val)
            g.dma_start(out=dram['xch_S'].ap()[slot, 0], in_=dram['p_Sstage'].ap()[0]).then_inc(xq, 16)
            g.dma_start(out=dram['xch_S'].ap()[slot, 1], in_=dram['p_Sstage'].ap()[1]).then_inc(xq, 16)
            g.dma_start(out=dram['xch_A'].ap()[slot, 0].unsqueeze(1),
                        in_=sb['c_aseg'][:, 0:1]).then_inc(xq, 16)
            g.dma_start(out=dram['xch_A'].ap()[slot, 1].unsqueeze(1),
                        in_=sb['c_aseg'][:, 1:2]).then_inc(xq, 16)
            g.wait_ge(xq, 64)
            g.dma_start(out=dram['xch_F'].ap()[slot], in_=sb['c_magic'][0:1, :]).then_inc(xq, 16)
            g.wait_ge(xq, 80)
        xp = bld.xp_sem
        xdone = bld.xdone_sem
        with nc.Block('poll') as blk:
            def _g(g):
                with g.register('rend') as rend, g.register('rwait') as rwait:
                    g.reg_mov(rend, 500000)
                    g.reg_mov(rwait, 16)
                    with g.Fori(0, rend):
                        g.dma_start(out=sb['c_flags'][0:1, 0:8],
                                    in_=dram['xch_F'].ap().rearrange('a b -> (a b)')
                                    .unsqueeze(0)).then_inc(xp, 16)
                        g.wait_ge(xp, rwait)
                        g.reg_add(rwait, rwait, 16)
                        f0 = g.to_reg(g.value_load(sb['c_flags'][0:1, 0:1]))
                        f1 = g.to_reg(g.value_load(sb['c_flags'][0:1, 4:5]))
                        with g.If_eq(f0, MAGIC_X):
                            with g.If_eq(f1, MAGIC_X):
                                g.reg_mov(rend, 0)
                g.memset(sb['c_flags'][0:1, 4:8], 0).then_inc(xdone, 1)
            blk.gpsimd(_g)
    return emitter


def ph_combine_I(b, p):
    sb, ps, dram = b.sb, b.ps, b.dram
    p.op('gpsimd', lambda e: (e.wait_ge(b.xdone_sem, 1),
                              e.memset(sb['c_flags'][0:1, 6:8], 0))[-1],
         writes=['xch_ready'], tag='xbridge')
    asg = sb['cs_dtc']
    for j in range(4):
        s, i = j // 2, j % 2
        p.dma('sync', lambda e, s=s, i=i, j=j: e.dma_start(
            out=asg[:, j:j + 1], in_=dram['xch_A'].ap()[s, i].unsqueeze(1)),
            reads=['xch_ready'], writes=['asg'], tag='asgload')
    for j in range(4):
        p.op('vector', lambda e, j=j: e.tensor_tensor(
            asg[:, 8 + j:9 + j], asg[:, 0:1], sb['c_Msel'][:, 4 * j:4 * j + 1], ALU.mult),
            reads=['asg', 'c_Msel'], writes=['asg'], tag='sumt')
        for i in range(1, 4):
            p.op('vector', lambda e, j=j, i=i: e.scalar_tensor_tensor(
                asg[:, 8 + j:9 + j], asg[:, i:i + 1], sb['c_Msel'][:, 4 * j + i:4 * j + i + 1],
                asg[:, 8 + j:9 + j], ALU.mult, ALU.add),
                reads=['asg', 'c_Msel'], writes=['asg'], tag='sumt')
        p.op('scalar', lambda e, j=j: e.activation(asg[:, 16 + j:17 + j],
                                                   asg[:, 8 + j:9 + j], AF.Exp),
             reads=['asg'], writes=['asg'], tag='wexp')
        p.op('vector', lambda e, j=j: e.tensor_tensor(
            asg[:, 16 + j:17 + j], asg[:, 16 + j:17 + j], sb['c_wsel'][:, j:j + 1], ALU.mult),
            reads=['asg', 'c_wsel'], writes=['asg'], tag='wmask')
    for j in range(4):
        s, i = j // 2, j % 2
        hb = j % 2
        p.dma('sync', lambda e, s=s, i=i, hb=hb: e.dma_start(
            out=sb['cs_hsT'][:, hb * INTER:(hb + 1) * INTER], in_=dram['xch_S'].ap()[s, i]),
            reads=['xch_ready'], writes=[f'cs_hsT:{hb}'], tag='sload')
        p.op('tensor', lambda e, j=j: e.transpose(
            ps['pb2'][0:1, 0:128], asg[:, 16 + j:17 + j], sb['c_ident32'][:, :]),
            reads=['asg', 'c_ident32'], writes=['pb2'], tag='wtr')
        p.op('vector', lambda e: e.tensor_copy(sb['cs_drow'][:, :], ps['pb2'][0:1, 0:128]),
             reads=['pb2'], writes=['cs_drow'], tag='wtrev')
        p.op('tensor', lambda e: e.matmul(ps['pb2'][:, 0:128], sb['c_ones_row_f32'][:, :],
                                          sb['cs_drow'][:, :], start=True, stop=True),
             reads=['cs_drow', 'c_ones_row_f32'], writes=['pb2'], tag='wrepmm')
        p.op('vector', lambda e: e.tensor_copy(sb['cs_drep'][:, :], ps['pb2'][:, 0:128]),
             reads=['pb2'], writes=['cs_drep'], tag='wrepev')
        p.op('vector', lambda e, hb=hb: e.tensor_tensor(
            sb['cs_y1'][:, :].rearrange('n (h q) -> n h q', q=P),
            sb['cs_hsT'][:, hb * INTER:(hb + 1) * INTER].rearrange('n (h q) -> n h q', q=P),
            sb['cs_drep'][:, :].unsqueeze(2).broadcast_to((128, 128, P)), ALU.mult),
            reads=[f'cs_hsT:{hb}', 'cs_drep'], writes=['cs_y1'], tag='imul')
        p.op('vector', lambda e: e.tensor_tensor(sb['prefix'][:, :], sb['prefix'][:, :],
                                                 sb['cs_y1'][:, :], ALU.add),
             reads=['prefix', 'cs_y1'], writes=['prefix'], tag='iadd')
    p.dma('gpsimd', lambda e: e.dma_start(out=dram['p_I'].ap(), in_=sb['prefix'][:, :]),
          reads=['prefix'], writes=['p_I'], tag='ipark')


def _slab(b, p, st_ap, oc, n_k, wkeys, tag):
    f32k, bfk, f32t, bft, bfofs = wkeys
    sb = b.sb
    nq = (n_k * 128) // 1024
    for qtr in range(nq):
        wq = qtr % 2
        p.dma('sync', lambda e, oc=oc, qtr=qtr, wq=wq: e.dma_start(
            out=sb[f32t][:, wq * 1024:(wq + 1) * 1024].rearrange('p (a f) -> p a f', f=128),
            in_=st_ap[oc].rearrange('a p f -> p a f')[:, qtr * 8:(qtr + 1) * 8, :]),
            writes=[f'{f32k}:{wq}'], tag=tag)
        p.op('scalar', lambda e, qtr=qtr, wq=wq: e.copy(
            sb[bft][:, bfofs + qtr * 1024:bfofs + (qtr + 1) * 1024],
            sb[f32t][:, wq * 1024:(wq + 1) * 1024]),
            reads=[f'{f32k}:{wq}'], writes=[bfk], tag=tag + 'c')


def ph_ffn_a(b, p):
    sb, ps, din, dram = b.sb, b.ps, b.din, b.dram
    xn = sb['xn']
    xn_deps = [f'xn:{k}' for k in range(NKC)]
    p.dma('sync', lambda e: e.dma_start(
        out=sb['ff_gw'][:, :].rearrange('p (a f) -> p a f', f=8),
        in_=din['gatew'].ap().rearrange('a p f -> p a f')),
        writes=['ff_gw'], tag='gw')
    p.op('vector', lambda e: e.tensor_copy(sb['ff_gwb'][:, :], sb['ff_gw'][:, :]),
         reads=['ff_gw'], writes=['ff_gwb'], tag='gwc')
    for tt in range(8):
        bank, bk = (ps['pb0'], 'pb0') if tt % 2 == 0 else (ps['pb1'], 'pb1')
        for a in range(NKC):
            p.op('tensor', lambda e, a=a, tt=tt, bank=bank: e.matmul(
                bank[:, 0:8], xn[:, a * W + 128 + tt * 128:a * W + 128 + (tt + 1) * 128],
                sb['ff_gwb'][:, a * 8:(a + 1) * 8],
                start=(a == 0), stop=(a == NKC - 1)),
                reads=['ff_gwb', xn_deps[a]], writes=[bk], tag='lgt')
        p.op('vector', lambda e, bank=bank: e.tensor_copy(sb['ff_lgt'][:, :], bank[:, 0:8]),
             reads=[bk], writes=['ff_lgt'], tag='lgtev')
        for i in range(8):
            p.op('vector', lambda e, i=i: e.tensor_scalar_add(
                sb['ff_pairs'][:, i * 8:(i + 1) * 8], sb['ff_lgt'][:, :],
                sb['ff_lgt'][:, i:i + 1]),
                reads=['ff_lgt'], writes=['ff_pairs'], tag='pairs')
        for i in range(8):
            p.op('vector', lambda e, i=i: e.memset(sb['ff_pairs'][:, i * 9:i * 9 + 1], -1e30),
                 writes=['ff_pairs'], tag='pdiag')
        p.op('vector', lambda e, tt=tt: e.reduce_max(sb['ff_s'][:, tt:tt + 1],
                                                     sb['ff_pairs'][:, :], axis=AX.X),
             reads=['ff_pairs'], writes=['ff_s'], tag='top2')
    for tt in range(8):
        p.op('tensor', lambda e, tt=tt: e.transpose(
            ps['pb2'][0:1, 0:128], sb['ff_s'][:, tt:tt + 1], sb['c_ident32'][:, :]),
            reads=['ff_s', 'c_ident32'], writes=['pb2'], tag='str')
        p.op('vector', lambda e, tt=tt: e.tensor_copy(
            sb['ff_srow'][0:1, tt * 128:(tt + 1) * 128], ps['pb2'][0:1, 0:128]),
            reads=['pb2'], writes=['ff_srow'], tag='strev')
    for half in range(2):
        p.op('tensor', lambda e, half=half: e.matmul(
            ps['pb2'][:, 0:512], sb['c_ones_row_f32'][:, :],
            sb['ff_srow'][0:1, half * 512:(half + 1) * 512], start=True, stop=True),
            reads=['ff_srow', 'c_ones_row_f32'], writes=['pb2'], tag='srep')
        p.op('vector', lambda e, half=half: e.tensor_copy(
            sb['ff_srep'][:, half * 512:(half + 1) * 512], ps['pb2'][:, 0:512]),
            reads=['pb2'], writes=['ff_srep'], tag='srepev')
    for oc in range(16):
        _slab(b, p, din['fc1_st'].ap(), oc, NKC, ('ff_wf32', 'ff_wbf', 'ff_wf32', 'ff_wbf', 0), 'fc1w')
        for tb in range(2):
            bank, bk = (ps['pb0'], 'pb0') if tb == 0 else (ps['pb1'], 'pb1')
            for a in range(NKC):
                p.op('tensor', lambda e, a=a, tb=tb, bank=bank: e.matmul(
                    bank[:, 0:512], sb['ff_wbf'][:, a * 128:(a + 1) * 128],
                    xn[:, a * W + 128 + tb * 512:a * W + 128 + (tb + 1) * 512],
                    start=(a == 0), stop=(a == NKC - 1)),
                    reads=['ff_wbf', xn_deps[a]], writes=[bk], tag='fc1mm')
            p.op('vector', lambda e, tb=tb, oc=oc, bank=bank: e.tensor_tensor(
                sb['ff_xle'][:, (oc % 2) * TOK + tb * 512:(oc % 2) * TOK + (tb + 1) * 512],
                bank[:, 0:512], sb['ff_srep'][:, tb * 512:(tb + 1) * 512], ALU.mult),
                reads=[bk, 'ff_srep'], writes=[f'ff_xle:{oc % 2}'], tag='fc1ev')
        p.dma('gpsimd', lambda e, oc=oc: e.dma_start(
            out=dram['p_xlat'].ap()[oc], in_=sb['ff_xle'][:, (oc % 2) * TOK:(oc % 2 + 1) * TOK]),
            reads=[f'ff_xle:{oc % 2}'], tag='xlpark')
    for ic in range(64):
        _slab(b, p, din['upw_st'].ap(), ic, NKC, ('ff_wf32', 'ff_wbf', 'ff_wf32', 'ff_wbf', 0), 'upw')
        for tb in range(2):
            bank, bk = (ps['pb0'], 'pb0') if tb == 0 else (ps['pb1'], 'pb1')
            for a in range(NKC):
                p.op('tensor', lambda e, a=a, tb=tb, bank=bank: e.matmul(
                    bank[:, 0:512], sb['ff_wbf'][:, a * 128:(a + 1) * 128],
                    xn[:, a * W + 128 + tb * 512:a * W + 128 + (tb + 1) * 512],
                    start=(a == 0), stop=(a == NKC - 1)),
                    reads=['ff_wbf', xn_deps[a]], writes=[bk], tag='upmm')
            p.op('scalar', lambda e, tb=tb, ic=ic, bank=bank: e.activation(
                sb['ff_hrow'][:, (ic % 2) * TOK + tb * 512:(ic % 2) * TOK + (tb + 1) * 512],
                bank[:, 0:512], AF.Relu),
                reads=[bk], writes=[f'ff_hrow:{ic % 2}'], tag='uprelu')
        p.op('vector', lambda e, ic=ic: e.tensor_tensor(
            sb['ff_hrow'][:, (ic % 2) * TOK:(ic % 2 + 1) * TOK],
            sb['ff_hrow'][:, (ic % 2) * TOK:(ic % 2 + 1) * TOK],
            sb['ff_hrow'][:, (ic % 2) * TOK:(ic % 2 + 1) * TOK], ALU.mult),
            reads=[f'ff_hrow:{ic % 2}'], writes=[f'ff_hrow:{ic % 2}'], tag='upsq')
        p.dma('gpsimd', lambda e, ic=ic: e.dma_start(
            out=dram['p_h'].ap()[ic], in_=sb['ff_hrow'][:, (ic % 2) * TOK:(ic % 2 + 1) * TOK]),
            reads=[f'ff_hrow:{ic % 2}'], tag='hpark')


def ph_ffn_b(b, p):
    sb, ps, din, dram = b.sb, b.ps, b.din, b.dram
    for th in range(2):
        for ic in range(64):
            p.dma('sync', lambda e, ic=ic, th=th: e.dma_start(
                out=sb['fb_h'][:, ic * 512:(ic + 1) * 512],
                in_=dram['p_h'].ap()[ic][:, th * 512:(th + 1) * 512]),
                writes=[f'fb_h:{ic}'], tag='hload')
        for k in range(16):
            p.dma('sync', lambda e, k=k, th=th: e.dma_start(
                out=sb['fb_xlat'][:, k * 512:(k + 1) * 512],
                in_=dram['p_xlat'].ap()[k][:, th * 512:(th + 1) * 512]),
                writes=[f'fb_xlat:{k}'], tag='xlload')
        for oc in range(16):
            _slab(b, p, din['exp_st'].ap(), oc, 16, ('fb_wf32', 'fb_wbf', 'fb_wf32', 'fb_wbf', 0), 'expw')
            bank, bk = (ps['pb0'], 'pb0') if oc % 2 == 0 else (ps['pb1'], 'pb1')
            for a in range(16):
                p.op('tensor', lambda e, a=a, oc=oc, bank=bank: e.matmul(
                    bank[:, 0:512], sb['fb_wbf'][:, a * 128:(a + 1) * 128],
                    sb['fb_xlat'][:, a * 512:(a + 1) * 512],
                    start=(a == 0), stop=(a == 15)),
                    reads=['fb_wbf', f'fb_xlat:{a}'], writes=[bk], tag='expmm')
            p.op('vector', lambda e, oc=oc, bank=bank: e.tensor_copy(
                sb['fb_exp'][:, oc * 512:(oc + 1) * 512], bank[:, 0:512]),
                reads=[bk], writes=[f'fb_exp:{oc}'], tag='expev')
        for hc in range(32):
            bank, bk = (ps['pb0'], 'pb0') if hc % 2 == 0 else (ps['pb1'], 'pb1')
            _slab(b, p, din['fc2_st'].ap(), hc, 16, ('fb_wf32', 'fb_wbf', 'fb_wf32', 'fb_wbf', 0), 'fc2w')
            for a in range(16):
                p.op('tensor', lambda e, a=a, bank=bank: e.matmul(
                    bank[:, 0:512], sb['fb_wbf'][:, a * 128:(a + 1) * 128],
                    sb['fb_exp'][:, a * 512:(a + 1) * 512],
                    start=(a == 0), stop=False),
                    reads=['fb_wbf', f'fb_exp:{a}'], writes=[bk], tag='fc2mm')
            for half in range(2):
                for qtr in range(4):
                    wq = qtr % 2
                    p.dma('sync', lambda e, hc=hc, half=half, qtr=qtr, wq=wq: e.dma_start(
                        out=sb['fb_wf32'][:, wq * 1024:(wq + 1) * 1024]
                        .rearrange('p (a f) -> p a f', f=128),
                        in_=din['dnw_st'].ap()[hc].rearrange('a p f -> p a f')
                        [:, half * 32 + qtr * 8:half * 32 + (qtr + 1) * 8, :]),
                        writes=[f'fb_wf32:{wq}'], tag='dnw')
                    p.op('scalar', lambda e, qtr=qtr, wq=wq: e.copy(
                        sb['fb_wbf'][:, qtr * 1024:(qtr + 1) * 1024],
                        sb['fb_wf32'][:, wq * 1024:(wq + 1) * 1024]),
                        reads=[f'fb_wf32:{wq}'], writes=['fb_wbf'], tag='dnwc')
                for a in range(32):
                    ag = half * 32 + a
                    p.op('tensor', lambda e, a=a, ag=ag, bank=bank: e.matmul(
                        bank[:, 0:512], sb['fb_wbf'][:, a * 128:(a + 1) * 128],
                        sb['fb_h'][:, ag * 512:(ag + 1) * 512],
                        start=False, stop=(ag == 63)),
                        reads=['fb_wbf', f'fb_h:{ag}'], writes=[bk], tag='dnmm')
            p.op('vector', lambda e, hc=hc, bank=bank: e.tensor_copy(
                sb['fb_pre'][:, (hc % 2) * 512:(hc % 2 + 1) * 512], bank[:, 0:512]),
                reads=[bk], writes=[f'fb_pre:{hc % 2}'], tag='dnev')
            p.dma('gpsimd', lambda e, hc=hc, th=th: e.dma_start(
                out=dram['p_fpre'].ap()[hc][:, th * 512:(th + 1) * 512],
                in_=sb['fb_pre'][:, (hc % 2) * 512:(hc % 2 + 1) * 512]),
                reads=[f'fb_pre:{hc % 2}'], writes=[f'p_fpre:{hc}:{th}'], tag='fprepark')
            p.op('scalar', lambda e, hc=hc: e.activation(
                sb['fb_sq'][:, :], sb['fb_pre'][:, (hc % 2) * 512:(hc % 2 + 1) * 512], AF.Square),
                reads=[f'fb_pre:{hc % 2}'], writes=['fb_sq'], tag='fsq')
            p.op('tensor', lambda e, hc=hc: e.matmul(
                ps['pb4'][0:1, 0:512], sb['c_ones_col_bf'][:, :],
                sb['fb_pre'][:, (hc % 2) * 512:(hc % 2 + 1) * 512],
                start=(hc == 0), stop=(hc == 31)),
                reads=[f'fb_pre:{hc % 2}', 'c_ones_col'], writes=['pb4'], tag='fsum')
            p.op('tensor', lambda e, hc=hc: e.matmul(
                ps['pb5'][0:1, 0:512], sb['c_ones_col_bf'][:, :], sb['fb_sq'][:, :],
                start=(hc == 0), stop=(hc == 31)),
                reads=['fb_sq', 'c_ones_col'], writes=['pb5'], tag='fsq2')
        r = sb['fb_rows']

        def R(i):
            return r[0:1, i * 512:(i + 1) * 512]
        p.op('vector', lambda e: e.tensor_scalar_mul(R(0), ps['pb4'][0:1, 0:512], 1.0 / HID),
             reads=['pb4'], writes=['fb_mean'], tag='fln')
        p.op('vector', lambda e: e.tensor_scalar_mul(R(1), ps['pb5'][0:1, 0:512], 1.0 / HID),
             reads=['pb5'], writes=['fb_msq'], tag='fln')
        p.op('vector', lambda e: e.tensor_mul(R(2), R(0), R(0)),
             reads=['fb_mean'], writes=['fb_m2'], tag='fln')
        p.op('vector', lambda e: e.tensor_sub(R(3), R(1), R(2)),
             reads=['fb_msq', 'fb_m2'], writes=['fb_var'], tag='fln')
        p.op('scalar', lambda e: e.activation(R(4), R(3), AF.Ln, bias=sb['c_eps'][0:1, :]),
             reads=['fb_var', 'c_eps'], writes=['fb_lnv'], tag='fln')
        p.op('scalar', lambda e: e.activation(R(5), R(4), AF.Exp, scale=sb['c_mhalf'][0:1, :]),
             reads=['fb_lnv', 'c_mhalf'], writes=['fb_rstd'], tag='fln')
        p.op('vector', lambda e: e.tensor_copy(sb['fb_rowbf'][0:1, 0:512], R(0)),
             reads=['fb_mean'], writes=['fb_mean_bf'], tag='fln')
        p.op('vector', lambda e: e.tensor_copy(sb['fb_rowbf'][0:1, 512:1024], R(5)),
             reads=['fb_rstd'], writes=['fb_rstd_bf'], tag='fln')
        p.op('tensor', lambda e: e.matmul(ps['pb2'][:, 0:512], sb['c_ones_row_bf'][:, :],
                                          sb['fb_rowbf'][0:1, 0:512], start=True, stop=True),
             reads=['fb_mean_bf', 'c_ones_row'], writes=['pb2'], tag='fln')
        p.op('scalar', lambda e: e.copy(sb['fb_mrep'][:, :], ps['pb2'][:, 0:512]),
             reads=['pb2'], writes=['fb_mrep'], tag='fln')
        p.op('tensor', lambda e: e.matmul(ps['pb3'][:, 0:512], sb['c_ones_row_bf'][:, :],
                                          sb['fb_rowbf'][0:1, 512:1024], start=True, stop=True),
             reads=['fb_rstd_bf', 'c_ones_row'], writes=['pb3'], tag='fln')
        p.op('scalar', lambda e: e.copy(sb['fb_rrep'][:, :], ps['pb3'][:, 0:512]),
             reads=['pb3'], writes=['fb_rrep'], tag='fln')
        for hc in range(32):
            s2 = hc % 2
            p.dma('sync', lambda e, hc=hc, th=th, s2=s2: e.dma_start(
                out=sb['fb_ln'][:, s2 * 512:(s2 + 1) * 512],
                in_=dram['p_fpre'].ap()[hc][:, th * 512:(th + 1) * 512]),
                reads=[f'p_fpre:{hc}:{th}'], writes=[f'fb_ln:{s2}'], tag='fpreload')
            p.op('vector', lambda e, s2=s2: e.tensor_sub(
                sb['fb_ln'][:, s2 * 512:(s2 + 1) * 512],
                sb['fb_ln'][:, s2 * 512:(s2 + 1) * 512],
                sb['fb_mrep'][:, :]),
                reads=[f'fb_ln:{s2}', 'fb_mrep'], writes=[f'fb_ln:{s2}'], tag='fsub')
            p.op('vector', lambda e, s2=s2: e.tensor_tensor(
                sb['fb_ln'][:, s2 * 512:(s2 + 1) * 512],
                sb['fb_ln'][:, s2 * 512:(s2 + 1) * 512],
                sb['fb_rrep'][:, :], ALU.mult),
                reads=[f'fb_ln:{s2}', 'fb_rrep'], writes=[f'fb_ln:{s2}'], tag='fmul')
            p.op('vector', lambda e, hc=hc, s2=s2: e.scalar_tensor_tensor(
                sb['fb_sq'][:, :], sb['fb_ln'][:, s2 * 512:(s2 + 1) * 512],
                sb['c_lng'][:, hc:hc + 1],
                sb['c_lnb'][:, hc:hc + 1].broadcast_to((128, 512)),
                ALU.mult, ALU.add),
                reads=[f'fb_ln:{s2}', 'c_lng', 'c_lnb'], writes=['fb_sq'], tag='faff')
            p.dma('gpsimd', lambda e, hc=hc, th=th: e.dma_start(
                out=dram['p_fout'].ap()[hc][:, th * 512:(th + 1) * 512],
                in_=sb['fb_sq'][:, :]),
                reads=['fb_sq'], writes=[f'p_fout:{hc}:{th}'], tag='foutpark')


def ph_yoff(b, p):
    sb, ps, dram = b.sb, b.ps, b.dram
    p.op('vector', lambda e: e.memset(sb['c_dI'][:, 0:1], 1.0), writes=['c_dI'], tag='g')
    for c in range(NCH):
        p.dma('sync', lambda e, c=c: e.dma_start(out=sb['gs_y1'][:, :], in_=dram['p_y1'].ap()[c]),
              reads=[f'p_y1:{c}'], writes=['gs_y1'], tag='gy1')
        p.dma('sync', lambda e, c=c: e.dma_start(out=sb['gs_lp'][:, :], in_=dram['p_lp'].ap()[c]),
              reads=[f'p_lp:{c}'], writes=['gs_lp'], tag='glp')
        p.dma('sync', lambda e, c=c: e.dma_start(
            out=sb['gs_C'][:, :].rearrange('p (g t) -> p g t', t=128),
            in_=dram['p_C'].ap().rearrange('g p t -> p g t')[:, :, c * 128:(c + 1) * 128]),
            writes=['gs_C'], tag='gC')
        p.dma('sync', lambda e, c=c: e.dma_start(out=sb['gs_sdoT'][:, :], in_=dram['p_sdoT'].ap()[c]),
              reads=[f'p_sdoT:{c}'], writes=['gs_sdoT'], tag='gsdo')
        p.dma('sync', lambda e, c=c: e.dma_start(
            out=sb['gs_dcol'][:, 0:1], in_=dram['p_acs'].ap()[0][:, c * 128 + 127:c * 128 + 128],
            allow_slow_non_contiguous=True),
            writes=['gs_dcol'], tag='gd')
        p.op('scalar', lambda e: e.activation(sb['gs_dcol'][:, 1:2], sb['gs_dcol'][:, 0:1], AF.Exp),
             reads=['gs_dcol'], writes=['gs_dcol'], tag='gdexp')
        p.op('tensor', lambda e: e.transpose(ps['pb2'][0:1, 0:128], sb['c_dI'][:, 0:1],
                                             sb['c_ident32'][:, :]),
             reads=['c_dI', 'c_ident32'], writes=['pb2'], tag='gtr')
        p.op('vector', lambda e: e.tensor_copy(sb['gs_drow'][:, :], ps['pb2'][0:1, 0:128]),
             reads=['pb2'], writes=['gs_drow'], tag='gtrev')
        p.op('tensor', lambda e: e.matmul(ps['pb2'][:, 0:128], sb['c_ones_row_f32'][:, :],
                                          sb['gs_drow'][:, :], start=True, stop=True),
             reads=['gs_drow', 'c_ones_row_f32'], writes=['pb2'], tag='gmm')
        p.op('vector', lambda e: e.tensor_copy(sb['gs_dIrep'][:, :], ps['pb2'][:, 0:128]),
             reads=['pb2'], writes=['gs_dIrep'], tag='gmmev')
        p.op('vector', lambda e: e.tensor_tensor(
            sb['gs_prev'][:, :].rearrange('n (h q) -> n h q', q=P),
            sb['prefix'][:, :].rearrange('n (h q) -> n h q', q=P),
            sb['gs_dIrep'][:, :].unsqueeze(2).broadcast_to((128, 128, P)), ALU.mult),
            reads=['prefix', 'gs_dIrep'], writes=['gs_prev'], tag='gprev')
        p.op('vector', lambda e: e.tensor_tensor(sb['gs_prev'][:, :], sb['gs_prev'][:, :],
                                                 sb['gs_lp'][:, :], ALU.add),
             reads=['gs_prev', 'gs_lp'], writes=['gs_prev'], tag='gprev2')
        p.op('vector', lambda e: e.tensor_copy(sb['gs_prevbf'][:, :], sb['gs_prev'][:, :]),
             reads=['gs_prev'], writes=['gs_prevbf'], tag='gprevbf')
        p.op('vector', lambda e: e.tensor_tensor(sb['c_dI'][:, 0:1], sb['c_dI'][:, 0:1],
                                                 sb['gs_dcol'][:, 1:2], ALU.mult),
             reads=['c_dI', 'gs_dcol', 'gs_dIrep'], writes=['c_dI'], tag='gdI')
        for w in range(16):
            bank, bk = (ps['pb0'], 'pb0') if w % 2 == 0 else (ps['pb1'], 'pb1')
            for j in range(8):
                p.op('tensor', lambda e, j=j, w=w, bank=bank: e.matmul(
                    bank[:, j * 64:(j + 1) * 64], sb['gs_C'][:, j * 128:(j + 1) * 128],
                    sb['gs_prevbf'][:, (8 * w + j) * 64:(8 * w + j + 1) * 64],
                    start=True, stop=True),
                    reads=['gs_C', 'gs_prevbf'], writes=[bk], tag='yoffmm')
            p.op('vector', lambda e, w=w, bank=bank: e.tensor_tensor(
                sb['gs_t'][:, :].rearrange('p (h q) -> p h q', q=P),
                bank[:, 0:512].rearrange('p (h q) -> p h q', q=P),
                sb['gs_sdoT'][:, 8 * w:8 * w + 8].unsqueeze(2).broadcast_to((128, 8, P)),
                ALU.mult),
                reads=[bk, 'gs_sdoT'], writes=['gs_t'], tag='yoffsdo')
            p.op('vector', lambda e, w=w: e.tensor_tensor(
                sb['gs_ytok'][:, w * 512:(w + 1) * 512], sb['gs_t'][:, :],
                sb['gs_y1'][:, w * 512:(w + 1) * 512], ALU.add),
                reads=['gs_t', 'gs_y1'], writes=['gs_ytok'], tag='yadd')
        for j in range(64):
            trb = ps['pb6'] if (j % 2) == 0 else ps['pb7']
            trk = f'pb{6 + (j % 2)}'
            p.op('tensor', lambda e, j=j, trb=trb: e.transpose(
                trb[:, 0:128], sb['gs_ytok'][:, j * 128:(j + 1) * 128], sb['c_ident_bf'][:, :]),
                reads=['gs_ytok', 'c_ident_bf'], writes=[trk], tag='ytr')
            p.op('vector', lambda e, j=j, trb=trb: e.tensor_copy(
                sb['gs_yf'][:, j * 128:(j + 1) * 128], trb[:, 0:128]),
                reads=[trk], writes=['gs_yf'], tag='ytrev')
        for j in range(64):
            p.dma('gpsimd', lambda e, j=j, c=c: e.dma_start(
                out=dram['p_yf'].ap()[j][:, c * 128:(c + 1) * 128],
                in_=sb['gs_yf'][:, j * 128:(j + 1) * 128]),
                reads=['gs_yf'], writes=[f'p_yf:{j}:{c}'], tag='yfpark')


def ph_final(b, p):
    sb, ps, din, dram = b.sb, b.ps, b.din, b.dram
    yfdeps = lambda j: [f'p_yf:{j}:{c}' for c in range(NCH)]
    for g in range(8):
        for jc in range(8):
            j = g * 8 + jc
            s = jc % 2
            p.dma('sync', lambda e, j=j, s=s: e.dma_start(
                out=sb['hs_y'][:, s * TOK:(s + 1) * TOK], in_=dram['p_yf'].ap()[j]),
                reads=yfdeps(j), writes=[f'hs_y:{s}'], tag='hy')
            p.dma('sync', lambda e, j=j, s=s: e.dma_start(
                out=sb['hs_sg'][:, s * TOK:(s + 1) * TOK], in_=dram['p_sgate'].ap()[j]),
                writes=[f'hs_sg:{s}'], tag='hsg')
            p.op('vector', lambda e, j=j, s=s: e.tensor_tensor(
                sb['hn'][:, j * TOK:(j + 1) * TOK], sb['hs_y'][:, s * TOK:(s + 1) * TOK],
                sb['hs_sg'][:, s * TOK:(s + 1) * TOK], ALU.mult),
                reads=[f'hs_y:{s}', f'hs_sg:{s}'], writes=[f'hn:{j}'], tag='xg')
            p.op('scalar', lambda e, j=j: e.activation(sb['hs_sq'][:, :],
                                                       sb['hn'][:, j * TOK:(j + 1) * TOK], AF.Square),
                 reads=[f'hn:{j}'], writes=['hs_sq'], tag='xgsq')
            for half in range(2):
                bank, bk = (ps['pb4'], 'pb4') if half == 0 else (ps['pb5'], 'pb5')
                p.op('tensor', lambda e, jc=jc, half=half, bank=bank: e.matmul(
                    bank[0:1, 0:512], sb['c_ones_col_bf'][:, :],
                    sb['hs_sq'][:, half * 512:(half + 1) * 512],
                    start=(jc == 0), stop=(jc == 7)),
                    reads=['hs_sq', 'c_ones_col'], writes=[bk], tag='rmssum')
        r = sb['hs_rows']
        for half in range(2):
            bank, bk = (ps['pb4'], 'pb4') if half == 0 else (ps['pb5'], 'pb5')
            p.op('vector', lambda e, half=half, bank=bank: e.tensor_scalar_mul(
                r[0:1, half * 512:(half + 1) * 512], bank[0:1, 0:512], 1.0 / (INTER // G)),
                reads=[bk], writes=['hs_msq'], tag='rms')
        p.op('scalar', lambda e: e.activation(r[0:1, TOK:2 * TOK], r[0:1, 0:TOK],
                                              AF.Ln, bias=sb['c_eps'][0:1, :]),
             reads=['hs_msq', 'c_eps'], writes=['hs_lnv'], tag='rms')
        p.op('scalar', lambda e: e.activation(r[0:1, TOK:2 * TOK], r[0:1, TOK:2 * TOK],
                                              AF.Exp, scale=sb['c_mhalf'][0:1, :]),
             reads=['hs_lnv', 'c_mhalf'], writes=['hs_rstd'], tag='rms')
        p.op('vector', lambda e: e.tensor_copy(sb['hs_rowbf'][0:1, :], r[0:1, TOK:2 * TOK]),
             reads=['hs_rstd'], writes=['hs_rowbf'], tag='rms')
        for half in range(2):
            p.op('tensor', lambda e, half=half: e.matmul(
                ps['pb2'][:, 0:512], sb['c_ones_row_bf'][:, :],
                sb['hs_rowbf'][0:1, half * 512:(half + 1) * 512], start=True, stop=True),
                reads=['hs_rowbf', 'c_ones_row'], writes=['pb2'], tag='rmsrep')
            p.op('scalar', lambda e, half=half: e.copy(
                sb['hs_rep'][:, half * 512:(half + 1) * 512], ps['pb2'][:, 0:512]),
                reads=['pb2'], writes=['hs_rep'], tag='rmsrepev')
        for jc in range(8):
            j = g * 8 + jc
            p.op('vector', lambda e, j=j: e.tensor_tensor(
                sb['hn'][:, j * TOK:(j + 1) * TOK], sb['hn'][:, j * TOK:(j + 1) * TOK],
                sb['hs_rep'][:, :], ALU.mult),
                reads=[f'hn:{j}', 'hs_rep'], writes=[f'hn:{j}'], tag='rmsapply')
            p.op('vector', lambda e, j=j: e.tensor_scalar_mul(
                sb['hn'][:, j * TOK:(j + 1) * TOK], sb['hn'][:, j * TOK:(j + 1) * TOK],
                sb['c_nw'][:, j:j + 1]),
                reads=[f'hn:{j}', 'c_nw'], writes=[f'hn:{j}'], tag='nw')
    p.barrier()
    hndeps = [f'hn:{j}' for j in range(64)]
    xk = din['xT_own'].ap().rearrange('(a p) w -> a p w', p=128)
    outk = []
    for hc in range(32):
        for half in range(2):
            hs2 = half % 2
            for qtr in range(4):
                p.dma('sync', lambda e, hc=hc, half=half, qtr=qtr: e.dma_start(
                    out=sb['ho_wf32'][:, :].rearrange('p (a f) -> p a f', f=128),
                    in_=din['opw_st'].ap()[hc].rearrange('a p f -> p a f')
                    [:, half * 32 + qtr * 8:half * 32 + (qtr + 1) * 8, :]),
                    writes=['ho_wf32'], tag='ow')
                p.op('scalar', lambda e, hs2=hs2, qtr=qtr: e.copy(
                    sb['ho_wbf'][:, hs2 * 4096 + qtr * 1024:hs2 * 4096 + (qtr + 1) * 1024],
                    sb['ho_wf32'][:, :]),
                    reads=['ho_wf32'], writes=[f'ho_wbf:{hs2}'], tag='owc')
        for tb in range(2):
            bank, bk = (ps['pb0'], 'pb0') if tb == 0 else (ps['pb1'], 'pb1')
            for half in range(2):
                hs2 = half % 2
                for a in range(32):
                    k = half * 32 + a
                    p.op('tensor', lambda e, a=a, k=k, hs2=hs2, tb=tb, bank=bank: e.matmul(
                        bank[:, 0:512], sb['ho_wbf'][:, hs2 * 4096 + a * 128:hs2 * 4096 + (a + 1) * 128],
                        sb['hn'][:, k * TOK + tb * 512:k * TOK + tb * 512 + 512],
                        start=(k == 0), stop=(k == 63)),
                        reads=[f'ho_wbf:{hs2}'] + hndeps, writes=[bk], tag='opmm')
            p.dma('scalar', lambda e, hc=hc, tb=tb: e.dma_start(
                out=sb['ho_x'][:, tb * 512:(tb + 1) * 512],
                in_=xk[hc][:, 128 + tb * 512:128 + (tb + 1) * 512]),
                writes=[f'ho_x:{tb}'], tag='ox')
            p.dma('scalar', lambda e, hc=hc, tb=tb: e.dma_start(
                out=sb['ho_f'][:, tb * 512:(tb + 1) * 512],
                in_=dram['p_fout'].ap()[hc][:, tb * 512:(tb + 1) * 512]),
                reads=[f'p_fout:{hc}:{tb}'], writes=[f'ho_f:{tb}'], tag='of')
            p.op('vector', lambda e, tb=tb, bank=bank: e.tensor_tensor(
                sb['ho_o'][:, tb * 512:(tb + 1) * 512], bank[:, 0:512],
                sb['ho_x'][:, tb * 512:(tb + 1) * 512], ALU.add),
                reads=[bk, f'ho_x:{tb}'], writes=[f'ho_o:{tb}'], tag='oadd')
            p.op('vector', lambda e, tb=tb: e.tensor_tensor(
                sb['ho_o'][:, tb * 512:(tb + 1) * 512], sb['ho_o'][:, tb * 512:(tb + 1) * 512],
                sb['ho_f'][:, tb * 512:(tb + 1) * 512], ALU.add),
                reads=[f'ho_o:{tb}', f'ho_f:{tb}'], writes=[f'ho_o:{tb}'], tag='oadd2')
            p.dma('gpsimd', lambda e, hc=hc, tb=tb: e.dma_start(
                out=dram['out_fm'].ap().rearrange('(a p) t -> a p t', p=128)[hc]
                [:, tb * 512:(tb + 1) * 512],
                in_=sb['ho_o'][:, tb * 512:(tb + 1) * 512]),
                reads=[f'ho_o:{tb}'], writes=[f'out:{hc}:{tb}'], tag='outdma')
            outk.append(f'out:{hc}:{tb}')
    p.op('gpsimd', lambda e: e.memset(sb['c_magic'][:, 2:3], 0),
         reads=outk, writes=['final'], tag='final')


def ph_out_stub(b, p):
    sb = b.sb
    p.op('vector', lambda e: e.memset(sb['ln_x'][:, 0:TOK], 0.0),
         writes=['ln_x:0'], tag='outz')
    outk = []
    for kc in range(NKC):
        p.dma('gpsimd', lambda e, kc=kc: e.dma_start(
            out=b.dram['out_fm'].ap().rearrange('(a p) t -> a p t', p=128)[kc],
            in_=sb['ln_x'][:, 0:TOK]),
            reads=['ln_x:0'], writes=[f'out:{kc}'], tag='outdma')
        outk.append(f'out:{kc}')
    p.op('gpsimd', lambda e: e.memset(sb['c_magic'][:, 2:3], 0),
         reads=outk, writes=['final'], tag='final')



def build_program(debug=False, stage=BUILD_STAGE):
    b = Builder(debug)
    declare(b, stage)
    carve_lnip(b)
    p = Prog(b)
    ph_consts(b, p)
    ph_ln(b, p, 'own')
    if debug:
        for k in range(NKC):
            p.dma('gpsimd', lambda e, k=k: e.dma_start(
                out=b.dram['p_xn'].ap()[k], in_=b.sb['xn'][:, k * W:(k + 1) * W]),
                reads=[f'xn:{k}'], tag='dbgxn')
    p.barrier()
    if stage >= 2:
        ph_inproj(b, p, 'own')
        p.barrier()
    if stage >= 3:
        ph_scan(b, p, 'own')
        p.barrier()
    if stage >= 4:
        ph_park_xn(b, p)
        p.barrier()
        ph_ln(b, p, 'ast')
        p.barrier()
        ph_inproj(b, p, 'ast')
        p.barrier()
        ph_scan(b, p, 'ast')
        p.custom(make_exchange_emitter(b))
        ph_combine_I(b, p)
        p.barrier()
    if stage >= 5:
        ph_load_xn(b, p)
        ph_ffn_a(b, p)
        p.barrier()
        ph_ffn_b(b, p)
        p.barrier()
    if stage >= 6:
        ph_yoff(b, p)
        p.barrier()
        ph_final(b, p)
    else:
        ph_out_stub(b, p)
    p.emit()
    b.prog = p
    return b


def kernel(**inputs):
    per_core = host_prep(inputs)
    bld = build_program(debug=DEBUG)
    res = run_bass_kernel_spmd(
        bld.nc, [{k: np.asarray(v) for k, v in d.items()} for d in per_core],
        core_ids=list(range(8)))
    x = np.asarray(inputs['L0_attn_hidden_states'])
    out = np.empty_like(x)
    for c in range(8):
        bq, q = c // 4, c % 4
        out[bq, q * TOK:(q + 1) * TOK, :] = np.asarray(res.results[c]['out_fm']).T
    return out


# version 02
# speedup vs baseline: 3.3340x; 3.3340x over previous
import numpy as np
import ml_dtypes
from contextlib import ExitStack

import concourse.bass as bass
import concourse.mybir as mybir
from concourse.bass_utils import run_bass_kernel_spmd

F32 = mybir.dt.float32
BF16 = mybir.dt.bfloat16
I32 = mybir.dt.int32
AF = mybir.ActivationFunctionType
ALU = mybir.AluOpType
AX = mybir.AxisListType

HID = 4096
H = 128
P = 64
N = 128
G = 8
KCONV = 4
CH = 128
INTER = 8192
NKC = 32
TOK = 1024
W = 1152
NCH = 8
EPS = 1e-5

FC_DT = [144]
FC_B = list(range(128, 136))
FC_C = list(range(136, 144))
FC_HS = list(range(64, 128))
FC_GATE = list(range(0, 64))

MAGIC_X = 0x5AC70001

DEBUG = False
BUILD_STAGE = 6
IP_LIMIT = 0
ARENA_BYTES = 160 * 1024



def _stream4(wmat, n_oc, n_kc):
    t = wmat.reshape(n_kc, 128, n_oc, 128).transpose(2, 0, 1, 3)
    return np.ascontiguousarray(t)


def host_prep(inputs):
    x = np.asarray(inputs['L0_attn_hidden_states'])
    shared = {}
    shared['ipw_st'] = _stream4(
        np.ascontiguousarray(np.asarray(inputs['L0_attn_in_proj_w']).T), 145, NKC)
    shared['opw_st'] = _stream4(
        np.ascontiguousarray(np.asarray(inputs['L0_attn_out_proj_w']).T), 32, 64)
    shared['upw_st'] = _stream4(np.asarray(inputs['L0_ffn_up_w']), 64, NKC)
    shared['dnw_st'] = _stream4(np.asarray(inputs['L0_ffn_down_w']), 32, 64)
    shared['fc1_st'] = _stream4(np.asarray(inputs['L0_ffn_fc1_w']), 16, NKC)
    shared['fc2_st'] = _stream4(np.asarray(inputs['L0_ffn_fc2_w']), 32, 16)
    shared['exp_st'] = _stream4(np.asarray(inputs['L0_ffn_expert_w']), 16, 16)
    shared['gatew'] = np.ascontiguousarray(
        np.asarray(inputs['L0_ffn_gate_w']).reshape(NKC, 128, 8))
    conv_w = np.asarray(inputs['L0_attn_conv_w'])
    shared['cw'] = np.ascontiguousarray(
        conv_w[:, 0, :].reshape(80, 128, KCONV).transpose(1, 2, 0))
    shared['cb'] = np.ascontiguousarray(
        np.asarray(inputs['L0_attn_conv_b']).reshape(80, 128).T)
    shared['dtb'] = np.ascontiguousarray(np.asarray(inputs['L0_attn_dt_bias']).reshape(128, 1))
    shared['alog'] = np.ascontiguousarray(np.asarray(inputs['L0_attn_A_log']).reshape(128, 1))
    shared['drow'] = np.ascontiguousarray(np.asarray(inputs['L0_attn_D']).reshape(1, 128))
    shared['nw'] = np.ascontiguousarray(np.asarray(inputs['L0_attn_norm_weight']).reshape(64, 128).T)
    shared['lng'] = np.ascontiguousarray(np.asarray(inputs['L0_ffn_ln_gamma']).reshape(32, 128).T)
    shared['lnb'] = np.ascontiguousarray(np.asarray(inputs['L0_ffn_ln_beta']).reshape(32, 128).T)
    shared['ident'] = np.eye(128, dtype=np.float32).astype(ml_dtypes.bfloat16)
    shared['ident32'] = np.eye(128, dtype=np.float32)
    shared['mask_lges'] = np.ascontiguousarray(
        np.triu(np.ones((128, 128), np.float32)).astype(ml_dtypes.bfloat16))

    per_core = []
    for c in range(8):
        bb, q = c // 4, c % 4
        d = dict(shared)

        def tslice(q0):
            lo = q0 * TOK
            seg = x[bb, lo:lo + TOK]
            halo = np.zeros((128, HID), np.float32) if lo == 0 else x[bb, lo - 128:lo]
            return np.ascontiguousarray(np.concatenate([halo, seg], 0).T.astype(np.float32))

        d['xT_own'] = tslice(q)
        d['xT_ast'] = tslice(q - 2 if q >= 2 else q)
        me = c % 2
        ot = 1 - me
        S_OWN = lambda s: 2 * s
        S_AST = lambda s: 2 * s + 1
        Msel = np.zeros((4, 4), np.float32)
        wsel = np.zeros((4,), np.float32)
        if q == 1:
            wsel[S_OWN(ot)] = 1.0
        elif q == 2:
            wsel[S_AST(me)] = 1.0
            Msel[S_AST(me), S_AST(ot)] = 1.0
            wsel[S_AST(ot)] = 1.0
        elif q == 3:
            wsel[S_AST(ot)] = 1.0
            Msel[S_AST(ot), S_AST(me)] = 1.0
            Msel[S_AST(ot), S_OWN(ot)] = 1.0
            wsel[S_AST(me)] = 1.0
            Msel[S_AST(me), S_OWN(ot)] = 1.0
            wsel[S_OWN(ot)] = 1.0
        d['Msel'] = np.ascontiguousarray(
            np.broadcast_to(Msel.reshape(1, 16), (128, 16)).astype(np.float32))
        d['wsel'] = np.ascontiguousarray(
            np.broadcast_to(wsel.reshape(1, 4), (128, 4)).astype(np.float32))
        per_core.append(d)
    return per_core



ENGS = ('sync', 'vector', 'scalar', 'tensor', 'gpsimd')
DMA_QUEUES = ('sync', 'gpsimd', 'scalar')
NPOOL = 8
CAP = 30000


class Op:
    __slots__ = ('eng', 'fn', 'reads', 'writes', 'dma', 'marker', 'deps', 'tag')

    def __init__(self, eng, fn, reads, writes, dma, tag):
        self.eng = eng
        self.fn = fn
        self.reads = tuple(reads)
        self.writes = tuple(writes)
        self.dma = dma
        self.tag = tag


class Prog:
    def __init__(self, bld):
        self.bld = bld
        self.segments = [[]]
        self.seg_names = ['s0']
        self.custom_emitters = {}
        self.fences = set()

    def op(self, eng, fn, reads=(), writes=(), dma=False, tag=''):
        o = Op(eng, fn, reads, writes, dma, tag)
        self.segments[-1].append(o)
        return o

    def dma(self, eng, fn, reads=(), writes=(), tag=''):
        return self.op(eng, fn, reads, writes, dma=True, tag=tag)

    def barrier(self, fence=True, name=None):
        if fence:
            self.fences.add(len(self.segments))
        self.segments.append([])
        self.seg_names.append(name or f's{len(self.segments) - 1}')

    def custom(self, emitter, name=None):
        self.custom_emitters[len(self.segments)] = emitter
        self.segments.append([])
        self.seg_names.append(name or f's{len(self.segments) - 1}')

    def resolve(self, key):
        return self.marker_sem(self.last_write[key])

    def marker_sem(self, mk):
        if mk[0] == 'dma':
            semh = self.qsems[mk[1]][(mk[2] - 1) % NPOOL]
            val = ((mk[2] - 1) // NPOOL + 1) * 16
        else:
            gen = (mk[2] - 1) // CAP
            semh = self.csems[(mk[1], gen)]
            val = ((mk[2] - 1) % CAP) + 1
        return semh, val

    def plan(self):
        bld = self.bld
        dma_count = {qn: 0 for qn in DMA_QUEUES}
        comp_count = {e: 0 for e in ENGS}
        last_write = {}
        readers = {}
        fence_pending = {}
        for si, seg in enumerate(self.segments):
            if si in self.fences:
                snap = [('dma', qn, dma_count[qn]) for qn in DMA_QUEUES if dma_count[qn] > 0]
                fence_pending = {e: list(snap) for e in ENGS}
            for o in seg:
                deps = []
                if fence_pending.get(o.eng):
                    deps.extend(fence_pending[o.eng])
                    fence_pending[o.eng] = []
                for r in o.reads:
                    wsrc = last_write.get(r)
                    if wsrc is not None:
                        deps.append(wsrc)
                for w in o.writes:
                    wsrc = last_write.get(w)
                    if wsrc is not None:
                        deps.append(wsrc)
                    for rd in readers.get(w, ()):
                        deps.append(rd)
                if o.dma:
                    qn = o.eng
                    dma_count[qn] += 1
                    marker = ('dma', qn, dma_count[qn])
                    if dma_count[qn] > NPOOL:
                        deps.append(('dma', qn, dma_count[qn] - NPOOL))
                else:
                    comp_count[o.eng] += 1
                    marker = ('comp', o.eng, comp_count[o.eng])
                o.marker = marker
                o.deps = sorted(set(
                    d for d in deps
                    if d != marker
                    and not (d[0] == 'comp' and d[1] == 'tensor'
                             and o.eng == 'tensor' and not o.dma)))
                for w in o.writes:
                    last_write[w] = marker
                    readers[w] = []
                for r in o.reads:
                    readers.setdefault(r, []).append(marker)
        self.last_write = last_write
        self.qsems = {qn: [bld.sem(f'q_{qn}_{i}') for i in range(NPOOL)]
                      for qn in DMA_QUEUES if dma_count[qn] > 0}
        self.csems = {}
        for e in ENGS:
            for gen in range((comp_count[e] + CAP - 1) // CAP):
                self.csems[(e, gen)] = bld.sem(f'c_{e}_{gen}')
        self.stats = dict(dma=dict(dma_count), comp=dict(comp_count))

    def emit(self):
        self.plan()
        nc = self.bld.nc
        for si, seg in enumerate(self.segments):
            if si in self.custom_emitters:
                self.custom_emitters[si](self.bld, self)
            if not seg:
                continue
            per_eng = {e: [] for e in ENGS}
            for o in seg:
                per_eng[o.eng].append(o)
            with nc.named_scope(f'P_{self.seg_names[si]}'), nc.Block(f'seg{si}') as blk:
                for eng in ENGS:
                    items = per_eng[eng]
                    if not items:
                        continue

                    def run(e, items=items):
                        guar = {}
                        for o in items:
                            for d in o.deps:
                                semh, val = self.marker_sem(d)
                                k = id(semh)
                                if guar.get(k, 0) < val:
                                    e.wait_ge(semh, val)
                                    guar[k] = val
                            inst = o.fn(e)
                            semh, val = self.marker_sem(o.marker)
                            inst.then_inc(semh, 16 if o.dma else 1)
                    getattr(blk, eng)(run)


class Builder:
    def __init__(self, debug=False):
        self.nc = bass.Bass('TRN2', target_bir_lowering=False)
        self.es = ExitStack()
        self.debug = debug
        self.din = {}
        self.dram = {}
        self.sb = {}
        self.ps = {}
        self._arena = None
        self._abase = 0
        self._vn = 0

    def dinp(self, name, shape, dt=F32):
        t = self.nc.dram_tensor(name, list(shape), dt, kind='ExternalInput')
        self.din[name] = t
        return t

    def dout(self, name, shape, dt=F32):
        t = self.nc.dram_tensor(name, list(shape), dt, kind='ExternalOutput')
        self.dram[name] = t
        return t

    def dpark(self, name, shape, dt=BF16, shared=False):
        kind = 'ExternalOutput' if (self.debug and not shared) else 'Internal'
        kw = {'addr_space': 'Shared'} if shared else {}
        t = self.nc.dram_tensor(name, list(shape), dt, kind=kind, **kw)
        self.dram[name] = t
        return t

    def sbuf(self, name, shape, dt=F32):
        t = self.es.enter_context(self.nc.sbuf_tensor(name, list(shape), dt))
        self.sb[name] = t
        return t

    def psum(self, name, shape, dt=F32):
        t = self.es.enter_context(self.nc.psum_tensor(name, list(shape), dt))
        self.ps[name] = t
        return t

    def sem(self, name):
        return self.es.enter_context(self.nc.semaphore(name))

    def arena_init(self):
        self._arena = self.nc.alloc_sbuf_tensor('ARENA', [128, ARENA_BYTES], mybir.dt.uint8)
        self._abase = self.nc.lookup_mloc(self._arena).addr

    def carve(self, layout, base=0, limit=ARENA_BYTES):
        off = base
        for name, shape, dt in layout:
            esz = {F32: 4, BF16: 2, I32: 4}[dt]
            nb = int(np.prod(shape[1:])) * esz
            off = (off + 63) & ~63
            assert off + nb <= limit, f'arena overflow at {name}: {off + nb} > {limit}'
            self._vn += 1
            h = self.nc.alloc_sbuf_tensor_at(
                f'{name}_v{self._vn}', list(shape), dt, offset=self._abase + off)
            off += nb
            self.sb[name] = h
        return off



def declare(b, stage):
    b.dinp('xT_own', (HID, W)); b.dinp('xT_ast', (HID, W))
    b.dinp('ipw_st', (145, NKC, 128, 128))
    b.dinp('opw_st', (32, 64, 128, 128))
    b.dinp('upw_st', (64, NKC, 128, 128))
    b.dinp('dnw_st', (32, 64, 128, 128))
    b.dinp('fc1_st', (16, NKC, 128, 128))
    b.dinp('fc2_st', (32, 16, 128, 128))
    b.dinp('exp_st', (16, 16, 128, 128))
    b.dinp('gatew', (NKC, 128, 8))
    b.dinp('cw', (128, KCONV, 80)); b.dinp('cb', (128, 80))
    b.dinp('dtb', (128, 1)); b.dinp('alog', (128, 1)); b.dinp('drow', (1, 128))
    b.dinp('nw', (128, 64)); b.dinp('lng', (128, 32)); b.dinp('lnb', (128, 32))
    b.dinp('ident', (128, 128), BF16)
    b.dinp('ident32', (128, 128), F32)
    b.dinp('mask_lges', (128, 128), BF16)
    b.dinp('Msel', (128, 16)); b.dinp('wsel', (128, 4))

    b.dout('out_fm', (HID, TOK), F32)
    b.dpark('p_hsT_own', (NCH, 128, INTER))
    b.dpark('p_hsT_ast', (NCH, 128, INTER))
    b.dpark('p_sgate', (64, 128, TOK))
    b.dpark('p_B_own', (8, 128, TOK))
    b.dpark('p_B_ast', (8, 128, TOK))
    b.dpark('p_C', (8, 128, TOK))
    b.dpark('p_dt', (2, 128, TOK), F32)
    b.dpark('p_acs', (2, 128, TOK), F32)
    b.dpark('p_sdoT', (NCH, 128, 128), F32)
    b.dpark('p_y1', (NCH, 128, INTER))
    b.dpark('p_lp', (NCH, 128, INTER), F32)
    b.dpark('p_h', (64, 128, TOK))
    b.dpark('p_xlat', (16, 128, TOK))
    b.dpark('p_fpre', (32, 128, TOK))
    b.dpark('p_fout', (32, 128, TOK))
    b.dpark('p_yf', (64, 128, TOK))
    b.dpark('p_xn', (NKC, 128, W))
    b.dpark('acs_scr', (2, 128, 128), F32)
    b.dpark('p_Sstage', (2, 128, INTER))
    b.dpark('p_I', (128, INTER), F32)
    b.dpark('ones_scr', (1, 1024), F32)
    b.dpark('xch_S', (2, 2, 128, INTER), BF16, shared=True)
    b.dpark('xch_A', (2, 4, 128), F32, shared=True)
    b.dpark('xch_F', (2, 4), I32, shared=True)

    b.sbuf('c_ident_bf', (128, 128), BF16)
    b.sbuf('c_ident32', (128, 128), F32)
    b.sbuf('c_Drep_bf', (128, 128), BF16)
    b.sbuf('c_mask_lges', (128, 128), BF16)
    b.sbuf('c_cw', (128, KCONV * 80), F32)
    b.sbuf('c_cb', (128, 80), F32)
    b.sbuf('c_dtb', (128, 1), F32)
    b.sbuf('c_A', (128, 1), F32)
    b.sbuf('c_Aexp', (128, 1), F32)
    b.sbuf('c_Drep', (128, 128), F32)
    b.sbuf('c_nw', (128, 64), F32)
    b.sbuf('c_lng', (128, 32), F32)
    b.sbuf('c_lnb', (128, 32), F32)
    b.sbuf('c_Msel', (128, 16), F32)
    b.sbuf('c_wsel', (128, 4), F32)
    b.sbuf('c_ones_col_bf', (128, 1), BF16)
    b.sbuf('c_ones_row_bf', (1, 128), BF16)
    b.sbuf('c_ones_row_f32', (1, 128), F32)
    b.sbuf('c_zeros_128', (128, 128), F32)
    b.sbuf('c_magic', (1, 4), I32)
    b.sbuf('c_drow_f32', (1, 128), F32)
    b.sbuf('c_drow_bf', (1, 128), BF16)
    b.sbuf('c_eps', (128, 1), F32)
    b.sbuf('c_mhalf', (128, 1), F32)
    b.sbuf('c_aseg', (128, 4), F32)
    b.sbuf('c_flags', (1, 8), I32)
    b.sbuf('c_dI', (128, 2), F32)
    b.sbuf('prefix', (128, INTER), F32)

    for i in range(6):
        b.psum(f'pb{i}', (128, 512), F32)
    b.psum('pb6', (128, 1024), BF16)
    b.psum('pb7', (128, 1024), BF16)

    b.xq_sem = b.sem('xq')
    b.xp_sem = b.sem('xp')
    b.xdone_sem = b.sem('xdone')
    b.arena_init()


XN_BYTES = NKC * W * 2
SCR_BASE = 74 * 1024

LAY_LN = [
    ('ln_x', (128, 2 * W), F32),
    ('ln_xbf', (128, 2 * W), BF16),
    ('ln_sq', (128, 2 * W), BF16),
    ('ln_rep', (128, 2 * W), BF16),
    ('ln_rows', (1, 6 * W), F32),
    ('ln_rows_bf', (1, 2 * W), BF16),
]
LAY_C = [
    ('cs_hsT', (128, 2 * INTER), BF16),
    ('cs_y1', (128, INTER), BF16),
    ('cs_Bc', (128, 1024), BF16),
    ('cs_BT', (128, 1024), BF16),
    ('cs_Cc', (128, 1024), BF16),
    ('cs_Gm', (128, 1024), BF16),
    ('cs_M', (128, 1024), BF16),
    ('cs_mtmp', (128, 1024), F32),
    ('cs_X', (128, 512), BF16),
    ('cs_Xd', (128, 512), BF16),
    ('cs_dres', (128, 512), BF16),
    ('cs_dtc', (128, 128), F32),
    ('cs_acsc', (128, 128), F32),
    ('cs_nacs', (128, 128), F32),
    ('cs_dtT', (128, 128), BF16),
    ('cs_DecT', (128, 128), BF16),
    ('cs_tmp', (128, 128), F32),
    ('cs_drow', (1, 128), F32),
    ('cs_drep', (128, 128), F32),
    ('cs_flat_l', (2, 1024), F32),
    ('cs_flat_r', (2, 1024), F32),
]

LAY_FFN_A = [
    ('ff_wf32', (128, 2 * 1024), F32),
    ('ff_wbf', (128, 2 * 4096), BF16),
    ('ff_hrow', (128, 2 * TOK), BF16),
    ('ff_xle', (128, 2 * TOK), BF16),
    ('ff_gw', (128, 32 * 8), F32),
    ('ff_gwb', (128, 32 * 8), BF16),
    ('ff_lgt', (128, 8), F32),
    ('ff_pairs', (128, 64), F32),
    ('ff_s', (128, 8), F32),
    ('ff_srow', (1, 1024), F32),
    ('ff_srep', (128, 1024), F32),
]
LAY_FFN_B = [
    ('fb_h', (128, 64 * 512), BF16),
    ('fb_xlat', (128, 16 * 512), BF16),
    ('fb_exp', (128, 16 * 512), BF16),
    ('fb_wf32', (128, 2 * 1024), F32),
    ('fb_wbf', (128, 2 * 4096), BF16),
    ('fb_wbf2', (128, 2 * 2048), BF16),
    ('fb_pre', (128, 2 * 512), BF16),
    ('fb_sq', (128, 512), BF16),
    ('fb_rows', (1, 6 * 512), F32),
    ('fb_rowbf', (1, 2 * 512), BF16),
    ('fb_mrep', (128, 512), BF16),
    ('fb_rrep', (128, 512), BF16),
    ('fb_ln', (128, 2 * 512), BF16),
]

LAY_G = [
    ('gs_y1', (128, INTER), BF16),
    ('gs_lp', (128, INTER), F32),
    ('gs_prev', (128, INTER), F32),
    ('gs_prevbf', (128, INTER), BF16),
    ('gs_C', (128, 1024), BF16),
    ('gs_sdoT', (128, 128), F32),
    ('gs_t', (128, 512), BF16),
    ('gs_ytok', (128, INTER), BF16),
    ('gs_yf', (128, INTER), BF16),
    ('gs_dcol', (128, 4), F32),
    ('gs_drow', (1, 128), F32),
    ('gs_dIrep', (128, 128), F32),
]
LAY_H = [
    ('hn', (128, 64 * TOK), BF16),
    ('hs_y', (128, 2 * TOK), BF16),
    ('hs_sg', (128, 2 * TOK), BF16),
    ('hs_sq', (128, TOK), BF16),
    ('hs_rows', (1, 2 * TOK), F32),
    ('hs_rowbf', (1, TOK), BF16),
    ('hs_rep', (128, TOK), BF16),
]
LAY_H2 = [
    ('ho_wf32', (128, 1024), F32),
    ('ho_wbf', (128, 2 * 4096), BF16),
    ('ho_x', (128, 2 * 512), F32),
    ('ho_f', (128, 2 * 512), BF16),
    ('ho_o', (128, 2 * 512), F32),
]

LAY_IP = [
    ('ip_wf32', (128, 2 * 1024), F32),
    ('ip_wbf', (128, 2 * HID), BF16),
    ('ip_pre', (128, 2 * W), F32),
    ('ip_conv', (128, 2 * TOK), F32),
    ('ip_out', (128, 2 * TOK), BF16),
    ('ip_trst', (128, 2 * TOK), BF16),
    ('sp_t1', (128, TOK), F32),
    ('sp_t2', (128, TOK), F32),
    ('dt_cur', (128, TOK), F32),
    ('acs_cur', (128, TOK), F32),
]


def carve_lnip(b):
    b._vn += 1
    b.sb['xn'] = b.nc.alloc_sbuf_tensor_at(
        f'xn_v{b._vn}', [128, NKC * W], BF16, offset=b._abase)
    end1 = b.carve(LAY_LN, base=SCR_BASE)
    end2 = b.carve(LAY_IP, base=SCR_BASE)
    end3 = b.carve(LAY_C, base=SCR_BASE)
    end4 = b.carve(LAY_FFN_A, base=SCR_BASE)
    end5 = b.carve(LAY_FFN_B, base=0)
    end6 = b.carve(LAY_G, base=0)
    end7 = b.carve(LAY_H, base=0)
    end8 = b.carve(LAY_H2, base=128 * 1024)
    assert max(end1, end2, end3, end4, end5, end6, end7, end8) <= ARENA_BYTES, (
        end1, end2, end3, end4, end5, end6, end7, end8)



def ph_consts(b, p):
    sb, din, ps = b.sb, b.din, b.ps
    loads = [
        (sb['c_ident_bf'][:, :], din['ident'].ap(), 'c_ident_bf'),
        (sb['c_ident32'][:, :], din['ident32'].ap(), 'c_ident32'),
        (sb['c_mask_lges'][:, :], din['mask_lges'].ap(), 'c_mask_lges'),
        (sb['c_cw'][:, :], din['cw'].ap().rearrange('p a b -> p (a b)'), 'c_cw'),
        (sb['c_cb'][:, :], din['cb'].ap(), 'c_cb'),
        (sb['c_dtb'][:, :], din['dtb'].ap(), 'c_dtb'),
        (sb['c_A'][:, :], din['alog'].ap(), 'c_A'),
        (sb['c_nw'][:, :], din['nw'].ap(), 'c_nw'),
        (sb['c_lng'][:, :], din['lng'].ap(), 'c_lng'),
        (sb['c_lnb'][:, :], din['lnb'].ap(), 'c_lnb'),
        (sb['c_Msel'][:, :], din['Msel'].ap(), 'c_Msel'),
        (sb['c_wsel'][:, :], din['wsel'].ap(), 'c_wsel'),
        (sb['c_drow_f32'][:, :], din['drow'].ap(), 'c_drow_f32'),
    ]
    for dst, src, key in loads:
        p.dma('sync', lambda e, dst=dst, src=src: e.dma_start(out=dst, in_=src),
              writes=[key], tag='const')
    p.op('vector', lambda e: e.memset(sb['c_ones_col_bf'][:, :], 1.0), writes=['c_ones_col'])
    p.op('vector', lambda e: e.memset(sb['c_ones_row_bf'][:, :], 1.0), writes=['c_ones_row'])
    p.op('vector', lambda e: e.memset(sb['c_ones_row_f32'][:, :], 1.0), writes=['c_ones_row_f32'])
    p.op('vector', lambda e: e.memset(sb['c_zeros_128'][:, :], 0.0), writes=['c_zeros_128'])
    p.op('vector', lambda e: e.memset(sb['c_magic'][:, :], MAGIC_X), writes=['c_magic'])
    p.op('vector', lambda e: e.memset(sb['prefix'][:, :], 0.0), writes=['prefix'])
    p.op('vector', lambda e: e.memset(sb['c_eps'][:, :], EPS), writes=['c_eps'])
    p.op('vector', lambda e: e.memset(sb['c_mhalf'][:, :], -0.5), writes=['c_mhalf'])
    p.op('scalar', lambda e: e.activation(sb['c_Aexp'][:, :], sb['c_A'][:, :], AF.Exp),
         reads=['c_A'], writes=['c_Aexp'])
    p.op('vector', lambda e: e.tensor_scalar_mul(sb['c_A'][:, :], sb['c_Aexp'][:, :], -1.0),
         reads=['c_Aexp'], writes=['c_A'])
    p.op('vector', lambda e: e.tensor_copy(sb['c_drow_bf'][:, :], sb['c_drow_f32'][:, :]),
         reads=['c_drow_f32'], writes=['c_drow_bf'])
    p.op('tensor', lambda e: e.matmul(ps['pb0'][:, 0:128], sb['c_ones_row_bf'][:, :],
                                      sb['c_drow_bf'][:, :], start=True, stop=True),
         reads=['c_ones_row', 'c_drow_bf'], writes=['pb0'])
    p.op('scalar', lambda e: e.copy(sb['c_Drep'][:, :], ps['pb0'][:, 0:128]),
         reads=['pb0'], writes=['c_Drep'])
    p.op('vector', lambda e: e.tensor_copy(sb['c_Drep_bf'][:, :], sb['c_Drep'][:, :]),
         reads=['c_Drep'], writes=['c_Drep_bf'])


def ph_ln(b, p, which):
    sb, din, ps = b.sb, b.din, b.ps
    xn = sb['xn']
    xbf = sb['ln_xbf']
    BL = [(0, 512), (512, 512), (1024, 128)]
    SBANK = ['pb0', 'pb1', 'pb2']
    QBANK = ['pb3', 'pb4', 'pb5']
    xk = din['xT_' + which].ap().rearrange('(a p) w -> a p w', p=128)
    for k in range(NKC):
        s = k % 2
        sl = slice(s * W, s * W + W)
        p.dma('sync', lambda e, k=k, sl=sl: e.dma_start(out=sb['ln_x'][:, sl], in_=xk[k]),
              writes=[f'ln_x:{s}'], tag='lnload')
        p.op('vector', lambda e, sl=sl: e.tensor_copy(xbf[:, sl], sb['ln_x'][:, sl]),
             reads=[f'ln_x:{s}'], writes=[f'ln_xbf:{s}'])
        p.op('scalar', lambda e, sl=sl: e.activation(sb['ln_sq'][:, sl], xbf[:, sl], AF.Square),
             reads=[f'ln_xbf:{s}'], writes=[f'ln_sq:{s}'])
        for bi, (lo, ln_) in enumerate(BL):
            p.op('tensor', lambda e, s=s, k=k, lo=lo, ln_=ln_, bi=bi:
                 e.matmul(ps[SBANK[bi]][0:1, 0:ln_], sb['c_ones_col_bf'][:, :],
                          xbf[:, s * W + lo:s * W + lo + ln_],
                          start=(k == 0), stop=(k == NKC - 1)),
                 reads=[f'ln_xbf:{s}', 'c_ones_col'], writes=[SBANK[bi]])
            p.op('tensor', lambda e, s=s, k=k, lo=lo, ln_=ln_, bi=bi:
                 e.matmul(ps[QBANK[bi]][0:1, 0:ln_], sb['c_ones_col_bf'][:, :],
                          sb['ln_sq'][:, s * W + lo:s * W + lo + ln_],
                          start=(k == 0), stop=(k == NKC - 1)),
                 reads=[f'ln_sq:{s}', 'c_ones_col'], writes=[QBANK[bi]])
    r = sb['ln_rows']

    def R(i, lo=0, ln_=W):
        return r[0:1, i * W + lo:i * W + lo + ln_]
    for bi, (lo, ln_) in enumerate(BL):
        p.op('vector', lambda e, bi=bi, lo=lo, ln_=ln_:
             e.tensor_scalar_mul(R(0, lo, ln_), ps[SBANK[bi]][0:1, 0:ln_], 1.0 / HID),
             reads=[SBANK[bi]], writes=['ln_mean'])
        p.op('vector', lambda e, bi=bi, lo=lo, ln_=ln_:
             e.tensor_scalar_mul(R(1, lo, ln_), ps[QBANK[bi]][0:1, 0:ln_], 1.0 / HID),
             reads=[QBANK[bi]], writes=['ln_msq'])
    p.op('vector', lambda e: e.tensor_mul(R(2), R(0), R(0)),
         reads=['ln_mean'], writes=['ln_m2'])
    p.op('vector', lambda e: e.tensor_sub(R(3), R(1), R(2)),
         reads=['ln_msq', 'ln_m2'], writes=['ln_var'])
    p.op('scalar', lambda e: e.activation(R(4), R(3), AF.Ln, bias=sb['c_eps'][0:1, :]),
         reads=['ln_var', 'c_eps'], writes=['ln_lnv'])
    p.op('scalar', lambda e: e.activation(R(5), R(4), AF.Exp, scale=sb['c_mhalf'][0:1, :]),
         reads=['ln_lnv', 'c_mhalf'], writes=['ln_rstd'])
    rb = sb['ln_rows_bf']
    p.op('vector', lambda e: e.tensor_copy(rb[0:1, 0:W], R(0)),
         reads=['ln_mean'], writes=['ln_mean_bf'])
    p.op('vector', lambda e: e.tensor_copy(rb[0:1, W:2 * W], R(5)),
         reads=['ln_rstd'], writes=['ln_rstd_bf'])
    for bi, (lo, ln_) in enumerate(BL):
        p.op('tensor', lambda e, lo=lo, ln_=ln_:
             e.matmul(ps['pb0'][:, 0:ln_], sb['c_ones_row_bf'][:, :],
                      rb[0:1, lo:lo + ln_], start=True, stop=True),
             reads=['ln_mean_bf', 'c_ones_row'], writes=['pb0'])
        p.op('scalar', lambda e, lo=lo, ln_=ln_:
             e.copy(sb['ln_rep'][:, lo:lo + ln_], ps['pb0'][:, 0:ln_]),
             reads=['pb0'], writes=['ln_rep:m'])
        p.op('tensor', lambda e, lo=lo, ln_=ln_:
             e.matmul(ps['pb1'][:, 0:ln_], sb['c_ones_row_bf'][:, :],
                      rb[0:1, W + lo:W + lo + ln_], start=True, stop=True),
             reads=['ln_rstd_bf', 'c_ones_row'], writes=['pb1'])
        p.op('scalar', lambda e, lo=lo, ln_=ln_:
             e.copy(sb['ln_rep'][:, W + lo:W + lo + ln_], ps['pb1'][:, 0:ln_]),
             reads=['pb1'], writes=['ln_rep:r'])
    repdeps = ['ln_rep:m', 'ln_rep:r']
    for k in range(NKC):
        s = k % 2
        sl = slice(s * W, s * W + W)
        osl = slice(k * W, (k + 1) * W)
        p.dma('sync', lambda e, k=k, sl=sl: e.dma_start(out=sb['ln_x'][:, sl], in_=xk[k]),
              writes=[f'ln_x:{s}'], tag='lnload2')
        p.op('vector', lambda e, sl=sl: e.tensor_copy(xbf[:, sl], sb['ln_x'][:, sl]),
             reads=[f'ln_x:{s}'], writes=[f'ln_xbf:{s}'])
        p.op('vector', lambda e, sl=sl: e.tensor_sub(xbf[:, sl], xbf[:, sl], sb['ln_rep'][:, 0:W]),
             reads=[f'ln_xbf:{s}'] + repdeps, writes=[f'ln_xbf:{s}'])
        p.op('vector', lambda e, sl=sl, osl=osl, k=k:
             e.tensor_tensor(xn[:, osl], xbf[:, sl], sb['ln_rep'][:, W:2 * W], ALU.mult),
             reads=[f'ln_xbf:{s}'] + repdeps, writes=[f'xn:{k}'])


def ph_inproj(b, p, which):
    sb, din, ps = b.sb, b.din, b.ps
    xn = sb['xn']
    own = which == 'own'
    fcs = (FC_DT + FC_B + (FC_C if own else []) + FC_HS + (FC_GATE if own else []))
    if IP_LIMIT:
        fcs = fcs[:IP_LIMIT]
    wst = din['ipw_st'].ap()
    ipb = [ps['pb0'], ps['pb1'], ps['pb2'], ps['pb3']]
    tbi = [0]
    xn_deps = [f'xn:{a}' for a in range(NKC)]

    def wload(u):
        fc = fcs[u]
        ub = u % 2
        for qtr in range(4):
            wq = qtr % 2
            p.dma('sync', lambda e, fc=fc, qtr=qtr, wq=wq:
                  e.dma_start(out=sb['ip_wf32'][:, wq * 1024:(wq + 1) * 1024]
                              .rearrange('p (a f) -> p a f', f=128),
                              in_=wst[fc].rearrange('a p f -> p a f')
                              [:, qtr * 8:(qtr + 1) * 8, :]),
                  writes=[f'ip_wf32:{wq}'], tag='wip')
            p.op('scalar', lambda e, ub=ub, qtr=qtr, wq=wq:
                 e.copy(sb['ip_wbf'][:, ub * HID + qtr * 1024:ub * HID + (qtr + 1) * 1024],
                        sb['ip_wf32'][:, wq * 1024:(wq + 1) * 1024]),
                 reads=[f'ip_wf32:{wq}'], writes=[f'ip_wbf:{ub}'], tag='wcast')

    wload(0)
    for u, fc in enumerate(fcs):
        fam = ('dt' if fc in FC_DT else 'B' if fc in FC_B else
               'C' if fc in FC_C else 'hs' if fc in FC_HS else 'gate')
        conv = fam in ('B', 'C', 'hs')
        BL = [(0, 512), (512, 512), (1024, 128)] if conv else [(128, 512), (640, 512)]
        ub = u % 2
        bkey = f'ip_wbf:{ub}'
        if u + 1 < len(fcs):
            wload(u + 1)
        for lo, ln_ in BL:
            bank = ipb[tbi[0] % 4]
            bank_key = f'pb{tbi[0] % 4}'
            tbi[0] += 1
            for a in range(NKC):
                p.op('tensor', lambda e, a=a, lo=lo, ln_=ln_, ub=ub, bank=bank:
                     e.matmul(bank[:, 0:ln_],
                              sb['ip_wbf'][:, ub * HID + a * 128:ub * HID + (a + 1) * 128],
                              xn[:, a * W + lo:a * W + lo + ln_],
                              start=(a == 0), stop=(a == NKC - 1)),
                     reads=[bkey, xn_deps[a]], writes=[bank_key], tag='ipmm')
            if conv:
                p.op('vector', lambda e, lo=lo, ln_=ln_, ub=ub, bank=bank:
                     e.tensor_copy(sb['ip_pre'][:, ub * W + lo:ub * W + lo + ln_],
                                   bank[:, 0:ln_]),
                     reads=[bank_key], writes=[f'ip_pre:{ub}'], tag='ipevac')
            elif fam == 'dt':
                p.op('vector', lambda e, lo=lo, ln_=ln_, bank=bank:
                     e.tensor_scalar_add(sb['sp_t1'][:, lo - 128:lo - 128 + ln_],
                                         bank[:, 0:ln_], sb['c_dtb'][:, :]),
                     reads=[bank_key, 'c_dtb'], writes=['sp_t1'], tag='dtbias')
            else:
                p.op('scalar', lambda e, lo=lo, ln_=ln_, ub=ub, bank=bank:
                     e.activation(sb['ip_out'][:, ub * TOK + lo - 128:ub * TOK + lo - 128 + ln_],
                                  bank[:, 0:ln_], AF.Silu),
                     reads=[bank_key], writes=[f'ip_out:{ub}'], tag='gsilu')
        if fam == 'dt':
            _emit_dt_chain(b, p, which)
        elif conv:
            _emit_conv_unit(b, p, which, fam, fc, ub)
        else:
            p.dma('gpsimd', lambda e, fc=fc, ub=ub: e.dma_start(
                out=b.dram['p_sgate'].ap()[fc],
                in_=sb['ip_out'][:, ub * TOK:(ub + 1) * TOK]),
                reads=[f'ip_out:{ub}'], tag='gpark')


def _emit_conv_unit(b, p, which, fam, fc, ub):
    sb, ps = b.sb, b.ps
    if fam == 'hs':
        ci = fc - 64
    elif fam == 'B':
        ci = 64 + (fc - 128)
    else:
        ci = 72 + (fc - 136)
    cw = sb['c_cw']
    pre = sb['ip_pre']
    conv = sb['ip_conv']
    cvs = slice(ub * TOK, (ub + 1) * TOK)
    p.op('vector', lambda e: e.tensor_scalar_mul(
        conv[:, cvs], pre[:, ub * W + 128:ub * W + 128 + TOK],
        cw[:, 3 * 80 + ci:3 * 80 + ci + 1]),
        reads=[f'ip_pre:{ub}', 'c_cw'], writes=[f'ip_conv:{ub}'], tag='conv')
    for k in range(3):
        off = 128 - 3 + k
        p.op('vector', lambda e, k=k, off=off: e.scalar_tensor_tensor(
            conv[:, cvs], pre[:, ub * W + off:ub * W + off + TOK],
            cw[:, k * 80 + ci:k * 80 + ci + 1], conv[:, cvs],
            ALU.mult, ALU.add),
            reads=[f'ip_pre:{ub}', f'ip_conv:{ub}', 'c_cw'],
            writes=[f'ip_conv:{ub}'], tag='conv')
    p.op('scalar', lambda e: e.activation(
        sb['ip_out'][:, cvs], conv[:, cvs], AF.Silu, bias=sb['c_cb'][:, ci:ci + 1]),
        reads=[f'ip_conv:{ub}', 'c_cb'], writes=[f'ip_out:{ub}'], tag='silu')
    if fam == 'B':
        g = fc - 128
        p.dma('gpsimd', lambda e, g=g: e.dma_start(
            out=b.dram['p_B_' + which].ap()[g], in_=sb['ip_out'][:, cvs]),
            reads=[f'ip_out:{ub}'], tag='bpark')
    elif fam == 'C':
        g = fc - 136
        p.dma('gpsimd', lambda e, g=g: e.dma_start(
            out=b.dram['p_C'].ap()[g], in_=sb['ip_out'][:, cvs]),
            reads=[f'ip_out:{ub}'], tag='cpark')
    else:
        j = fc - 64
        for c in range(NCH):
            trb = ps['pb6'] if (c % 2) == 0 else ps['pb7']
            trk = f'pb{6 + (c % 2)}'
            p.op('tensor', lambda e, c=c, trb=trb: e.transpose(
                trb[:, 0:128], sb['ip_out'][:, ub * TOK + c * 128:ub * TOK + (c + 1) * 128],
                b.sb['c_ident_bf'][:, :]),
                reads=[f'ip_out:{ub}', 'c_ident_bf'], writes=[trk], tag='hstr')
            p.op('vector', lambda e, c=c, trb=trb: e.tensor_copy(
                sb['ip_trst'][:, ub * TOK + c * 128:ub * TOK + (c + 1) * 128], trb[:, 0:128]),
                reads=[trk], writes=[f'ip_trst:{ub}:{c}'], tag='hstrev')
            p.dma('gpsimd', lambda e, c=c, j=j: e.dma_start(
                out=b.dram['p_hsT_' + which].ap()[c][:, j * 128:(j + 1) * 128],
                in_=sb['ip_trst'][:, ub * TOK + c * 128:ub * TOK + (c + 1) * 128]),
                reads=[f'ip_trst:{ub}:{c}'], tag='hspark')


def _emit_dt_chain(b, p, which):
    sb = b.sb
    i = 0 if which == 'own' else 1
    dtt = sb['dt_cur']
    acs = sb['acs_cur']
    t1 = sb['sp_t1']
    t2 = sb['sp_t2']
    p.op('scalar', lambda e: e.activation(t2[:, :], t1[:, :], AF.Abs),
         reads=['sp_t1'], writes=['sp_t2'], tag='sp')
    p.op('scalar', lambda e: e.activation(t2[:, :], t2[:, :], AF.Exp, scale=-1.0),
         reads=['sp_t2'], writes=['sp_t2'], tag='sp')
    p.op('scalar', lambda e: e.activation(t2[:, :], t2[:, :], AF.Ln, bias=1.0),
         reads=['sp_t2'], writes=['sp_t2'], tag='sp')
    p.op('vector', lambda e: e.tensor_scalar_max(t1[:, :], t1[:, :], 0.0),
         reads=['sp_t1'], writes=['sp_t1'], tag='sp')
    p.op('vector', lambda e: e.tensor_add(dtt[:, :], t1[:, :], t2[:, :]),
         reads=['sp_t1', 'sp_t2'], writes=['dt_cur'], tag='sp')
    p.op('vector', lambda e: e.tensor_scalar_mul(t1[:, :], dtt[:, :], sb['c_A'][:, :]),
         reads=['dt_cur', 'c_A'], writes=['sp_t1'], tag='adt')
    for c in range(NCH):
        p.op('vector', lambda e, c=c: e.tensor_tensor_scan(
            acs[:, c * 128:(c + 1) * 128], t1[:, c * 128:(c + 1) * 128],
            sb['c_zeros_128'][:, :], 0.0, ALU.add, ALU.add),
            reads=['sp_t1', 'c_zeros_128'], writes=[f'acs_cur:{c}'], tag='acs')
    p.op('vector', lambda e: e.tensor_copy(sb['c_aseg'][:, i:i + 1], acs[:, 127:128]),
         reads=['acs_cur:0'], writes=[f'c_aseg:{i}'], tag='aseg')
    for c in range(1, NCH):
        p.op('vector', lambda e, c=c: e.tensor_add(
            sb['c_aseg'][:, i:i + 1], sb['c_aseg'][:, i:i + 1], acs[:, c * 128 + 127:c * 128 + 128]),
            reads=[f'c_aseg:{i}', f'acs_cur:{c}'], writes=[f'c_aseg:{i}'], tag='aseg')
    acsdeps = [f'acs_cur:{c}' for c in range(NCH)]
    p.dma('gpsimd', lambda e, i=i: e.dma_start(out=b.dram['p_dt'].ap()[i], in_=dtt[:, :]),
          reads=['dt_cur'], tag='dtpark')
    p.dma('gpsimd', lambda e, i=i: e.dma_start(out=b.dram['p_acs'].ap()[i], in_=acs[:, :]),
          reads=acsdeps, tag='acspark')


def ph_scan(b, p, which):
    sb, ps, dram = b.sb, b.ps, b.dram
    own = which == 'own'
    i = 0 if own else 1
    hsT = dram['p_hsT_' + which].ap()
    Bpark = dram['p_B_' + which].ap()

    p.op('vector', lambda e: e.memset(sb['cs_flat_l'][0:1, :], 1.0), writes=['fl_ones'])
    p.op('vector', lambda e: e.memset(sb['cs_mtmp'][0:1, 0:1024], 1.0), writes=['cs_mtmp'])
    p.dma('gpsimd', lambda e: e.dma_start(out=dram['ones_scr'].ap(), in_=sb['cs_mtmp'][0:1, 0:1024]),
          reads=['cs_mtmp'], writes=['ones_scr'], tag='ones')
    p.dma('gpsimd', lambda e: e.dma_start(out=sb['cs_flat_r'][1:2, :], in_=dram['ones_scr'].ap()),
          reads=['ones_scr'], writes=['fr_ones'], tag='ones')

    for c in range(NCH):
        hb = c % 2
        p.dma('sync', lambda e, c=c, hb=hb: e.dma_start(
            out=sb['cs_hsT'][:, hb * INTER:(hb + 1) * INTER], in_=hsT[c]),
            writes=[f'cs_hsT:{hb}'], tag='chsT')
        p.dma('sync', lambda e, c=c: e.dma_start(
            out=sb['cs_Bc'][:, :].rearrange('p (g t) -> p g t', t=128),
            in_=Bpark.rearrange('g p t -> p g t')[:, :, c * 128:(c + 1) * 128]),
            writes=['cs_Bc'], tag='cB')
        if own:
            p.dma('sync', lambda e, c=c: e.dma_start(
                out=sb['cs_Cc'][:, :].rearrange('p (g t) -> p g t', t=128),
                in_=dram['p_C'].ap().rearrange('g p t -> p g t')[:, :, c * 128:(c + 1) * 128]),
                writes=['cs_Cc'], tag='cC')
        p.dma('sync', lambda e, c=c: e.dma_start(
            out=sb['cs_dtc'][:, :], in_=dram['p_dt'].ap()[i][:, c * 128:(c + 1) * 128]),
            writes=['cs_dtc'], tag='cdt')
        p.dma('sync', lambda e, c=c: e.dma_start(
            out=sb['cs_acsc'][:, :], in_=dram['p_acs'].ap()[i][:, c * 128:(c + 1) * 128]),
            writes=['cs_acsc'], tag='cacs')
        if own:
            p.dma('gpsimd', lambda e, c=c: e.dma_start(
                out=dram['p_lp'].ap()[c], in_=sb['prefix'][:, :]),
                reads=['prefix'], writes=[f'p_lp:{c}'], tag='lppark')
        p.op('vector', lambda e: e.tensor_scalar_mul(sb['cs_nacs'][:, :], sb['cs_acsc'][:, :], -1.0),
             reads=['cs_acsc'], writes=['cs_nacs'], tag='nacs')
        p.dma('gpsimd', lambda e: e.dma_start(out=dram['acs_scr'].ap()[0], in_=sb['cs_acsc'][:, :]),
              reads=['cs_acsc'], writes=['acs_scr0'], tag='acsrt')
        p.dma('gpsimd', lambda e: e.dma_start(out=dram['acs_scr'].ap()[1], in_=sb['cs_nacs'][:, :]),
              reads=['cs_nacs'], writes=['acs_scr1'], tag='acsrt')
        p.op('vector', lambda e: e.tensor_copy(sb['cs_tmp'][:, :], sb['cs_dtc'][:, :]),
             reads=['cs_dtc'], writes=['cs_tmp'], tag='x')
        p.op('tensor', lambda e: e.transpose(ps['pb2'][:, 0:128], sb['cs_tmp'][:, :],
                                             sb['c_ident32'][:, :]),
             reads=['cs_tmp', 'c_ident32'], writes=['pb2'], tag='trdt')
        p.op('vector', lambda e: e.tensor_copy(sb['cs_dtT'][:, :], ps['pb2'][:, 0:128]),
             reads=['pb2'], writes=['cs_dtT'], tag='trdtev')
        p.op('vector', lambda e: e.tensor_scalar_sub(sb['cs_tmp'][:, :], sb['cs_acsc'][:, :],
                                                     sb['cs_acsc'][:, 127:128]),
             reads=['cs_acsc'], writes=['cs_tmp'], tag='dec')
        p.op('scalar', lambda e: e.activation(sb['cs_tmp'][:, :], sb['cs_tmp'][:, :],
                                              AF.Exp, scale=-1.0),
             reads=['cs_tmp'], writes=['cs_tmp'], tag='dec')
        p.op('tensor', lambda e: e.transpose(ps['pb3'][:, 0:128], sb['cs_tmp'][:, :],
                                             sb['c_ident32'][:, :]),
             reads=['cs_tmp', 'c_ident32'], writes=['pb3'], tag='trdec')
        p.op('vector', lambda e: e.tensor_copy(sb['cs_DecT'][:, :], ps['pb3'][:, 0:128]),
             reads=['pb3'], writes=['cs_DecT'], tag='trdecev')
        if own:
            p.op('scalar', lambda e: e.activation(sb['cs_tmp'][:, :], sb['cs_acsc'][:, :], AF.Exp),
                 reads=['cs_acsc'], writes=['cs_tmp'], tag='sdo')
            p.op('tensor', lambda e: e.transpose(ps['pb2'][:, 0:128], sb['cs_tmp'][:, :],
                                                 sb['c_ident32'][:, :]),
                 reads=['cs_tmp', 'c_ident32'], writes=['pb2'], tag='trsdo')
            p.op('vector', lambda e: e.tensor_copy(sb['cs_nacs'][:, :], ps['pb2'][:, 0:128]),
                 reads=['pb2'], writes=['cs_nacs'], tag='trsdoev')
            p.dma('gpsimd', lambda e, c=c: e.dma_start(
                out=dram['p_sdoT'].ap()[c], in_=sb['cs_nacs'][:, :]),
                reads=['cs_nacs'], writes=[f'p_sdoT:{c}'], tag='sdopark')
        p.op('scalar', lambda e: e.activation(sb['cs_tmp'][:, 0:1], sb['cs_acsc'][:, 127:128], AF.Exp),
             reads=['cs_acsc'], writes=['cs_tmp'], tag='dcol')
        p.op('tensor', lambda e: e.transpose(ps['pb2'][0:1, 0:128], sb['cs_tmp'][:, 0:1],
                                             sb['c_ident32'][:, :]),
             reads=['cs_tmp', 'c_ident32'], writes=['pb2'], tag='trd')
        p.op('vector', lambda e: e.tensor_copy(sb['cs_drow'][:, :], ps['pb2'][0:1, 0:128]),
             reads=['pb2'], writes=['cs_drow'], tag='trdev')
        p.op('tensor', lambda e: e.matmul(ps['pb2'][:, 0:128], sb['c_ones_row_f32'][:, :],
                                          sb['cs_drow'][:, :], start=True, stop=True),
             reads=['cs_drow', 'c_ones_row_f32'], writes=['pb2'], tag='drepmm')
        p.op('vector', lambda e: e.tensor_copy(sb['cs_drep'][:, :], ps['pb2'][:, 0:128]),
             reads=['pb2'], writes=['cs_drep'], tag='drepev')
        p.op('vector', lambda e: e.tensor_tensor(
            sb['prefix'][:, :].rearrange('n (h q) -> n h q', q=P), sb['prefix'][:, :].rearrange('n (h q) -> n h q', q=P),
            sb['cs_drep'][:, :].unsqueeze(2).broadcast_to((128, 128, P)), ALU.mult),
            reads=['prefix', 'cs_drep'], writes=['prefix'], tag='pfxmul')
        if own:
            for g in range(8):
                bank = ps['pb0'] if g < 4 else ps['pb1']
                bk = 'pb0' if g < 4 else 'pb1'
                p.op('tensor', lambda e, g=g, bank=bank: e.matmul(
                    bank[:, (g % 4) * 128:(g % 4) * 128 + 128],
                    sb['cs_Bc'][:, g * 128:(g + 1) * 128],
                    sb['cs_Cc'][:, g * 128:(g + 1) * 128],
                    start=True, stop=True),
                    reads=['cs_Bc', 'cs_Cc'], writes=[bk], tag='gmm')
            for half in range(2):
                bank = ps['pb0'] if half == 0 else ps['pb1']
                bk = 'pb0' if half == 0 else 'pb1'
                p.op('vector', lambda e, half=half, bank=bank: e.tensor_tensor(
                    sb['cs_Gm'][:, half * 512:(half + 1) * 512].rearrange('p (g t) -> p g t', t=128),
                    bank[:, 0:512].rearrange('p (g t) -> p g t', t=128),
                    sb['c_mask_lges'][:, :].unsqueeze(1).broadcast_to((128, 4, 128)), ALU.mult),
                    reads=[bk, 'c_mask_lges'], writes=['cs_Gm'], tag='gmmask')
        for g in range(8):
            trb = ps['pb6'] if (g % 2) == 0 else ps['pb7']
            trk = f'pb{6 + (g % 2)}'
            p.op('tensor', lambda e, g=g, trb=trb: e.transpose(
                trb[:, 0:128], sb['cs_Bc'][:, g * 128:(g + 1) * 128], sb['c_ident_bf'][:, :]),
                reads=['cs_Bc', 'c_ident_bf'], writes=[trk], tag='btr')
            p.op('vector', lambda e, g=g, trb=trb: e.tensor_copy(
                sb['cs_BT'][:, g * 128:(g + 1) * 128], trb[:, 0:128]),
                reads=[trk], writes=['cs_BT'], tag='btrev')
        for w in range(16):
            if own:
                p.dma('scalar', lambda e, w=w: e.dma_start(
                    out=sb['cs_flat_r'][0:1, :],
                    in_=dram['acs_scr'].ap()[0].rearrange('p t -> (p t)').unsqueeze(0)
                    [:, w * 1024:(w + 1) * 1024]),
                    reads=['acs_scr0'], writes=['fr'], tag='cflat')
                p.dma('scalar', lambda e, w=w: e.dma_start(
                    out=sb['cs_flat_l'][1:2, :],
                    in_=dram['acs_scr'].ap()[1].rearrange('p t -> (p t)').unsqueeze(0)
                    [:, w * 1024:(w + 1) * 1024]),
                    reads=['acs_scr1'], writes=['fl'], tag='cflat')
            ybank, yk = (ps['pb4'], 'pb4') if w % 2 == 0 else (ps['pb0'], 'pb0')
            sbank, sk = (ps['pb5'], 'pb5') if w % 2 == 0 else (ps['pb1'], 'pb1')
            if own:
                for j in range(8):
                    dbank, dk = (ps['pb2'], 'pb2') if j < 4 else (ps['pb3'], 'pb3')
                    p.op('tensor', lambda e, j=j, dbank=dbank: e.matmul(
                        dbank[:, (j % 4) * 128:(j % 4) * 128 + 128],
                        sb['cs_flat_l'][:, j * 128:(j + 1) * 128],
                        sb['cs_flat_r'][:, j * 128:(j + 1) * 128],
                        start=True, stop=True),
                        reads=['fl', 'fr', 'fl_ones', 'fr_ones'],
                        writes=[dk], tag='diffmm')
                for half, dk in ((0, 'pb2'), (1, 'pb3')):
                    bank = ps[dk]
                    p.op('vector', lambda e, half=half, bank=bank: e.tensor_scalar_min(
                        sb['cs_mtmp'][:, half * 512:(half + 1) * 512], bank[:, 0:512], 0.0),
                        reads=[dk], writes=['cs_mtmp'], tag='mmin')
                p.op('scalar', lambda e: e.activation(sb['cs_M'][:, :], sb['cs_mtmp'][:, :], AF.Exp),
                     reads=['cs_mtmp'], writes=['cs_M'], tag='mexp')
                p.op('vector', lambda e: e.tensor_tensor(sb['cs_M'][:, :], sb['cs_M'][:, :],
                                                         sb['cs_Gm'][:, :], ALU.mult),
                     reads=['cs_M', 'cs_Gm'], writes=['cs_M'], tag='mmul')
            p.op('vector', lambda e, w=w, hb=hb: e.tensor_tensor(
                sb['cs_X'][:, :].rearrange('p (h q) -> p h q', q=P),
                sb['cs_hsT'][:, hb * INTER + w * 512:hb * INTER + (w + 1) * 512]
                .rearrange('p (h q) -> p h q', q=P),
                sb['cs_dtT'][:, w * 8:(w + 1) * 8].unsqueeze(2).broadcast_to((128, 8, P)),
                ALU.mult),
                reads=[f'cs_hsT:{hb}', 'cs_dtT'], writes=['cs_X'], tag='xbuild')
            p.op('vector', lambda e, w=w: e.tensor_tensor(
                sb['cs_Xd'][:, :].rearrange('p (h q) -> p h q', q=P),
                sb['cs_X'][:, :].rearrange('p (h q) -> p h q', q=P),
                sb['cs_DecT'][:, w * 8:(w + 1) * 8].unsqueeze(2).broadcast_to((128, 8, P)),
                ALU.mult),
                reads=['cs_X', 'cs_DecT'], writes=['cs_Xd'], tag='xdbuild')
            if own:
                for j in range(8):
                    p.op('tensor', lambda e, j=j, ybank=ybank: e.matmul(
                        ybank[:, j * 64:(j + 1) * 64],
                        sb['cs_M'][:, j * 128:(j + 1) * 128],
                        sb['cs_X'][:, j * 64:(j + 1) * 64], start=True, stop=True),
                        reads=['cs_M', 'cs_X'], writes=[yk], tag='ydiagmm')
            for j in range(8):
                p.op('tensor', lambda e, j=j, sbank=sbank: e.matmul(
                    sbank[:, j * 64:(j + 1) * 64],
                    sb['cs_BT'][:, j * 128:(j + 1) * 128],
                    sb['cs_Xd'][:, j * 64:(j + 1) * 64], start=True, stop=True),
                    reads=['cs_BT', 'cs_Xd'], writes=[sk], tag='statesmm')
            if own:
                p.op('vector', lambda e, w=w, hb=hb: e.tensor_tensor(
                    sb['cs_dres'][:, :].rearrange('p (h q) -> p h q', q=P),
                    sb['cs_hsT'][:, hb * INTER + w * 512:hb * INTER + (w + 1) * 512]
                    .rearrange('p (h q) -> p h q', q=P),
                    sb['c_Drep_bf'][:, w * 8:(w + 1) * 8].unsqueeze(2).broadcast_to((128, 8, P)),
                    ALU.mult),
                    reads=[f'cs_hsT:{hb}', 'c_Drep_bf'], writes=['cs_dres'], tag='dres')
                p.op('vector', lambda e, w=w, ybank=ybank: e.tensor_tensor(
                    sb['cs_y1'][:, w * 512:(w + 1) * 512], ybank[:, 0:512],
                    sb['cs_dres'][:, :], ALU.add),
                    reads=[yk, 'cs_dres'], writes=['cs_y1'], tag='y1add')
            p.op('vector', lambda e, w=w, sbank=sbank: e.tensor_tensor(
                sb['prefix'][:, w * 512:(w + 1) * 512], sb['prefix'][:, w * 512:(w + 1) * 512],
                sbank[:, 0:512], ALU.add),
                reads=['prefix', sk], writes=['prefix'], tag='pfxadd')
        if own:
            p.dma('gpsimd', lambda e, c=c: e.dma_start(
                out=dram['p_y1'].ap()[c], in_=sb['cs_y1'][:, :]),
                reads=['cs_y1'], writes=[f'p_y1:{c}'], tag='y1park')
    p.op('vector', lambda e: e.tensor_copy(sb['cs_y1'][:, :], sb['prefix'][:, :]),
         reads=['prefix'], writes=['cs_y1'], tag='sexp')
    p.dma('gpsimd', lambda e, i=i: e.dma_start(out=dram['p_Sstage'].ap()[i], in_=sb['cs_y1'][:, :]),
          reads=['cs_y1'], writes=[f'p_Sstage:{i}'], tag='sexp')
    p.op('vector', lambda e: e.memset(sb['prefix'][:, :], 0.0),
         reads=['cs_y1'], writes=['prefix'], tag='pfxreset')


def ph_park_xn(b, p):
    for k in range(NKC):
        p.dma('gpsimd', lambda e, k=k: e.dma_start(
            out=b.dram['p_xn'].ap()[k], in_=b.sb['xn'][:, k * W:(k + 1) * W]),
            reads=[f'xn:{k}'], writes=[f'p_xn:{k}'], tag='xnpark')


def ph_load_xn(b, p):
    for k in range(NKC):
        p.dma('sync', lambda e, k=k: e.dma_start(
            out=b.sb['xn'][:, k * W:(k + 1) * W], in_=b.dram['p_xn'].ap()[k]),
            reads=[f'p_xn:{k}'], writes=[f'xn:{k}'], tag='xnload')


def make_exchange_emitter(b):
    def emitter(bld, sched):
        nc = bld.nc
        dram = bld.dram
        sb = bld.sb
        deps = [sched.resolve(k) for k in
                ['p_Sstage:0', 'p_Sstage:1', 'c_aseg:0', 'c_aseg:1', 'c_magic']]
        xq = bld.xq_sem
        for arm in nc.switch_core_id([nc.gpsimd], n=128):
            slot = int(arm) % 2
            g = nc.gpsimd
            for semh, val in deps:
                g.wait_ge(semh, val)
            g.dma_start(out=dram['xch_S'].ap()[slot, 0], in_=dram['p_Sstage'].ap()[0]).then_inc(xq, 16)
            g.dma_start(out=dram['xch_S'].ap()[slot, 1], in_=dram['p_Sstage'].ap()[1]).then_inc(xq, 16)
            g.dma_start(out=dram['xch_A'].ap()[slot, 0].unsqueeze(1),
                        in_=sb['c_aseg'][:, 0:1]).then_inc(xq, 16)
            g.dma_start(out=dram['xch_A'].ap()[slot, 1].unsqueeze(1),
                        in_=sb['c_aseg'][:, 1:2]).then_inc(xq, 16)
            g.wait_ge(xq, 64)
            g.dma_start(out=dram['xch_F'].ap()[slot], in_=sb['c_magic'][0:1, :]).then_inc(xq, 16)
            g.wait_ge(xq, 80)
        xp = bld.xp_sem
        xdone = bld.xdone_sem
        with nc.Block('poll') as blk:
            def _g(g):
                with g.register('rend') as rend, g.register('rwait') as rwait:
                    g.reg_mov(rend, 500000)
                    g.reg_mov(rwait, 16)
                    with g.Fori(0, rend):
                        g.dma_start(out=sb['c_flags'][0:1, 0:8],
                                    in_=dram['xch_F'].ap().rearrange('a b -> (a b)')
                                    .unsqueeze(0)).then_inc(xp, 16)
                        g.wait_ge(xp, rwait)
                        g.reg_add(rwait, rwait, 16)
                        f0 = g.to_reg(g.value_load(sb['c_flags'][0:1, 0:1]))
                        f1 = g.to_reg(g.value_load(sb['c_flags'][0:1, 4:5]))
                        with g.If_eq(f0, MAGIC_X):
                            with g.If_eq(f1, MAGIC_X):
                                g.reg_mov(rend, 0)
                g.memset(sb['c_flags'][0:1, 4:8], 0).then_inc(xdone, 1)
            blk.gpsimd(_g)
    return emitter


def ph_combine_I(b, p):
    sb, ps, dram = b.sb, b.ps, b.dram
    p.op('gpsimd', lambda e: (e.wait_ge(b.xdone_sem, 1),
                              e.memset(sb['c_flags'][0:1, 6:8], 0))[-1],
         writes=['xch_ready'], tag='xbridge')
    asg = sb['cs_dtc']
    for j in range(4):
        s, i = j // 2, j % 2
        p.dma('sync', lambda e, s=s, i=i, j=j: e.dma_start(
            out=asg[:, j:j + 1], in_=dram['xch_A'].ap()[s, i].unsqueeze(1)),
            reads=['xch_ready'], writes=['asg'], tag='asgload')
    for j in range(4):
        p.op('vector', lambda e, j=j: e.tensor_tensor(
            asg[:, 8 + j:9 + j], asg[:, 0:1], sb['c_Msel'][:, 4 * j:4 * j + 1], ALU.mult),
            reads=['asg', 'c_Msel'], writes=['asg'], tag='sumt')
        for i in range(1, 4):
            p.op('vector', lambda e, j=j, i=i: e.scalar_tensor_tensor(
                asg[:, 8 + j:9 + j], asg[:, i:i + 1], sb['c_Msel'][:, 4 * j + i:4 * j + i + 1],
                asg[:, 8 + j:9 + j], ALU.mult, ALU.add),
                reads=['asg', 'c_Msel'], writes=['asg'], tag='sumt')
        p.op('scalar', lambda e, j=j: e.activation(asg[:, 16 + j:17 + j],
                                                   asg[:, 8 + j:9 + j], AF.Exp),
             reads=['asg'], writes=['asg'], tag='wexp')
        p.op('vector', lambda e, j=j: e.tensor_tensor(
            asg[:, 16 + j:17 + j], asg[:, 16 + j:17 + j], sb['c_wsel'][:, j:j + 1], ALU.mult),
            reads=['asg', 'c_wsel'], writes=['asg'], tag='wmask')
    for j in range(4):
        s, i = j // 2, j % 2
        hb = j % 2
        p.dma('sync', lambda e, s=s, i=i, hb=hb: e.dma_start(
            out=sb['cs_hsT'][:, hb * INTER:(hb + 1) * INTER], in_=dram['xch_S'].ap()[s, i]),
            reads=['xch_ready'], writes=[f'cs_hsT:{hb}'], tag='sload')
        p.op('tensor', lambda e, j=j: e.transpose(
            ps['pb2'][0:1, 0:128], asg[:, 16 + j:17 + j], sb['c_ident32'][:, :]),
            reads=['asg', 'c_ident32'], writes=['pb2'], tag='wtr')
        p.op('vector', lambda e: e.tensor_copy(sb['cs_drow'][:, :], ps['pb2'][0:1, 0:128]),
             reads=['pb2'], writes=['cs_drow'], tag='wtrev')
        p.op('tensor', lambda e: e.matmul(ps['pb2'][:, 0:128], sb['c_ones_row_f32'][:, :],
                                          sb['cs_drow'][:, :], start=True, stop=True),
             reads=['cs_drow', 'c_ones_row_f32'], writes=['pb2'], tag='wrepmm')
        p.op('vector', lambda e: e.tensor_copy(sb['cs_drep'][:, :], ps['pb2'][:, 0:128]),
             reads=['pb2'], writes=['cs_drep'], tag='wrepev')
        p.op('vector', lambda e, hb=hb: e.tensor_tensor(
            sb['cs_y1'][:, :].rearrange('n (h q) -> n h q', q=P),
            sb['cs_hsT'][:, hb * INTER:(hb + 1) * INTER].rearrange('n (h q) -> n h q', q=P),
            sb['cs_drep'][:, :].unsqueeze(2).broadcast_to((128, 128, P)), ALU.mult),
            reads=[f'cs_hsT:{hb}', 'cs_drep'], writes=['cs_y1'], tag='imul')
        p.op('vector', lambda e: e.tensor_tensor(sb['prefix'][:, :], sb['prefix'][:, :],
                                                 sb['cs_y1'][:, :], ALU.add),
             reads=['prefix', 'cs_y1'], writes=['prefix'], tag='iadd')
    p.dma('gpsimd', lambda e: e.dma_start(out=dram['p_I'].ap(), in_=sb['prefix'][:, :]),
          reads=['prefix'], writes=['p_I'], tag='ipark')


def _slab(b, p, st_ap, oc, n_k, wkeys, tag, half=None):
    f32k, f32t, bfk, bft, bfofs = wkeys
    sb = b.sb
    nq = (n_k * 128) // 1024
    a0 = 0 if half is None else half * n_k
    for qtr in range(nq):
        wq = qtr % 2
        p.dma('sync', lambda e, oc=oc, qtr=qtr, wq=wq, a0=a0: e.dma_start(
            out=sb[f32t][:, wq * 1024:(wq + 1) * 1024].rearrange('p (a f) -> p a f', f=128),
            in_=st_ap[oc].rearrange('a p f -> p a f')[:, a0 + qtr * 8:a0 + (qtr + 1) * 8, :]),
            writes=[f'{f32k}:{wq}'], tag=tag)
        p.op('scalar', lambda e, qtr=qtr, wq=wq: e.copy(
            sb[bft][:, bfofs + qtr * 1024:bfofs + (qtr + 1) * 1024],
            sb[f32t][:, wq * 1024:(wq + 1) * 1024]),
            reads=[f'{f32k}:{wq}'], writes=[bfk], tag=tag + 'c')


def ph_ffn_a(b, p):
    sb, ps, din, dram = b.sb, b.ps, b.din, b.dram
    xn = sb['xn']
    xn_deps = [f'xn:{k}' for k in range(NKC)]
    p.dma('sync', lambda e: e.dma_start(
        out=sb['ff_gw'][:, :].rearrange('p (a f) -> p a f', f=8),
        in_=din['gatew'].ap().rearrange('a p f -> p a f')),
        writes=['ff_gw'], tag='gw')
    p.op('vector', lambda e: e.tensor_copy(sb['ff_gwb'][:, :], sb['ff_gw'][:, :]),
         reads=['ff_gw'], writes=['ff_gwb'], tag='gwc')
    for tt in range(8):
        bank, bk = (ps['pb0'], 'pb0') if tt % 2 == 0 else (ps['pb1'], 'pb1')
        for a in range(NKC):
            p.op('tensor', lambda e, a=a, tt=tt, bank=bank: e.matmul(
                bank[:, 0:8], xn[:, a * W + 128 + tt * 128:a * W + 128 + (tt + 1) * 128],
                sb['ff_gwb'][:, a * 8:(a + 1) * 8],
                start=(a == 0), stop=(a == NKC - 1)),
                reads=['ff_gwb', xn_deps[a]], writes=[bk], tag='lgt')
        p.op('vector', lambda e, bank=bank: e.tensor_copy(sb['ff_lgt'][:, :], bank[:, 0:8]),
             reads=[bk], writes=['ff_lgt'], tag='lgtev')
        for i in range(8):
            p.op('vector', lambda e, i=i: e.tensor_scalar_add(
                sb['ff_pairs'][:, i * 8:(i + 1) * 8], sb['ff_lgt'][:, :],
                sb['ff_lgt'][:, i:i + 1]),
                reads=['ff_lgt'], writes=['ff_pairs'], tag='pairs')
        for i in range(8):
            p.op('vector', lambda e, i=i: e.memset(sb['ff_pairs'][:, i * 9:i * 9 + 1], -1e30),
                 writes=['ff_pairs'], tag='pdiag')
        p.op('vector', lambda e, tt=tt: e.reduce_max(sb['ff_s'][:, tt:tt + 1],
                                                     sb['ff_pairs'][:, :], axis=AX.X),
             reads=['ff_pairs'], writes=['ff_s'], tag='top2')
    for tt in range(8):
        p.op('tensor', lambda e, tt=tt: e.transpose(
            ps['pb2'][0:1, 0:128], sb['ff_s'][:, tt:tt + 1], sb['c_ident32'][:, :]),
            reads=['ff_s', 'c_ident32'], writes=['pb2'], tag='str')
        p.op('vector', lambda e, tt=tt: e.tensor_copy(
            sb['ff_srow'][0:1, tt * 128:(tt + 1) * 128], ps['pb2'][0:1, 0:128]),
            reads=['pb2'], writes=['ff_srow'], tag='strev')
    for half in range(2):
        p.op('tensor', lambda e, half=half: e.matmul(
            ps['pb2'][:, 0:512], sb['c_ones_row_f32'][:, :],
            sb['ff_srow'][0:1, half * 512:(half + 1) * 512], start=True, stop=True),
            reads=['ff_srow', 'c_ones_row_f32'], writes=['pb2'], tag='srep')
        p.op('vector', lambda e, half=half: e.tensor_copy(
            sb['ff_srep'][:, half * 512:(half + 1) * 512], ps['pb2'][:, 0:512]),
            reads=['pb2'], writes=['ff_srep'], tag='srepev')
    def fc1_slab(oc):
        _slab(b, p, din['fc1_st'].ap(), oc, NKC,
              ('ff_wf32', 'ff_wf32', f'ff_wbf:{oc % 2}', 'ff_wbf', (oc % 2) * 4096), 'fc1w')

    fc1_slab(0)
    for oc in range(16):
        sl = oc % 2
        if oc + 1 < 16:
            fc1_slab(oc + 1)
        for tb in range(2):
            bank, bk = (ps['pb0'], 'pb0') if tb == 0 else (ps['pb1'], 'pb1')
            for a in range(NKC):
                p.op('tensor', lambda e, a=a, tb=tb, sl=sl, bank=bank: e.matmul(
                    bank[:, 0:512], sb['ff_wbf'][:, sl * 4096 + a * 128:sl * 4096 + (a + 1) * 128],
                    xn[:, a * W + 128 + tb * 512:a * W + 128 + (tb + 1) * 512],
                    start=(a == 0), stop=(a == NKC - 1)),
                    reads=[f'ff_wbf:{sl}', xn_deps[a]], writes=[bk], tag='fc1mm')
            p.op('vector', lambda e, tb=tb, oc=oc, bank=bank: e.tensor_tensor(
                sb['ff_xle'][:, (oc % 2) * TOK + tb * 512:(oc % 2) * TOK + (tb + 1) * 512],
                bank[:, 0:512], sb['ff_srep'][:, tb * 512:(tb + 1) * 512], ALU.mult),
                reads=[bk, 'ff_srep'], writes=[f'ff_xle:{oc % 2}'], tag='fc1ev')
        p.dma('gpsimd', lambda e, oc=oc: e.dma_start(
            out=dram['p_xlat'].ap()[oc], in_=sb['ff_xle'][:, (oc % 2) * TOK:(oc % 2 + 1) * TOK]),
            reads=[f'ff_xle:{oc % 2}'], tag='xlpark')
    def up_slab(ic):
        _slab(b, p, din['upw_st'].ap(), ic, NKC,
              ('ff_wf32', 'ff_wf32', f'ff_wbf:{ic % 2}', 'ff_wbf', (ic % 2) * 4096), 'upw')

    up_slab(0)
    for ic in range(64):
        sl = ic % 2
        if ic + 1 < 64:
            up_slab(ic + 1)
        for tb in range(2):
            bank, bk = (ps['pb0'], 'pb0') if tb == 0 else (ps['pb1'], 'pb1')
            for a in range(NKC):
                p.op('tensor', lambda e, a=a, tb=tb, sl=sl, bank=bank: e.matmul(
                    bank[:, 0:512], sb['ff_wbf'][:, sl * 4096 + a * 128:sl * 4096 + (a + 1) * 128],
                    xn[:, a * W + 128 + tb * 512:a * W + 128 + (tb + 1) * 512],
                    start=(a == 0), stop=(a == NKC - 1)),
                    reads=[f'ff_wbf:{sl}', xn_deps[a]], writes=[bk], tag='upmm')
            p.op('scalar', lambda e, tb=tb, ic=ic, bank=bank: e.activation(
                sb['ff_hrow'][:, (ic % 2) * TOK + tb * 512:(ic % 2) * TOK + (tb + 1) * 512],
                bank[:, 0:512], AF.Relu),
                reads=[bk], writes=[f'ff_hrow:{ic % 2}'], tag='uprelu')
        p.op('vector', lambda e, ic=ic: e.tensor_tensor(
            sb['ff_hrow'][:, (ic % 2) * TOK:(ic % 2 + 1) * TOK],
            sb['ff_hrow'][:, (ic % 2) * TOK:(ic % 2 + 1) * TOK],
            sb['ff_hrow'][:, (ic % 2) * TOK:(ic % 2 + 1) * TOK], ALU.mult),
            reads=[f'ff_hrow:{ic % 2}'], writes=[f'ff_hrow:{ic % 2}'], tag='upsq')
        p.dma('gpsimd', lambda e, ic=ic: e.dma_start(
            out=dram['p_h'].ap()[ic], in_=sb['ff_hrow'][:, (ic % 2) * TOK:(ic % 2 + 1) * TOK]),
            reads=[f'ff_hrow:{ic % 2}'], tag='hpark')


def ph_ffn_b(b, p):
    sb, ps, din, dram = b.sb, b.ps, b.din, b.dram
    for th in range(2):
        for ic in range(64):
            p.dma('sync', lambda e, ic=ic, th=th: e.dma_start(
                out=sb['fb_h'][:, ic * 512:(ic + 1) * 512],
                in_=dram['p_h'].ap()[ic][:, th * 512:(th + 1) * 512]),
                writes=[f'fb_h:{ic}'], tag='hload')
        for k in range(16):
            p.dma('sync', lambda e, k=k, th=th: e.dma_start(
                out=sb['fb_xlat'][:, k * 512:(k + 1) * 512],
                in_=dram['p_xlat'].ap()[k][:, th * 512:(th + 1) * 512]),
                writes=[f'fb_xlat:{k}'], tag='xlload')
        def exp_slab(oc):
            _slab(b, p, din['exp_st'].ap(), oc, 16,
                  ('fb_wf32', 'fb_wf32', f'fb_wbf2:{oc % 2}', 'fb_wbf2', (oc % 2) * 2048), 'expw')

        exp_slab(0)
        for oc in range(16):
            sl = oc % 2
            if oc + 1 < 16:
                exp_slab(oc + 1)
            bank, bk = (ps['pb0'], 'pb0') if oc % 2 == 0 else (ps['pb1'], 'pb1')
            for a in range(16):
                p.op('tensor', lambda e, a=a, sl=sl, bank=bank: e.matmul(
                    bank[:, 0:512], sb['fb_wbf2'][:, sl * 2048 + a * 128:sl * 2048 + (a + 1) * 128],
                    sb['fb_xlat'][:, a * 512:(a + 1) * 512],
                    start=(a == 0), stop=(a == 15)),
                    reads=[f'fb_wbf2:{sl}', f'fb_xlat:{a}'], writes=[bk], tag='expmm')
            p.op('vector', lambda e, oc=oc, bank=bank: e.tensor_copy(
                sb['fb_exp'][:, oc * 512:(oc + 1) * 512], bank[:, 0:512]),
                reads=[bk], writes=[f'fb_exp:{oc}'], tag='expev')
        def fc2_slab(hc):
            _slab(b, p, din['fc2_st'].ap(), hc, 16,
                  ('fb_wf32', 'fb_wf32', f'fb_wbf2:{hc % 2}', 'fb_wbf2', (hc % 2) * 2048), 'fc2w')

        def dn_slab(hc, half):
            for qtr in range(4):
                wq = qtr % 2
                p.dma('sync', lambda e, hc=hc, half=half, qtr=qtr, wq=wq: e.dma_start(
                    out=sb['fb_wf32'][:, wq * 1024:(wq + 1) * 1024]
                    .rearrange('p (a f) -> p a f', f=128),
                    in_=din['dnw_st'].ap()[hc].rearrange('a p f -> p a f')
                    [:, half * 32 + qtr * 8:half * 32 + (qtr + 1) * 8, :]),
                    writes=[f'fb_wf32:{wq}'], tag='dnw')
                p.op('scalar', lambda e, half=half, qtr=qtr, wq=wq: e.copy(
                    sb['fb_wbf'][:, half * 4096 + qtr * 1024:half * 4096 + (qtr + 1) * 1024],
                    sb['fb_wf32'][:, wq * 1024:(wq + 1) * 1024]),
                    reads=[f'fb_wf32:{wq}'], writes=[f'fb_wbf:{half}'], tag='dnwc')

        fc2_slab(0)
        dn_slab(0, 0)
        for hc in range(32):
            bank, bk = (ps['pb0'], 'pb0') if hc % 2 == 0 else (ps['pb1'], 'pb1')
            sl2 = hc % 2
            for a in range(16):
                p.op('tensor', lambda e, a=a, sl2=sl2, bank=bank: e.matmul(
                    bank[:, 0:512], sb['fb_wbf2'][:, sl2 * 2048 + a * 128:sl2 * 2048 + (a + 1) * 128],
                    sb['fb_exp'][:, a * 512:(a + 1) * 512],
                    start=(a == 0), stop=False),
                    reads=[f'fb_wbf2:{sl2}', f'fb_exp:{a}'], writes=[bk], tag='fc2mm')
            dn_slab(hc, 1)
            for a in range(32):
                p.op('tensor', lambda e, a=a, bank=bank: e.matmul(
                    bank[:, 0:512], sb['fb_wbf'][:, a * 128:(a + 1) * 128],
                    sb['fb_h'][:, a * 512:(a + 1) * 512],
                    start=False, stop=False),
                    reads=['fb_wbf:0', f'fb_h:{a}'], writes=[bk], tag='dnmm')
            if hc + 1 < 32:
                fc2_slab(hc + 1)
                dn_slab(hc + 1, 0)
            for a in range(32):
                ag = 32 + a
                p.op('tensor', lambda e, a=a, ag=ag, bank=bank: e.matmul(
                    bank[:, 0:512], sb['fb_wbf'][:, 4096 + a * 128:4096 + (a + 1) * 128],
                    sb['fb_h'][:, ag * 512:(ag + 1) * 512],
                    start=False, stop=(ag == 63)),
                    reads=['fb_wbf:1', f'fb_h:{ag}'], writes=[bk], tag='dnmm')
            p.op('vector', lambda e, hc=hc, bank=bank: e.tensor_copy(
                sb['fb_pre'][:, (hc % 2) * 512:(hc % 2 + 1) * 512], bank[:, 0:512]),
                reads=[bk], writes=[f'fb_pre:{hc % 2}'], tag='dnev')
            p.dma('gpsimd', lambda e, hc=hc, th=th: e.dma_start(
                out=dram['p_fpre'].ap()[hc][:, th * 512:(th + 1) * 512],
                in_=sb['fb_pre'][:, (hc % 2) * 512:(hc % 2 + 1) * 512]),
                reads=[f'fb_pre:{hc % 2}'], writes=[f'p_fpre:{hc}:{th}'], tag='fprepark')
            p.op('scalar', lambda e, hc=hc: e.activation(
                sb['fb_sq'][:, :], sb['fb_pre'][:, (hc % 2) * 512:(hc % 2 + 1) * 512], AF.Square),
                reads=[f'fb_pre:{hc % 2}'], writes=['fb_sq'], tag='fsq')
            p.op('tensor', lambda e, hc=hc: e.matmul(
                ps['pb4'][0:1, 0:512], sb['c_ones_col_bf'][:, :],
                sb['fb_pre'][:, (hc % 2) * 512:(hc % 2 + 1) * 512],
                start=(hc == 0), stop=(hc == 31)),
                reads=[f'fb_pre:{hc % 2}', 'c_ones_col'], writes=['pb4'], tag='fsum')
            p.op('tensor', lambda e, hc=hc: e.matmul(
                ps['pb5'][0:1, 0:512], sb['c_ones_col_bf'][:, :], sb['fb_sq'][:, :],
                start=(hc == 0), stop=(hc == 31)),
                reads=['fb_sq', 'c_ones_col'], writes=['pb5'], tag='fsq2')
        r = sb['fb_rows']

        def R(i):
            return r[0:1, i * 512:(i + 1) * 512]
        p.op('vector', lambda e: e.tensor_scalar_mul(R(0), ps['pb4'][0:1, 0:512], 1.0 / HID),
             reads=['pb4'], writes=['fb_mean'], tag='fln')
        p.op('vector', lambda e: e.tensor_scalar_mul(R(1), ps['pb5'][0:1, 0:512], 1.0 / HID),
             reads=['pb5'], writes=['fb_msq'], tag='fln')
        p.op('vector', lambda e: e.tensor_mul(R(2), R(0), R(0)),
             reads=['fb_mean'], writes=['fb_m2'], tag='fln')
        p.op('vector', lambda e: e.tensor_sub(R(3), R(1), R(2)),
             reads=['fb_msq', 'fb_m2'], writes=['fb_var'], tag='fln')
        p.op('scalar', lambda e: e.activation(R(4), R(3), AF.Ln, bias=sb['c_eps'][0:1, :]),
             reads=['fb_var', 'c_eps'], writes=['fb_lnv'], tag='fln')
        p.op('scalar', lambda e: e.activation(R(5), R(4), AF.Exp, scale=sb['c_mhalf'][0:1, :]),
             reads=['fb_lnv', 'c_mhalf'], writes=['fb_rstd'], tag='fln')
        p.op('vector', lambda e: e.tensor_copy(sb['fb_rowbf'][0:1, 0:512], R(0)),
             reads=['fb_mean'], writes=['fb_mean_bf'], tag='fln')
        p.op('vector', lambda e: e.tensor_copy(sb['fb_rowbf'][0:1, 512:1024], R(5)),
             reads=['fb_rstd'], writes=['fb_rstd_bf'], tag='fln')
        p.op('tensor', lambda e: e.matmul(ps['pb2'][:, 0:512], sb['c_ones_row_bf'][:, :],
                                          sb['fb_rowbf'][0:1, 0:512], start=True, stop=True),
             reads=['fb_mean_bf', 'c_ones_row'], writes=['pb2'], tag='fln')
        p.op('scalar', lambda e: e.copy(sb['fb_mrep'][:, :], ps['pb2'][:, 0:512]),
             reads=['pb2'], writes=['fb_mrep'], tag='fln')
        p.op('tensor', lambda e: e.matmul(ps['pb3'][:, 0:512], sb['c_ones_row_bf'][:, :],
                                          sb['fb_rowbf'][0:1, 512:1024], start=True, stop=True),
             reads=['fb_rstd_bf', 'c_ones_row'], writes=['pb3'], tag='fln')
        p.op('scalar', lambda e: e.copy(sb['fb_rrep'][:, :], ps['pb3'][:, 0:512]),
             reads=['pb3'], writes=['fb_rrep'], tag='fln')
        for hc in range(32):
            s2 = hc % 2
            p.dma('sync', lambda e, hc=hc, th=th, s2=s2: e.dma_start(
                out=sb['fb_ln'][:, s2 * 512:(s2 + 1) * 512],
                in_=dram['p_fpre'].ap()[hc][:, th * 512:(th + 1) * 512]),
                reads=[f'p_fpre:{hc}:{th}'], writes=[f'fb_ln:{s2}'], tag='fpreload')
            p.op('vector', lambda e, s2=s2: e.tensor_sub(
                sb['fb_ln'][:, s2 * 512:(s2 + 1) * 512],
                sb['fb_ln'][:, s2 * 512:(s2 + 1) * 512],
                sb['fb_mrep'][:, :]),
                reads=[f'fb_ln:{s2}', 'fb_mrep'], writes=[f'fb_ln:{s2}'], tag='fsub')
            p.op('vector', lambda e, s2=s2: e.tensor_tensor(
                sb['fb_ln'][:, s2 * 512:(s2 + 1) * 512],
                sb['fb_ln'][:, s2 * 512:(s2 + 1) * 512],
                sb['fb_rrep'][:, :], ALU.mult),
                reads=[f'fb_ln:{s2}', 'fb_rrep'], writes=[f'fb_ln:{s2}'], tag='fmul')
            p.op('vector', lambda e, hc=hc, s2=s2: e.scalar_tensor_tensor(
                sb['fb_sq'][:, :], sb['fb_ln'][:, s2 * 512:(s2 + 1) * 512],
                sb['c_lng'][:, hc:hc + 1],
                sb['c_lnb'][:, hc:hc + 1].broadcast_to((128, 512)),
                ALU.mult, ALU.add),
                reads=[f'fb_ln:{s2}', 'c_lng', 'c_lnb'], writes=['fb_sq'], tag='faff')
            p.dma('gpsimd', lambda e, hc=hc, th=th: e.dma_start(
                out=dram['p_fout'].ap()[hc][:, th * 512:(th + 1) * 512],
                in_=sb['fb_sq'][:, :]),
                reads=['fb_sq'], writes=[f'p_fout:{hc}:{th}'], tag='foutpark')


def ph_yoff(b, p):
    sb, ps, dram = b.sb, b.ps, b.dram
    p.op('vector', lambda e: e.memset(sb['c_dI'][:, 0:1], 1.0), writes=['c_dI'], tag='g')
    for c in range(NCH):
        p.dma('sync', lambda e, c=c: e.dma_start(out=sb['gs_y1'][:, :], in_=dram['p_y1'].ap()[c]),
              reads=[f'p_y1:{c}'], writes=['gs_y1'], tag='gy1')
        p.dma('sync', lambda e, c=c: e.dma_start(out=sb['gs_lp'][:, :], in_=dram['p_lp'].ap()[c]),
              reads=[f'p_lp:{c}'], writes=['gs_lp'], tag='glp')
        p.dma('sync', lambda e, c=c: e.dma_start(
            out=sb['gs_C'][:, :].rearrange('p (g t) -> p g t', t=128),
            in_=dram['p_C'].ap().rearrange('g p t -> p g t')[:, :, c * 128:(c + 1) * 128]),
            writes=['gs_C'], tag='gC')
        p.dma('sync', lambda e, c=c: e.dma_start(out=sb['gs_sdoT'][:, :], in_=dram['p_sdoT'].ap()[c]),
              reads=[f'p_sdoT:{c}'], writes=['gs_sdoT'], tag='gsdo')
        p.dma('sync', lambda e, c=c: e.dma_start(
            out=sb['gs_dcol'][:, 0:1], in_=dram['p_acs'].ap()[0][:, c * 128 + 127:c * 128 + 128],
            allow_slow_non_contiguous=True),
            writes=['gs_dcol'], tag='gd')
        p.op('scalar', lambda e: e.activation(sb['gs_dcol'][:, 1:2], sb['gs_dcol'][:, 0:1], AF.Exp),
             reads=['gs_dcol'], writes=['gs_dcol'], tag='gdexp')
        p.op('tensor', lambda e: e.transpose(ps['pb2'][0:1, 0:128], sb['c_dI'][:, 0:1],
                                             sb['c_ident32'][:, :]),
             reads=['c_dI', 'c_ident32'], writes=['pb2'], tag='gtr')
        p.op('vector', lambda e: e.tensor_copy(sb['gs_drow'][:, :], ps['pb2'][0:1, 0:128]),
             reads=['pb2'], writes=['gs_drow'], tag='gtrev')
        p.op('tensor', lambda e: e.matmul(ps['pb2'][:, 0:128], sb['c_ones_row_f32'][:, :],
                                          sb['gs_drow'][:, :], start=True, stop=True),
             reads=['gs_drow', 'c_ones_row_f32'], writes=['pb2'], tag='gmm')
        p.op('vector', lambda e: e.tensor_copy(sb['gs_dIrep'][:, :], ps['pb2'][:, 0:128]),
             reads=['pb2'], writes=['gs_dIrep'], tag='gmmev')
        p.op('vector', lambda e: e.tensor_tensor(
            sb['gs_prev'][:, :].rearrange('n (h q) -> n h q', q=P),
            sb['prefix'][:, :].rearrange('n (h q) -> n h q', q=P),
            sb['gs_dIrep'][:, :].unsqueeze(2).broadcast_to((128, 128, P)), ALU.mult),
            reads=['prefix', 'gs_dIrep'], writes=['gs_prev'], tag='gprev')
        p.op('vector', lambda e: e.tensor_tensor(sb['gs_prev'][:, :], sb['gs_prev'][:, :],
                                                 sb['gs_lp'][:, :], ALU.add),
             reads=['gs_prev', 'gs_lp'], writes=['gs_prev'], tag='gprev2')
        p.op('vector', lambda e: e.tensor_copy(sb['gs_prevbf'][:, :], sb['gs_prev'][:, :]),
             reads=['gs_prev'], writes=['gs_prevbf'], tag='gprevbf')
        p.op('vector', lambda e: e.tensor_tensor(sb['c_dI'][:, 0:1], sb['c_dI'][:, 0:1],
                                                 sb['gs_dcol'][:, 1:2], ALU.mult),
             reads=['c_dI', 'gs_dcol', 'gs_dIrep'], writes=['c_dI'], tag='gdI')
        for w in range(16):
            bank, bk = (ps['pb0'], 'pb0') if w % 2 == 0 else (ps['pb1'], 'pb1')
            for j in range(8):
                p.op('tensor', lambda e, j=j, w=w, bank=bank: e.matmul(
                    bank[:, j * 64:(j + 1) * 64], sb['gs_C'][:, j * 128:(j + 1) * 128],
                    sb['gs_prevbf'][:, (8 * w + j) * 64:(8 * w + j + 1) * 64],
                    start=True, stop=True),
                    reads=['gs_C', 'gs_prevbf'], writes=[bk], tag='yoffmm')
            p.op('vector', lambda e, w=w, bank=bank: e.tensor_tensor(
                sb['gs_t'][:, :].rearrange('p (h q) -> p h q', q=P),
                bank[:, 0:512].rearrange('p (h q) -> p h q', q=P),
                sb['gs_sdoT'][:, 8 * w:8 * w + 8].unsqueeze(2).broadcast_to((128, 8, P)),
                ALU.mult),
                reads=[bk, 'gs_sdoT'], writes=['gs_t'], tag='yoffsdo')
            p.op('vector', lambda e, w=w: e.tensor_tensor(
                sb['gs_ytok'][:, w * 512:(w + 1) * 512], sb['gs_t'][:, :],
                sb['gs_y1'][:, w * 512:(w + 1) * 512], ALU.add),
                reads=['gs_t', 'gs_y1'], writes=['gs_ytok'], tag='yadd')
        for j in range(64):
            trb = ps['pb6'] if (j % 2) == 0 else ps['pb7']
            trk = f'pb{6 + (j % 2)}'
            p.op('tensor', lambda e, j=j, trb=trb: e.transpose(
                trb[:, 0:128], sb['gs_ytok'][:, j * 128:(j + 1) * 128], sb['c_ident_bf'][:, :]),
                reads=['gs_ytok', 'c_ident_bf'], writes=[trk], tag='ytr')
            p.op('vector', lambda e, j=j, trb=trb: e.tensor_copy(
                sb['gs_yf'][:, j * 128:(j + 1) * 128], trb[:, 0:128]),
                reads=[trk], writes=['gs_yf'], tag='ytrev')
        for j in range(64):
            p.dma('gpsimd', lambda e, j=j, c=c: e.dma_start(
                out=dram['p_yf'].ap()[j][:, c * 128:(c + 1) * 128],
                in_=sb['gs_yf'][:, j * 128:(j + 1) * 128]),
                reads=['gs_yf'], writes=[f'p_yf:{j}:{c}'], tag='yfpark')


def ph_final(b, p):
    sb, ps, din, dram = b.sb, b.ps, b.din, b.dram
    yfdeps = lambda j: [f'p_yf:{j}:{c}' for c in range(NCH)]
    for g in range(8):
        for jc in range(8):
            j = g * 8 + jc
            s = jc % 2
            p.dma('sync', lambda e, j=j, s=s: e.dma_start(
                out=sb['hs_y'][:, s * TOK:(s + 1) * TOK], in_=dram['p_yf'].ap()[j]),
                reads=yfdeps(j), writes=[f'hs_y:{s}'], tag='hy')
            p.dma('sync', lambda e, j=j, s=s: e.dma_start(
                out=sb['hs_sg'][:, s * TOK:(s + 1) * TOK], in_=dram['p_sgate'].ap()[j]),
                writes=[f'hs_sg:{s}'], tag='hsg')
            p.op('vector', lambda e, j=j, s=s: e.tensor_tensor(
                sb['hn'][:, j * TOK:(j + 1) * TOK], sb['hs_y'][:, s * TOK:(s + 1) * TOK],
                sb['hs_sg'][:, s * TOK:(s + 1) * TOK], ALU.mult),
                reads=[f'hs_y:{s}', f'hs_sg:{s}'], writes=[f'hn:{j}'], tag='xg')
            p.op('scalar', lambda e, j=j: e.activation(sb['hs_sq'][:, :],
                                                       sb['hn'][:, j * TOK:(j + 1) * TOK], AF.Square),
                 reads=[f'hn:{j}'], writes=['hs_sq'], tag='xgsq')
            for half in range(2):
                bank, bk = (ps['pb4'], 'pb4') if half == 0 else (ps['pb5'], 'pb5')
                p.op('tensor', lambda e, jc=jc, half=half, bank=bank: e.matmul(
                    bank[0:1, 0:512], sb['c_ones_col_bf'][:, :],
                    sb['hs_sq'][:, half * 512:(half + 1) * 512],
                    start=(jc == 0), stop=(jc == 7)),
                    reads=['hs_sq', 'c_ones_col'], writes=[bk], tag='rmssum')
        r = sb['hs_rows']
        for half in range(2):
            bank, bk = (ps['pb4'], 'pb4') if half == 0 else (ps['pb5'], 'pb5')
            p.op('vector', lambda e, half=half, bank=bank: e.tensor_scalar_mul(
                r[0:1, half * 512:(half + 1) * 512], bank[0:1, 0:512], 1.0 / (INTER // G)),
                reads=[bk], writes=['hs_msq'], tag='rms')
        p.op('scalar', lambda e: e.activation(r[0:1, TOK:2 * TOK], r[0:1, 0:TOK],
                                              AF.Ln, bias=sb['c_eps'][0:1, :]),
             reads=['hs_msq', 'c_eps'], writes=['hs_lnv'], tag='rms')
        p.op('scalar', lambda e: e.activation(r[0:1, TOK:2 * TOK], r[0:1, TOK:2 * TOK],
                                              AF.Exp, scale=sb['c_mhalf'][0:1, :]),
             reads=['hs_lnv', 'c_mhalf'], writes=['hs_rstd'], tag='rms')
        p.op('vector', lambda e: e.tensor_copy(sb['hs_rowbf'][0:1, :], r[0:1, TOK:2 * TOK]),
             reads=['hs_rstd'], writes=['hs_rowbf'], tag='rms')
        for half in range(2):
            p.op('tensor', lambda e, half=half: e.matmul(
                ps['pb2'][:, 0:512], sb['c_ones_row_bf'][:, :],
                sb['hs_rowbf'][0:1, half * 512:(half + 1) * 512], start=True, stop=True),
                reads=['hs_rowbf', 'c_ones_row'], writes=['pb2'], tag='rmsrep')
            p.op('scalar', lambda e, half=half: e.copy(
                sb['hs_rep'][:, half * 512:(half + 1) * 512], ps['pb2'][:, 0:512]),
                reads=['pb2'], writes=['hs_rep'], tag='rmsrepev')
        for jc in range(8):
            j = g * 8 + jc
            p.op('vector', lambda e, j=j: e.tensor_tensor(
                sb['hn'][:, j * TOK:(j + 1) * TOK], sb['hn'][:, j * TOK:(j + 1) * TOK],
                sb['hs_rep'][:, :], ALU.mult),
                reads=[f'hn:{j}', 'hs_rep'], writes=[f'hn:{j}'], tag='rmsapply')
            p.op('vector', lambda e, j=j: e.tensor_scalar_mul(
                sb['hn'][:, j * TOK:(j + 1) * TOK], sb['hn'][:, j * TOK:(j + 1) * TOK],
                sb['c_nw'][:, j:j + 1]),
                reads=[f'hn:{j}', 'c_nw'], writes=[f'hn:{j}'], tag='nw')
    p.barrier(name='outproj')
    hndeps = [f'hn:{j}' for j in range(64)]
    xk = din['xT_own'].ap().rearrange('(a p) w -> a p w', p=128)
    outk = []
    def op_slab(hc, half):
        for qtr in range(8):
            wq = qtr % 2
            p.dma('sync', lambda e, hc=hc, half=half, qtr=qtr, wq=wq: e.dma_start(
                out=sb['ho_wf32'][:, wq * 512:wq * 512 + 512]
                .rearrange('p (a f) -> p a f', f=128),
                in_=din['opw_st'].ap()[hc].rearrange('a p f -> p a f')
                [:, half * 32 + qtr * 4:half * 32 + (qtr + 1) * 4, :]),
                writes=[f'ho_wf32:{wq}'], tag='ow')
            p.op('scalar', lambda e, half=half, qtr=qtr, wq=wq: e.copy(
                sb['ho_wbf'][:, half * 4096 + qtr * 512:half * 4096 + (qtr + 1) * 512],
                sb['ho_wf32'][:, wq * 512:wq * 512 + 512]),
                reads=[f'ho_wf32:{wq}'], writes=[f'ho_wbf:{half}'], tag='owc')

    op_slab(0, 0)
    for hc in range(32):
        op_slab(hc, 1)
        for half in range(2):
            for tb in range(2):
                bank, bk = (ps['pb0'], 'pb0') if tb == 0 else (ps['pb1'], 'pb1')
                for a in range(32):
                    k = half * 32 + a
                    p.op('tensor', lambda e, a=a, k=k, half=half, tb=tb, bank=bank: e.matmul(
                        bank[:, 0:512],
                        sb['ho_wbf'][:, half * 4096 + a * 128:half * 4096 + (a + 1) * 128],
                        sb['hn'][:, k * TOK + tb * 512:k * TOK + tb * 512 + 512],
                        start=(k == 0), stop=(k == 63)),
                        reads=[f'ho_wbf:{half}'] + hndeps, writes=[bk], tag='opmm')
            if half == 0 and hc + 1 < 32:
                op_slab(hc + 1, 0)
        for tb in range(2):
            bank, bk = (ps['pb0'], 'pb0') if tb == 0 else (ps['pb1'], 'pb1')
            p.dma('scalar', lambda e, hc=hc, tb=tb: e.dma_start(
                out=sb['ho_x'][:, tb * 512:(tb + 1) * 512],
                in_=xk[hc][:, 128 + tb * 512:128 + (tb + 1) * 512]),
                writes=[f'ho_x:{tb}'], tag='ox')
            p.dma('scalar', lambda e, hc=hc, tb=tb: e.dma_start(
                out=sb['ho_f'][:, tb * 512:(tb + 1) * 512],
                in_=dram['p_fout'].ap()[hc][:, tb * 512:(tb + 1) * 512]),
                reads=[f'p_fout:{hc}:{tb}'], writes=[f'ho_f:{tb}'], tag='of')
            p.op('vector', lambda e, tb=tb, bank=bank: e.tensor_tensor(
                sb['ho_o'][:, tb * 512:(tb + 1) * 512], bank[:, 0:512],
                sb['ho_x'][:, tb * 512:(tb + 1) * 512], ALU.add),
                reads=[bk, f'ho_x:{tb}'], writes=[f'ho_o:{tb}'], tag='oadd')
            p.op('vector', lambda e, tb=tb: e.tensor_tensor(
                sb['ho_o'][:, tb * 512:(tb + 1) * 512], sb['ho_o'][:, tb * 512:(tb + 1) * 512],
                sb['ho_f'][:, tb * 512:(tb + 1) * 512], ALU.add),
                reads=[f'ho_o:{tb}', f'ho_f:{tb}'], writes=[f'ho_o:{tb}'], tag='oadd2')
            p.dma('gpsimd', lambda e, hc=hc, tb=tb: e.dma_start(
                out=dram['out_fm'].ap().rearrange('(a p) t -> a p t', p=128)[hc]
                [:, tb * 512:(tb + 1) * 512],
                in_=sb['ho_o'][:, tb * 512:(tb + 1) * 512]),
                reads=[f'ho_o:{tb}'], writes=[f'out:{hc}:{tb}'], tag='outdma')
            outk.append(f'out:{hc}:{tb}')
    p.op('gpsimd', lambda e: e.memset(sb['c_magic'][:, 2:3], 0),
         reads=outk, writes=['final'], tag='final')


def ph_out_stub(b, p):
    sb = b.sb
    p.op('vector', lambda e: e.memset(sb['ln_x'][:, 0:TOK], 0.0),
         writes=['ln_x:0'], tag='outz')
    outk = []
    for kc in range(NKC):
        p.dma('gpsimd', lambda e, kc=kc: e.dma_start(
            out=b.dram['out_fm'].ap().rearrange('(a p) t -> a p t', p=128)[kc],
            in_=sb['ln_x'][:, 0:TOK]),
            reads=['ln_x:0'], writes=[f'out:{kc}'], tag='outdma')
        outk.append(f'out:{kc}')
    p.op('gpsimd', lambda e: e.memset(sb['c_magic'][:, 2:3], 0),
         reads=outk, writes=['final'], tag='final')



def build_program(debug=False, stage=BUILD_STAGE):
    b = Builder(debug)
    declare(b, stage)
    carve_lnip(b)
    p = Prog(b)
    ph_consts(b, p)
    ph_ln(b, p, 'own')
    if debug:
        for k in range(NKC):
            p.dma('gpsimd', lambda e, k=k: e.dma_start(
                out=b.dram['p_xn'].ap()[k], in_=b.sb['xn'][:, k * W:(k + 1) * W]),
                reads=[f'xn:{k}'], tag='dbgxn')
    p.barrier(name='ip_own')
    if stage >= 2:
        ph_inproj(b, p, 'own')
        p.barrier(name='scan_own')
    if stage >= 3:
        ph_scan(b, p, 'own')
        p.barrier(name='parkxn')
    if stage >= 4:
        ph_park_xn(b, p)
        p.barrier(name='ln_ast')
        ph_ln(b, p, 'ast')
        p.barrier(name='ip_ast')
        ph_inproj(b, p, 'ast')
        p.barrier(name='scan_ast')
        ph_scan(b, p, 'ast')
        p.custom(make_exchange_emitter(b), name='combI')
        ph_combine_I(b, p)
        p.barrier(name='ffn_a')
    if stage >= 5:
        ph_load_xn(b, p)
        ph_ffn_a(b, p)
        p.barrier(name='ffn_b')
        ph_ffn_b(b, p)
        p.barrier(name='yoff')
    if stage >= 6:
        ph_yoff(b, p)
        p.barrier(name='gating')
        ph_final(b, p)
    else:
        ph_out_stub(b, p)
    p.emit()
    b.prog = p
    return b


def kernel(**inputs):
    per_core = host_prep(inputs)
    bld = build_program(debug=DEBUG)
    res = run_bass_kernel_spmd(
        bld.nc, [{k: np.asarray(v) for k, v in d.items()} for d in per_core],
        core_ids=list(range(8)))
    x = np.asarray(inputs['L0_attn_hidden_states'])
    out = np.empty_like(x)
    for c in range(8):
        bq, q = c // 4, c % 4
        out[bq, q * TOK:(q + 1) * TOK, :] = np.asarray(res.results[c]['out_fm']).T
    return out
